# version 09
# baseline (speedup 1.0000x reference)
import contextlib
import numpy as np
import ml_dtypes
import concourse.bass as bass
import concourse.mybir as mybir
from concourse.bass_utils import run_bass_kernel_spmd

F32 = mybir.dt.float32
BF16 = mybir.dt.bfloat16
AF = mybir.ActivationFunctionType
ALU = mybir.AluOpType
BF = ml_dtypes.bfloat16

D = 1024
S = 4096
T = 2048
HD = 64
NE = 8
DC = 8
ATC = 512
NATC = S // ATC
NQC = 4
EPS = 1e-5
ALPHA = 1.702
CCAP = float(np.float32(7.0) * np.float32(
    1.0 / (1.0 + np.exp(-np.float64(np.float32(ALPHA) * np.float32(7.0))))))

ENGS = ("pe", "act", "dve", "pool", "sp")


class Sem:
    def __init__(self, name):
        self.name = name
        self.h = None
        self.v = 0


class Plan:
    def __init__(self):
        self.ops = {e: [] for e in ENGS}
        self.sems = {}

    def sem(self, name):
        if name not in self.sems:
            self.sems[name] = Sem(name)
        return self.sems[name]

    def op(self, eng, build, waits=(), inc=None, inc_n=1):
        if inc is not None:
            inc.v += inc_n

        def fn(engine, waits=tuple(waits), build=build, inc=inc, inc_n=inc_n):
            for s, v in waits:
                if v > 0:
                    engine.wait_ge(s.h, v)
            ins = build(engine)
            if inc is not None:
                ins.then_inc(inc.h, inc_n)
        self.ops[eng].append(fn)

    def wait(self, eng, sem, val):
        def fn(engine, sem=sem, val=val):
            engine.wait_ge(sem.h, val)
        self.ops[eng].append(fn)

    def drain(self, eng, inc=None, inc_n=1):
        self.op(eng, lambda e: e.drain(), inc=inc, inc_n=inc_n)


def _rope_tables(perm):
    dim = np.float32(HD)
    inv_freq = (np.float32(1.0) / (np.float32(10000.0) ** (
        np.arange(0, HD, 2, dtype=np.float32) / dim))).astype(np.float32)
    pos = np.arange(S, dtype=np.float32)
    freqs = (pos[:, None] * inv_freq[None, :]).astype(np.float32)
    freqs = np.repeat(freqs, 2, axis=-1)
    cos_t = np.cos(freqs.astype(np.float64)).astype(np.float32)
    sin_t = np.sin(freqs.astype(np.float64)).astype(np.float32)
    sign = np.ones(HD, dtype=np.float32)
    sign[0::2] = -1.0
    sin_s = sin_t * sign[None, :]
    cos2 = np.concatenate([cos_t[perm].T, cos_t[perm].T], axis=0)
    sin2 = np.concatenate([sin_s[perm].T, sin_s[perm].T], axis=0)
    sinp = sin2.copy()
    sinp[0::2], sinp[1::2] = sin2[1::2], sin2[0::2]
    return np.ascontiguousarray(cos2), np.ascontiguousarray(sinp)


class Arena:

    def __init__(self, nc, es, name, nbytes):
        self.nc = nc
        self.name = name
        self.nbytes = nbytes
        self.backing = es.enter_context(
            nc.sbuf_tensor(name, [128, nbytes], mybir.dt.uint8))
        self.base = nc.lookup_mloc(self.backing).addr

    def tensor(self, name, shape, dtype, off):
        per_part = int(np.prod(shape[1:])) * mybir.dt.size(dtype)
        assert off + per_part <= self.nbytes, (self.name, name, off, per_part)
        assert (self.base + off) % 32 == 0, (name, self.base, off)
        h = self.nc.virtual_sbuf_tensor(name, shape, dtype)
        self.nc._tensor(h.name, shape, dtype, type="SB")
        mloc = self.nc.lookup_mloc(h)
        mloc.allocated = True
        mloc.addr = self.base + off
        return h


def build_program(debug=False, stop_after=None, c_lim=None):
    _LVL = {"pre": 0, "A": 1, "B": 2, "C": 3, None: 3}[stop_after]
    _CT4, _CNE = c_lim if c_lim is not None else (NQC, NE)
    nc = bass.Bass("TRN2", target_bir_lowering=False, debug=False)

    x_fm = nc.dram_tensor("x_fm", [D, S], F32, kind="ExternalInput").ap()
    cos_in = nc.dram_tensor("cos_in", [128, S], F32, kind="ExternalInput").ap()
    sin_in = nc.dram_tensor("sin_in", [128, S], F32, kind="ExternalInput").ap()
    wqT = nc.dram_tensor("wqT", [D, D], F32, kind="ExternalInput").ap()
    wkT = nc.dram_tensor("wkT", [D, D], F32, kind="ExternalInput").ap()
    wvT = nc.dram_tensor("wvT", [D, D], F32, kind="ExternalInput").ap()
    woT = nc.dram_tensor("woT", [D, D], F32, kind="ExternalInput").ap()
    b_qkvo = nc.dram_tensor("b_qkvo", [128, 4 * DC], F32, kind="ExternalInput").ap()
    bv_row = nc.dram_tensor("bv_row", [1, D], F32, kind="ExternalInput").ap()
    gamma_in = nc.dram_tensor("gamma_in", [128, DC], F32, kind="ExternalInput").ap()
    wg_in = nc.dram_tensor("wg_in", [D, NE], BF16, kind="ExternalInput").ap()
    gb_row = nc.dram_tensor("gb_row", [1, NE], BF16, kind="ExternalInput").ap()
    w1gT = nc.dram_tensor("w1gT", [NE, D, D], BF16, kind="ExternalInput").ap()
    w1lT = nc.dram_tensor("w1lT", [NE, D, D], BF16, kind="ExternalInput").ap()
    w2T = nc.dram_tensor("w2T", [NE, D, D], BF16, kind="ExternalInput").ap()
    b1g_in = nc.dram_tensor("b1g_in", [128, DC * NE], BF16, kind="ExternalInput").ap()
    b1l_in = nc.dram_tensor("b1l_in", [128, DC * NE], BF16, kind="ExternalInput").ap()
    b2_in = nc.dram_tensor("b2_in", [NE, D], BF16, kind="ExternalInput").ap()
    ones_bf_in = nc.dram_tensor("ones_bf_in", [128, 128], BF16, kind="ExternalInput").ap()
    ident_in = nc.dram_tensor("ident_in", [128, 128], F32, kind="ExternalInput").ap()
    sel_in = nc.dram_tensor("sel_in", [NE, NE * 128], BF16, kind="ExternalInput").ap()

    out_fm = nc.dram_tensor("out_fm", [D, T], F32, kind="ExternalOutput").ap()

    k_sp = nc.dram_tensor("k_sp", [DC, 128, S], BF16, kind="Internal").ap()
    v_sp = nc.dram_tensor("v_sp", [DC, S, 130], BF16, kind="Internal").ap()
    o_sp = nc.dram_tensor("o_sp", [D, T], F32, kind="Internal").ap()

    dbg = {}
    if debug:
        dbg["ln1"] = nc.dram_tensor("d_ln1", [D, S], BF16, kind="ExternalOutput").ap()
        dbg["q"] = nc.dram_tensor("d_q", [D, T], BF16, kind="ExternalOutput").ap()
        dbg["attn"] = nc.dram_tensor("d_attn", [D, T], BF16, kind="ExternalOutput").ap()
        dbg["x1"] = nc.dram_tensor("d_x1", [D, T], F32, kind="ExternalOutput").ap()
        dbg["ct"] = nc.dram_tensor("d_ct", [NE, T], BF16, kind="ExternalOutput").ap()
        dbg["normed"] = nc.dram_tensor("d_normed", [D, T], BF16, kind="ExternalOutput").ap()
        dbg["acc"] = nc.dram_tensor("d_acc", [D, T], F32, kind="ExternalOutput").ap()

    es = contextlib.ExitStack()
    with es:
        KB = 1024

        def sbuf(name, shape, dt):
            return es.enter_context(nc.sbuf_tensor(name, shape, dt))

        s_ones = sbuf("s_ones", [128, 128], BF16)
        s_id = sbuf("s_id", [128, 128], F32)
        s_bias = sbuf("s_bias", [128, 4 * DC], F32)
        s_gamma = sbuf("s_gamma", [128, DC], F32)
        s_gw = sbuf("s_gw", [128, DC * NE], BF16)
        s_gbrow = sbuf("s_gbrow", [1, NE], BF16)
        s_b1g = sbuf("s_b1g", [128, DC * NE], F32)
        s_b1l = sbuf("s_b1l", [128, DC * NE], F32)
        s_b1gs = sbuf("s_b1gs", [128, DC * NE], F32)
        s_zero8 = sbuf("s_zero8", [128, NE], F32)
        s_eps = sbuf("s_eps", [1, 8], F32)
        s_x1 = sbuf("s_x1", [128, DC * 512], F32)

        U1 = Arena(nc, es, "U1", 96 * KB)
        s_wq = U1.tensor("s_wq", [128, DC * D], BF16, 0)
        s_wk = U1.tensor("s_wk", [128, DC * D], BF16, 16 * KB)
        s_wv = U1.tensor("s_wv", [128, DC * D], BF16, 32 * KB)
        s_wo = U1.tensor("s_wo", [128, DC * D], BF16, 48 * KB)
        s_cos = U1.tensor("s_cos", [128, S], F32, 64 * KB)
        s_sin = U1.tensor("s_sin", [128, S], F32, 80 * KB)
        s_wf = [U1.tensor(f"s_wf{i}", [128, 3 * DC * D], BF16, i * 48 * KB)
                for i in range(2)]
        s_num = U1.tensor("s_num", [128, 1024], F32, 64 * KB)
        s_att2 = U1.tensor("s_att2", [128, DC * 512], BF16, 68 * KB)

        U2 = Arena(nc, es, "U2", 32 * KB)
        s_q = U2.tensor("s_q", [128, DC * T], BF16, 0)
        c_acc = U2.tensor("c_acc", [128, DC * 512], F32, 0)
        c_xb = U2.tensor("c_xb", [128, DC * 512], BF16, 16 * KB)
        c_sq = U2.tensor("c_sq", [128, DC * 512], BF16, 24 * KB)

        U3 = Arena(nc, es, "U3", 8 * KB)
        a_ln1 = U3.tensor("a_ln1", [128, DC * ATC], BF16, 0)
        s_att = U3.tensor("s_att", [128, DC * 512], BF16, 0)
        s_sel = U3.tensor("s_sel", [NE, NE * 128], BF16, 0)
        c_sig1 = U3.tensor("c_sig1", [128, 512], F32, 2 * KB)
        c_t1a1 = U3.tensor("c_t1a1", [128, 512], F32, 4 * KB)
        c_lc1 = U3.tensor("c_lc1", [128, 512], F32, 6 * KB)

        U4 = Arena(nc, es, "U4", 16 * KB)
        a_xin = U4.tensor("a_xin", [128, DC * ATC], F32, 0)
        s_kh = [U4.tensor(f"s_kh{i}", [128, S], BF16, i * 8 * KB) for i in range(2)]
        c_xc = U4.tensor("c_xc", [128, DC * 512], BF16, 0)
        c_nrm = U4.tensor("c_nrm", [128, DC * 512], BF16, 8 * KB)

        U5 = Arena(nc, es, "U5", 16672)
        a_kst = U5.tensor("a_kst", [128, DC * ATC], BF16, 0)
        a_vst = [U5.tensor(f"a_vst{i}", [128, 16 * 65], BF16, 8 * KB + 2112 * i)
                 for i in range(2)]
        a_scrA = U5.tensor("a_scrA", [128, 512], F32, 12416)
        a_scrB = U5.tensor("a_scrB", [128, 512], F32, 14464)
        s_vh = [U5.tensor(f"s_vh{i}", [128, 32 * 130], BF16, i * 8320) for i in range(2)]
        c_act = U5.tensor("c_act", [128, DC * 512], BF16, 0)
        c_sig = U5.tensor("c_sig", [128, 512], F32, 8 * KB)
        c_t1a = U5.tensor("c_t1a", [128, 512], F32, 10 * KB)
        c_lc = U5.tensor("c_lc", [128, 512], F32, 12 * KB)
        c_ap = U5.tensor("c_ap", [128, 512], F32, 14 * KB)

        U6 = Arena(nc, es, "U6", 16 * KB)
        a_sq = U6.tensor("a_sq", [128, DC * 256], BF16, 0)
        s_stat = U6.tensor("s_stat", [1, 4 * 512], F32, 4 * KB)
        s_statb = U6.tensor("s_statb", [1, 2 * 512], BF16, 12 * KB)
        s_bvrow = U6.tensor("s_bvrow", [1, D], BF16, 14 * KB)
        s_ptb = [U6.tensor(f"s_ptb{i}", [128, 2048], BF16, i * 4 * KB) for i in range(2)]
        s_rec = U6.tensor("s_rec", [1, 2 * 512], F32, 8 * KB)
        s_recb = U6.tensor("s_recb", [1, 2 * 512], BF16, 12 * KB)
        s_rb = U6.tensor("s_rb", [64, 2 * 512], BF16, 14 * KB)
        c_ap1 = U6.tensor("c_ap1", [128, 512], F32, 4 * KB)
        c_gt = U6.tensor("c_gt", [128, 4 * 14 * NE], F32, 0)
        c_gtb = U6.tensor("c_gtb", [128, 4 * 2 * NE], BF16, 1792)
        c_gsc = U6.tensor("c_gsc", [128, 4 * 8], F32, 1920)
        s_b2 = U6.tensor("s_b2", [NE, D], BF16, 2 * KB)
        c_cbs = U6.tensor("c_cbs", [128, 512], F32, 14 * KB)

        U7 = Arena(nc, es, "U7", 4 * KB)
        a_xb = U7.tensor("a_xb", [128, DC * 256], BF16, 0)
        c_ct = U7.tensor("c_ct", [NE, T], BF16, 0)

        PBS = es.enter_context(nc.psum_tensor("pbs", [128, 2048], F32))
        PB2 = es.enter_context(nc.psum_tensor("pb2", [128, 1024], F32))
        PB3 = es.enter_context(nc.psum_tensor("pb3", [128, 1024], F32))
        PB = [PBS[:, 0:1024], PBS[:, 1024:2048], PB2[:], PB3[:]]

        P = Plan()
        SM = [P.sem(n) for n in (
            "ld_c", "ld_w", "ld_x", "ld_kv", "ld_wf", "ld_x1",
            "pe_a", "pe_b", "pe_c", "ac_a", "ac_b", "ac_c",
            "dv_a", "dv_b", "dv_c", "po_a", "po_c",
            "sp_k", "sp_v", "sp_o", "st_o", "st_d")]
        (ld_c, ld_w, ld_x, ld_kv, ld_wf, ld_x1,
         pe_a, pe_b, pe_c, ac_a, ac_b, ac_c,
         dv_a, dv_b, dv_c, po_a, po_c,
         sp_k, sp_v, sp_o, st_o, st_d) = SM

        def cview(t, n):
            return t[:].rearrange("p (c n) -> p c n", c=DC)

        xin_v = cview(a_xin, ATC)
        axb_v = cview(a_xb, 256)
        asq_v = cview(a_sq, 256)
        ln1_v = cview(a_ln1, ATC)
        kst_v = cview(a_kst, ATC)
        vst_v = [t[:].rearrange("p (j c) -> p j c", j=16) for t in a_vst]
        q_v = s_q[:].rearrange("p (c n) -> p c n", c=DC)
        wq_v = cview(s_wq, D)
        wk_v = cview(s_wk, D)
        wv_v = cview(s_wv, D)
        wo_v = cview(s_wo, D)
        vh_v = [t[:].rearrange("p (k c) -> p k c", k=32) for t in s_vh]
        att_v = cview(s_att, 512)
        att2_v = cview(s_att2, 512)
        x1_v = cview(s_x1, 512)
        wf_v = [t[:].rearrange("p (w c n) -> p w c n", w=3, c=DC) for t in s_wf]
        c_sigs = [c_sig, c_sig1]
        c_t1as = [c_t1a, c_t1a1]
        c_lcs = [c_lc, c_lc1]
        c_aps = [c_ap, c_ap1]
        xc_v = cview(c_xc, 512)
        nrm_v = cview(c_nrm, 512)
        acc_v = cview(c_acc, 512)
        act_v = cview(c_act, 512)
        cxb_v = cview(c_xb, 512)
        csq_v = cview(c_sq, 512)
        gw_v = s_gw[:].rearrange("p (c n) -> p c n", c=DC)
        gt_v = c_gt[:].rearrange("p (i n) -> p i n", i=4)
        gtb_v = c_gtb[:].rearrange("p (i n) -> p i n", i=4)
        gsc_v = c_gsc[:].rearrange("p (i n) -> p i n", i=4)
        x_fm_v = x_fm.rearrange("(c p) t -> p c t", p=128)
        o_sp_v = o_sp.rearrange("(c p) t -> p c t", p=128)
        out_fm_v = out_fm.rearrange("(c p) t -> p c t", p=128)

        st_m = s_stat[0:1, 0:512]
        st_m2 = s_stat[0:1, 512:1024]
        st_ln = s_stat[0:1, 1024:1536]
        st_rstd = s_stat[0:1, 1536:2048]

        nc.const_aps.register(F32, float(EPS), s_eps[0:1, 0:1])
        P.op("pool", lambda e: e.dma_start(out=wq_v,
                                           in_=wqT.rearrange("(c p) n -> p c n", p=128)), inc=ld_w, inc_n=16)
        P.op("pool", lambda e: e.dma_start(out=wk_v,
                                           in_=wkT.rearrange("(c p) n -> p c n", p=128)), inc=ld_w, inc_n=16)
        P.op("pool", lambda e: e.dma_start(out=wv_v,
                                           in_=wvT.rearrange("(c p) n -> p c n", p=128)), inc=ld_w, inc_n=16)
        P.op("pool", lambda e: e.dma_start(out=wo_v,
                                           in_=woT.rearrange("(c p) n -> p c n", p=128)), inc=ld_w, inc_n=16)
        P.op("pool", lambda e: e.dma_start(out=s_bvrow[:], in_=bv_row), inc=ld_c, inc_n=16)
        P.op("pool", lambda e: e.dma_start(out=s_b1g[:], in_=b1g_in), inc=ld_c, inc_n=16)
        P.op("pool", lambda e: e.dma_start(out=s_b1l[:], in_=b1l_in), inc=ld_c, inc_n=16)
        P.op("sp", lambda e: e.dma_start(out=s_bias[:], in_=b_qkvo), inc=ld_c, inc_n=16)
        P.op("sp", lambda e: e.dma_start(out=s_gamma[:], in_=gamma_in), inc=ld_c, inc_n=16)
        P.op("sp", lambda e: e.dma_start(out=s_cos[:], in_=cos_in), inc=ld_c, inc_n=16)
        P.op("sp", lambda e: e.dma_start(out=s_sin[:], in_=sin_in), inc=ld_c, inc_n=16)
        P.op("sp", lambda e: e.dma_start(out=gw_v,
                                         in_=wg_in.rearrange("(c p) n -> p c n", p=128)), inc=ld_c, inc_n=16)
        P.op("sp", lambda e: e.dma_start(out=s_gbrow[:], in_=gb_row), inc=ld_c, inc_n=16)
        P.op("sp", lambda e: e.dma_start(out=s_ones[:], in_=ones_bf_in), inc=ld_c, inc_n=16)
        P.op("sp", lambda e: e.dma_start(out=s_id[:], in_=ident_in), inc=ld_c, inc_n=16)
        n_ldc = ld_c.v
        n_ldw = ld_w.v
        P.op("dve", lambda e: e.memset(s_zero8[:], 0.0), waits=[(ld_c, n_ldc)])
        P.op("dve", lambda e: e.memset(s_eps[:], float(EPS)))
        P.op("dve", lambda e: e.tensor_scalar_mul(s_b1gs[:], s_b1g[:], ALPHA))
        P.op("dve", lambda e: e.tensor_scalar_add(s_b1l[:], s_b1l[:], 1.0))
        P.drain("dve", inc=dv_a)
        P.wait("pe", ld_c, n_ldc)
        P.wait("pe", ld_w, n_ldw)
        P.wait("pool", ld_c, n_ldc)

        _skipA = _LVL < 1
        rec_xin_free = 0
        rec_ln1_free = 0
        rec_kst_free = 0
        rec_xb_a = 0
        rec_stats_free = 0
        rec_mbrb_ac = 0
        rec_pqk_free = [0, 0]
        rec_scrA_free = 0
        rec_pv_free = 0
        rec_vst_free = [0, 0]
        p_sum1 = PB[0][0:1, 0:512]
        p_sum2 = PB[0][0:1, 512:1024]
        p_mb = PB[1][:, 0:512]
        p_rb = PB[1][:, 512:1024]
        p_qks = [PB[2][:, 0:512], PB[3][:, 512:1024]]
        p_v = [PB[2][:, 512:1024], PB[3][:, 0:512]]

        for tc in range(NATC if not _skipA else 0):
            ts = slice(tc * ATC, (tc + 1) * ATC)
            own = tc < NATC // 2
            P.op("sp", lambda e, ts=ts: e.dma_start(out=xin_v, in_=x_fm_v[:, :, ts]),
                 waits=[(dv_a, rec_xin_free)], inc=ld_x, inc_n=16)
            ldx_here = ld_x.v
            for hf in range(2):
                hsl = slice(tc * ATC + hf * 256, tc * ATC + (hf + 1) * 256)
                lsl = slice(hf * 256, (hf + 1) * 256)
                P.op("dve", lambda e, lsl=lsl: e.tensor_copy(
                    out=a_xb[:], in_=a_xin[:].rearrange("p (c n) -> p c n", c=DC)[:, :, lsl]),
                    waits=[(ld_x, ldx_here), (pe_a, rec_xb_a)])
                P.drain("dve", inc=dv_a)
                dv_xb = dv_a.v
                P.op("act", lambda e: e.activation(out=a_sq[:], in_=a_xb[:], func=AF.Square),
                     waits=[(dv_a, dv_xb)], inc=ac_a)
                ac_sq = ac_a.v
                for ci in range(DC):
                    P.op("pe", lambda e, ci=ci, hf=hf: e.matmul(
                        p_sum1[:, hf * 256:(hf + 1) * 256], s_ones[:, 0:1], axb_v[:, ci, :],
                        start=(ci == 0), stop=(ci == DC - 1)),
                        waits=([(dv_a, dv_xb)] + ([(dv_a, rec_stats_free)] if hf == 0 else [])) if ci == 0 else [],
                        inc=(pe_a if ci == DC - 1 else None))
                for ci in range(DC):
                    P.op("pe", lambda e, ci=ci, hf=hf: e.matmul(
                        p_sum2[:, hf * 256:(hf + 1) * 256], s_ones[:, 0:1], asq_v[:, ci, :],
                        start=(ci == 0), stop=(ci == DC - 1)),
                        waits=([(ac_a, ac_sq)] if ci == 0 else []),
                        inc=(pe_a if ci == DC - 1 else None))
                rec_xb_a = pe_a.v
            pe_stats = pe_a.v
            P.op("dve", lambda e: e.tensor_scalar_mul(st_m, p_sum1, 1.0 / D),
                 waits=[(pe_a, pe_stats)])
            P.drain("dve")
            P.op("dve", lambda e: e.tensor_mul(st_m2, st_m, st_m))
            P.drain("dve")
            P.op("dve", lambda e: e.scalar_tensor_tensor(
                out=st_m2, in0=p_sum2, scalar=1.0 / D, in1=st_m2,
                op0=ALU.mult, op1=ALU.subtract))
            P.drain("dve", inc=dv_a)
            dv_var = dv_a.v
            rec_stats_free = dv_var
            P.op("act", lambda e: e.activation(out=st_ln, in_=st_m2, func=AF.Ln, bias=EPS),
                 waits=[(dv_a, dv_var)])
            P.op("act", lambda e: e.drain())
            P.op("act", lambda e: e.activation(out=st_rstd, in_=st_ln, func=AF.Exp, scale=-0.5),
                 inc=ac_a)
            ac_rstd = ac_a.v
            P.op("dve", lambda e: e.tensor_copy(out=s_statb[0:1, 0:512], in_=st_m),
                 waits=[(ac_a, ac_rstd)])
            P.op("dve", lambda e: e.tensor_copy(out=s_statb[0:1, 512:1024], in_=st_rstd))
            P.drain("dve", inc=dv_a)
            dv_statb = dv_a.v
            P.op("pe", lambda e: e.matmul(p_mb, s_ones[0:1, :], s_statb[0:1, 0:512],
                                          start=True, stop=True),
                 waits=[(dv_a, dv_statb), (ac_a, rec_mbrb_ac)])
            P.op("pe", lambda e: e.matmul(p_rb, s_ones[0:1, :], s_statb[0:1, 512:1024],
                                          start=True, stop=True), inc=pe_a)
            pe_bc = pe_a.v
            P.op("act", lambda e: e.activation(out=a_scrA[:], in_=p_mb, func=AF.Copy),
                 waits=[(pe_a, pe_bc), (po_a, rec_scrA_free)])
            P.op("act", lambda e: e.activation(out=a_scrB[:], in_=p_rb, func=AF.Copy),
                 inc=ac_a)
            ac_mbrb = ac_a.v
            rec_mbrb_ac = ac_mbrb
            for ci in range(DC):
                P.op("pool", lambda e, ci=ci: e.tensor_tensor(
                    out=xin_v[:, ci, :], in0=xin_v[:, ci, :], in1=a_scrA[:],
                    op=ALU.subtract),
                    waits=([(ac_a, ac_mbrb)] if ci == 0 else []))
            for ci in range(DC):
                P.op("pool", lambda e, ci=ci: e.tensor_mul(
                    ln1_v[:, ci, :], xin_v[:, ci, :], a_scrB[:]),
                    waits=([(pe_a, rec_ln1_free)] if ci == 0 else []),
                    inc=(po_a if ci == DC - 1 else None))
            po_ln1 = po_a.v
            P.wait("dve", po_a, po_ln1)
            P.drain("dve", inc=dv_a)
            dv_ln1 = dv_a.v
            rec_xin_free = dv_ln1
            if debug:
                P.op("pool", lambda e, ts=ts: e.dma_start(
                    out=dbg["ln1"].rearrange("(c p) t -> p c t", p=128)[:, :, ts],
                    in_=ln1_v), waits=[(dv_a, dv_ln1)], inc=st_d, inc_n=16)

            for which in (("q",) if own else ()) + ("k",):
                wsel = wq_v if which == "q" else wk_v
                boff = 0 if which == "q" else DC
                for do in range(DC):
                    par = do % 2
                    p_qk = p_qks[par]
                    for ci in range(DC):
                        P.op("pe", lambda e, wsel=wsel, do=do, ci=ci, p_qk=p_qk: e.matmul(
                            p_qk, wsel[:, ci, do * 128:(do + 1) * 128],
                            ln1_v[:, ci, :],
                            start=(ci == 0), stop=(ci == DC - 1)),
                            waits=([(dv_a, dv_ln1), (dv_a, rec_pqk_free[par])] if ci == 0 else []),
                            inc=(pe_a if ci == DC - 1 else None))
                    pe_proj = pe_a.v
                    if which == "q":
                        dst = q_v[:, do, ts]
                    else:
                        dst = kst_v[:, do, :]
                    bslice = s_bias[:, boff + do:boff + do + 1]
                    dwaits = [(pe_a, pe_proj)]
                    if which == "k" and do == 0:
                        dwaits.append((sp_k, rec_kst_free))
                    P.op("dve", lambda e, dst=dst, bslice=bslice, ts=ts, p_qk=p_qk: e.scalar_tensor_tensor(
                        out=dst, in0=p_qk, scalar=bslice, in1=s_cos[:, ts],
                        op0=ALU.add, op1=ALU.mult), waits=dwaits)
                    P.op("dve", lambda e, bslice=bslice, ts=ts, p_qk=p_qk: e.scalar_tensor_tensor(
                        out=a_scrB[:], in0=p_qk, scalar=bslice, in1=s_sin[:, ts],
                        op0=ALU.add, op1=ALU.mult), inc=dv_a)
                    rec_pqk_free[par] = dv_a.v
                    P.drain("dve")
                    mask = [i ^ 1 for i in range(32)]
                    P.op("dve", lambda e, mask=mask: e.stream_shuffle(
                        out=a_scrA[:], in_=a_scrB[:], mask=mask),
                        waits=[(po_a, rec_scrA_free)], inc=dv_a)
                    dv_shf = dv_a.v
                    P.op("pool", lambda e, dst=dst: e.tensor_add(dst, dst, a_scrA[:]),
                         waits=[(dv_a, dv_shf)], inc=po_a)
                    rec_scrA_free = po_a.v
                po_rope = po_a.v
                if which == "k":
                    for hp in range(DC):
                        P.op("pool", lambda e, hp=hp, ts=ts: e.dma_start(
                            out=k_sp[hp, :, ts], in_=kst_v[:, hp, :]),
                            inc=sp_k, inc_n=16)
                    rec_kst_free = sp_k.v
                elif debug:
                    P.op("pool", lambda e, ts=ts: e.dma_start(
                        out=dbg["q"].rearrange("(c p) t -> p c t", p=128)[:, :, ts],
                        in_=q_v[:, :, ts]), inc=st_d, inc_n=16)

            for tt in range(ATC // 128):
                vb = tt % 2
                tsl = slice(tt * 128, (tt + 1) * 128)
                for nv in range(2):
                    for ci in range(DC):
                        P.op("pe", lambda e, nv=nv, ci=ci, tsl=tsl: e.matmul(
                            p_v[nv], ln1_v[:, ci, tsl],
                            wv_v[:, ci, nv * 512:(nv + 1) * 512],
                            start=(ci == 0), stop=False),
                            waits=([(dv_a, rec_pv_free)] if (ci == 0 and nv == 0) else []))
                    P.op("pe", lambda e, nv=nv: e.matmul(
                        p_v[nv], s_ones[0:1, 0:128], s_bvrow[0:1, nv * 512:(nv + 1) * 512],
                        start=False, stop=True), inc=pe_a)
                pe_vp = pe_a.v
                P.op("dve", lambda e, vb=vb: e.tensor_copy(
                    out=vst_v[vb][:, 0:8, 0:64],
                    in_=p_v[0].rearrange("p (j c) -> p j c", j=8)),
                    waits=[(pe_a, pe_vp), (sp_v, rec_vst_free[vb])])
                P.op("dve", lambda e, vb=vb: e.tensor_copy(
                    out=vst_v[vb][:, 8:16, 0:64],
                    in_=p_v[1].rearrange("p (j c) -> p j c", j=8)))
                P.op("dve", lambda e, vb=vb: e.memset(vst_v[vb][:, :, 64:65], 1.0),
                     inc=dv_a)
                dv_vst = dv_a.v
                rec_pv_free = dv_vst
                tok0 = tc * ATC + tt * 128
                for hp in range(DC):
                    P.op("pool", lambda e, hp=hp, tok0=tok0, vb=vb: e.dma_start(
                        out=v_sp[hp, tok0:tok0 + 128, :],
                        in_=a_vst[vb][:, hp * 130:(hp + 1) * 130]),
                        waits=([(dv_a, dv_vst)] if hp == 0 else []),
                        inc=sp_v, inc_n=16)
                rec_vst_free[vb] = sp_v.v
            rec_ln1_free = pe_a.v

        a_dv_end = dv_a.v
        a_pe_end = pe_a.v
        a_po_end = po_a.v
        a_spk_end = sp_k.v
        a_spv_end = sp_v.v

        _skipB = _LVL < 2
        p_sc = [PB[0][:, 0:1024], PB[1][:, 0:1024]]
        p_pv = [PB[2][:, 0:512], PB[2][:, 512:1024]]
        p_rbc = PB[3][0:64, 0:512]
        p_wo = PB[3][:, 512:1024]
        att_b = [att_v, att2_v]
        pt_slot = [s_ptb[0][:, 0:1024], s_ptb[0][:, 1024:2048], s_ptb[1][:, 0:1024]]

        rec_kh_free = [0, 0]
        rec_vh_free = [0, 0]
        rec_exp_done = [0, 0]
        rec_slot_pv = [0, 0, 0]
        rec_slot_exp = [0, 0, 0]
        rec_num_evac = [0, 0]
        rec_pwo = [0]
        rec_rbc = [0]
        dv_att_q = [0] * NQC
        spill_q = [0] * NQC

        def emit_wo_mm(ci, wo_qc, wo_do):
            wv = att_b[wo_qc % 2]
            w_waits = []
            if ci == 0:
                w_waits = [(dv_b, dv_att_q[wo_qc]), (dv_b, rec_pwo[0])]
            P.op("pe", lambda e, ci=ci, wo_do=wo_do, wv=wv: e.matmul(
                p_wo, wo_v[:, ci, wo_do * 128:(wo_do + 1) * 128], wv[:, ci, :],
                start=(ci == 0), stop=(ci == DC - 1)),
                waits=w_waits, inc=(pe_b if ci == DC - 1 else None))

        def emit_wo_evac(wo_qc, wo_do):
            pe_wo = pe_b.v
            e_waits = [(pe_b, pe_wo)]
            if wo_do == 0 and wo_qc >= 1:
                e_waits.append((sp_o, spill_q[wo_qc - 1]))
            P.op("dve", lambda e, wo_do=wo_do: e.tensor_scalar(
                out=x1_v[:, wo_do, :], in0=p_wo,
                scalar1=s_bias[:, 3 * DC + wo_do:3 * DC + wo_do + 1],
                scalar2=None, op0=ALU.add),
                waits=e_waits, inc=dv_b)
            rec_pwo[0] = dv_b.v

        def emit_spill(wo_qc):
            wqs = slice(wo_qc * 512, (wo_qc + 1) * 512)
            dv_o = dv_b.v
            P.op("pool", lambda e, wqs=wqs: e.dma_start(
                out=o_sp_v[:, :, wqs], in_=x1_v),
                waits=[(dv_b, dv_o)], inc=sp_o, inc_n=16)
            spill_q[wo_qc] = sp_o.v

        pend_wo = None
        for qc in range(NQC if not _skipB else 0):
            qs = slice(qc * 512, (qc + 1) * 512)
            av = att_b[qc % 2]
            for hp in range(DC):
                kb = hp % 2
                P.op("sp", lambda e, kb=kb, hp=hp: e.dma_start(
                    out=s_kh[kb][:], in_=k_sp[hp]),
                    waits=[(sp_k, a_spk_end), (dv_a, a_dv_end), (po_a, a_po_end),
                           (pe_b, rec_kh_free[kb])],
                    inc=ld_kv, inc_n=16)
                P.op("sp", lambda e, kb=kb, hp=hp: e.dma_start(
                    out=vh_v[kb], in_=v_sp[hp].rearrange("(k p) c -> p k c", p=128)),
                    waits=[(sp_v, a_spv_end), (sp_k, a_spk_end), (pe_b, rec_vh_free[kb])],
                    inc=ld_kv, inc_n=16)
                ldkv_here = ld_kv.v

                def emit_scores(kt):
                    par = kt % 2
                    sl = kt % 3
                    ksl = slice(kt * 128, (kt + 1) * 128)
                    sc_waits = [(ld_kv, ldkv_here), (ac_b, rec_exp_done[par])]
                    P.op("pe", lambda e, ksl=ksl, par=par, kb=kb, hp=hp, qs=qs: e.matmul(
                        p_sc[par][:, 0:512], s_kh[kb][0:64, ksl], q_v[0:64, hp, qs],
                        start=True, stop=True, tile_position=(0, 0)),
                        waits=sc_waits)
                    P.op("pe", lambda e, ksl=ksl, par=par, kb=kb, hp=hp, qs=qs: e.matmul(
                        p_sc[par][:, 512:1024], s_kh[kb][64:128, ksl], q_v[64:128, hp, qs],
                        start=True, stop=True, tile_position=(64, 0)),
                        inc=pe_b)
                    pe_sc_here = pe_b.v
                    P.op("act", lambda e, par=par, sl=sl: e.activation(
                        out=pt_slot[sl], in_=p_sc[par][:, 0:1024], func=AF.Exp, scale=0.125),
                        waits=[(pe_b, pe_sc_here), (pe_b, rec_slot_pv[sl])],
                        inc=ac_b)
                    rec_exp_done[par] = ac_b.v
                    rec_slot_exp[sl] = ac_b.v

                def emit_pv(kt):
                    sl = kt % 3
                    pv0_waits = [(ac_b, rec_slot_exp[sl])]
                    pv1_waits = []
                    if kt == 0:
                        pv0_waits.append((dv_b, rec_num_evac[0]))
                        pv1_waits = [(dv_b, rec_num_evac[1])]
                    P.op("pe", lambda e, kt=kt, sl=sl, kb=kb: e.matmul(
                        p_pv[0][0:65, :], vh_v[kb][:, kt, 0:65],
                        pt_slot[sl][:, 0:512],
                        start=(kt == 0), stop=(kt == 31)),
                        waits=pv0_waits)
                    P.op("pe", lambda e, kt=kt, sl=sl, kb=kb: e.matmul(
                        p_pv[1][0:65, :], vh_v[kb][:, kt, 65:130],
                        pt_slot[sl][:, 512:1024],
                        start=(kt == 0), stop=(kt == 31)),
                        waits=pv1_waits, inc=pe_b)
                    rec_slot_pv[sl] = pe_b.v

                for kt in range(33):
                    if kt < 32:
                        emit_scores(kt)
                        if pend_wo is not None and 16 <= kt < 24:
                            emit_wo_mm(kt - 16, pend_wo, hp)
                            if kt == 23:
                                emit_wo_evac(pend_wo, hp)
                                if hp == DC - 1:
                                    emit_spill(pend_wo)
                    if kt >= 1:
                        emit_pv(kt - 1)
                rec_kh_free[kb] = pe_b.v
                rec_vh_free[kb] = pe_b.v
                pe_pv_end = pe_b.v
                P.op("dve", lambda e: e.tensor_copy(
                    out=s_num[0:65, 0:512], in_=p_pv[0][0:65, :]),
                    waits=[(pe_b, pe_pv_end)], inc=dv_b)
                rec_num_evac[0] = dv_b.v
                P.op("dve", lambda e: e.tensor_copy(
                    out=s_num[0:65, 512:1024], in_=p_pv[1][0:65, :]), inc=dv_b)
                rec_num_evac[1] = dv_b.v
                P.drain("dve")
                P.op("dve", lambda e: e.reciprocal(
                    out=s_rec[0:1, 0:1024], in_=s_num[64:65, 0:1024]))
                P.drain("dve")
                P.op("dve", lambda e: e.tensor_copy(out=s_recb[:], in_=s_rec[:]),
                     inc=dv_b)
                dv_recb = dv_b.v
                for j in range(2):
                    P.op("pe", lambda e, j=j: e.matmul(
                        p_rbc, s_ones[0:1, 0:64], s_recb[0:1, j * 512:(j + 1) * 512],
                        start=True, stop=True),
                        waits=[(dv_b, dv_recb if j == 0 else 0),
                               (dv_b, rec_rbc[0])],
                        inc=pe_b)
                    pe_rbc = pe_b.v
                    P.op("dve", lambda e, j=j: e.tensor_copy(
                        out=s_rb[0:64, j * 512:(j + 1) * 512], in_=p_rbc),
                        waits=[(pe_b, pe_rbc)], inc=dv_b)
                    rec_rbc[0] = dv_b.v
                P.drain("dve")
                for j in range(2):
                    P.op("dve", lambda e, j=j, hp=hp, av=av: e.tensor_mul(
                        av[64 * j:64 * (j + 1), hp, :], s_num[0:64, j * 512:(j + 1) * 512],
                        s_rb[0:64, j * 512:(j + 1) * 512]))
                P.drain("dve", inc=dv_b)
            dv_att_q[qc] = dv_b.v
            if debug:
                P.op("pool", lambda e, qs=qs, av=av: e.dma_start(
                    out=dbg["attn"].rearrange("(c p) t -> p c t", p=128)[:, :, qs],
                    in_=av), waits=[(dv_b, dv_att_q[qc])], inc=st_d, inc_n=16)
            pend_wo = qc
            if qc == NQC - 1:
                for do in range(DC):
                    for ci in range(DC):
                        emit_wo_mm(ci, qc, do)
                    emit_wo_evac(qc, do)
                emit_spill(qc)

        b_spo_end = sp_o.v
        b_dv_end = dv_b.v
        b_pe_end = pe_b.v
        b_ac_end = ac_b.v

        _skipC = _LVL < 3
        p_sum1c = PB[0][0:1, 0:512]
        p_sum2c = PB[0][0:1, 512:1024]
        p_mb2 = PB[1][:, 0:512]
        p_rb2 = PB[1][:, 512:1024]
        p_lg = PB[2][:, 0:512]
        p_ct = PB[0][0:8, 512:640]
        p_cb = PB[1][:, 0:512]
        p_gl = [(PB[2][:, 0:512], PB[2][:, 512:1024]),
                (PB[1][:, 0:512], PB[1][:, 512:1024])]
        p_p2 = [PB[3][:, 0:512], PB[3][:, 512:1024]]

        rec_wf_free = [0, 0]
        rec_statc_free = 0
        rec_mbrb2_free = 0
        rec_gl_free = [0, 0]
        rec_pl_free = [0, 0]
        rec_sig_free = [0, 0]
        rec_lc_free = [0, 0]
        rec_ap_free = [0, 0]
        rec_ta_free = [0, 0]
        rec_p2_free = [0, 0]
        rec_cb_free = 0
        rec_cbs_ac = 0
        rec_x1_free = 0

        wf_loads = [(tc4, e_) for tc4 in range(NQC) for e_ in range(NE)]

        def emit_wf_load(idx):
            tc4, e_ = wf_loads[idx]
            wb = idx % 2
            waits = [(pe_c, rec_wf_free[wb])]
            if idx == 0:
                waits.append((pe_a, a_pe_end))
            if idx == 1:
                waits.append((pe_b, b_pe_end))
                waits.append((dv_b, b_dv_end))
            for wi, wt in ((0, w1gT), (1, w1lT), (2, w2T)):
                P.op("sp", lambda e, wb=wb, wi=wi, wt=wt, e_=e_: e.dma_start(
                    out=wf_v[wb][:, wi, :, :],
                    in_=wt[e_].rearrange("(c p) n -> p c n", p=128)),
                    waits=(waits if wi == 0 else []), inc=ld_wf, inc_n=16)
            return ld_wf.v

        if not _skipC:
            P.op("sp", lambda e: e.dma_start(out=s_sel[0:NE, :], in_=sel_in),
                 waits=[(pe_b, b_pe_end), (ac_b, b_ac_end)], inc=ld_x1, inc_n=16)
            P.op("sp", lambda e: e.dma_start(out=s_b2[0:NE, :], in_=b2_in),
                 inc=ld_x1, inc_n=16)
            sel_ld = ld_x1.v
            wf_ready = [0] * len(wf_loads)
            wf_ready[0] = emit_wf_load(0)
            wf_ready[1] = emit_wf_load(1)
            next_wf = 2

        for tc4 in range(_CT4 if not _skipC else 0):
            qs = slice(tc4 * 512, (tc4 + 1) * 512)
            ct_s = slice(tc4 * 512, (tc4 + 1) * 512)
            P.op("sp", lambda e, qs=qs: e.dma_start(out=x1_v, in_=o_sp_v[:, :, qs]),
                 waits=[(sp_o, b_spo_end), (dv_c, rec_x1_free)],
                 inc=ld_x1, inc_n=16)
            P.op("pool", lambda e, qs=qs: e.dma_start(out=acc_v, in_=x_fm_v[:, :, qs]),
                 waits=[(pe_b, b_pe_end), (dv_c, rec_x1_free)], inc=ld_x1, inc_n=16)
            ldx1_here = ld_x1.v
            P.op("dve", lambda e: e.tensor_add(s_x1[:], s_x1[:], c_acc[:]),
                 waits=[(ld_x1, ldx1_here), (dv_b, b_dv_end)])
            P.drain("dve", inc=dv_c)
            dv_x1c = dv_c.v
            if debug:
                P.op("pool", lambda e, qs=qs: e.dma_start(
                    out=dbg["x1"].rearrange("(c p) t -> p c t", p=128)[:, :, qs],
                    in_=x1_v), waits=[(dv_c, dv_x1c)], inc=st_d, inc_n=16)
            P.op("dve", lambda e: e.tensor_copy(out=c_xb[:], in_=s_x1[:]))
            P.drain("dve", inc=dv_c)
            dv_xbc = dv_c.v
            P.op("act", lambda e: e.activation(out=c_sq[:], in_=c_xb[:], func=AF.Square),
                 waits=[(dv_c, dv_xbc)], inc=ac_c)
            ac_sqc = ac_c.v
            for ci in range(DC):
                P.op("pe", lambda e, ci=ci: e.matmul(
                    p_sum1c, s_ones[:, 0:1], cxb_v[:, ci, :],
                    start=(ci == 0), stop=(ci == DC - 1)),
                    waits=([(dv_c, dv_xbc), (dv_c, rec_statc_free), (pe_b, b_pe_end)] if ci == 0 else []),
                    inc=(pe_c if ci == DC - 1 else None))
            for ci in range(DC):
                P.op("pe", lambda e, ci=ci: e.matmul(
                    p_sum2c, s_ones[:, 0:1], csq_v[:, ci, :],
                    start=(ci == 0), stop=(ci == DC - 1)),
                    waits=([(ac_c, ac_sqc)] if ci == 0 else []),
                    inc=(pe_c if ci == DC - 1 else None))
            pe_statc = pe_c.v
            P.op("dve", lambda e: e.tensor_scalar_mul(st_m, p_sum1c, 1.0 / D),
                 waits=[(pe_c, pe_statc)])
            P.drain("dve")
            P.op("dve", lambda e: e.tensor_mul(st_m2, st_m, st_m))
            P.drain("dve")
            P.op("dve", lambda e: e.scalar_tensor_tensor(
                out=st_m2, in0=p_sum2c, scalar=1.0 / D, in1=st_m2,
                op0=ALU.mult, op1=ALU.subtract))
            P.drain("dve", inc=dv_c)
            dv_varc = dv_c.v
            P.op("act", lambda e: e.activation(out=st_ln, in_=st_m2, func=AF.Ln, bias=EPS),
                 waits=[(dv_c, dv_varc)])
            P.op("act", lambda e: e.drain())
            P.op("act", lambda e: e.activation(out=st_rstd, in_=st_ln, func=AF.Exp, scale=-0.5),
                 inc=ac_c)
            ac_rstdc = ac_c.v
            P.op("dve", lambda e: e.tensor_copy(out=s_statb[0:1, 0:512], in_=st_m),
                 waits=[(ac_c, ac_rstdc)])
            P.op("dve", lambda e: e.tensor_copy(out=s_statb[0:1, 512:1024], in_=st_rstd))
            P.drain("dve", inc=dv_c)
            dv_statbc = dv_c.v
            P.op("pe", lambda e: e.matmul(p_mb2, s_ones[0:1, :], s_statb[0:1, 0:512],
                                          start=True, stop=True),
                 waits=[(dv_c, dv_statbc), (dv_c, rec_mbrb2_free), (ac_c, rec_cbs_ac)])
            P.op("pe", lambda e: e.matmul(p_rb2, s_ones[0:1, :], s_statb[0:1, 512:1024],
                                          start=True, stop=True), inc=pe_c)
            pe_bc2 = pe_c.v
            for ci in range(DC):
                P.op("dve", lambda e, ci=ci: e.tensor_tensor(
                    out=acc_v[:, ci, :], in0=x1_v[:, ci, :], in1=p_mb2, op=ALU.subtract),
                    waits=([(pe_c, pe_bc2)] if ci == 0 else []))
            P.drain("dve")
            for ci in range(DC):
                P.op("dve", lambda e, ci=ci: e.tensor_mul(
                    xc_v[:, ci, :], acc_v[:, ci, :], p_rb2),
                    waits=([(pe_b, b_pe_end)] if ci == 0 else []))
            P.drain("dve", inc=dv_c)
            dv_xc = dv_c.v
            P.op("act", lambda e: e.activation(out=c_sq[:], in_=c_xc[:], func=AF.Square),
                 waits=[(dv_c, dv_xc)], inc=ac_c)
            ac_sq2 = ac_c.v
            for ci in range(DC):
                P.op("pe", lambda e, ci=ci: e.matmul(
                    p_sum1c, s_ones[:, 0:1], csq_v[:, ci, :],
                    start=(ci == 0), stop=(ci == DC - 1)),
                    waits=([(ac_c, ac_sq2), (dv_c, dv_varc)] if ci == 0 else []),
                    inc=(pe_c if ci == DC - 1 else None))
            pe_rms = pe_c.v
            P.op("dve", lambda e: e.tensor_scalar_mul(st_m, p_sum1c, 1.0 / D),
                 waits=[(pe_c, pe_rms)])
            P.drain("dve", inc=dv_c)
            dv_rmsv = dv_c.v
            P.op("act", lambda e: e.activation(out=st_ln, in_=st_m, func=AF.Ln, bias=EPS),
                 waits=[(dv_c, dv_rmsv)])
            P.op("act", lambda e: e.drain())
            P.op("act", lambda e: e.activation(out=st_rstd, in_=st_ln, func=AF.Exp, scale=-0.5),
                 inc=ac_c)
            ac_rms = ac_c.v
            P.op("dve", lambda e: e.tensor_copy(out=s_statb[0:1, 512:1024], in_=st_rstd),
                 waits=[(ac_c, ac_rms)])
            P.drain("dve", inc=dv_c)
            dv_rmsb = dv_c.v
            P.op("pe", lambda e: e.matmul(p_rb2, s_ones[0:1, :], s_statb[0:1, 512:1024],
                                          start=True, stop=True),
                 waits=[(dv_c, dv_rmsb), (dv_c, dv_xc)], inc=pe_c)
            pe_rmsb = pe_c.v
            for ci in range(DC):
                P.op("dve", lambda e, ci=ci: e.scalar_tensor_tensor(
                    out=nrm_v[:, ci, :], in0=xc_v[:, ci, :], scalar=s_gamma[:, ci:ci + 1],
                    in1=p_rb2, op0=ALU.mult, op1=ALU.mult),
                    waits=([(pe_c, pe_rmsb)] if ci == 0 else []))
            P.drain("dve", inc=dv_c)
            dv_nrm = dv_c.v
            rec_mbrb2_free = dv_nrm
            if debug:
                P.op("pool", lambda e, qs=qs: e.dma_start(
                    out=dbg["normed"].rearrange("(c p) t -> p c t", p=128)[:, :, qs],
                    in_=nrm_v), waits=[(dv_c, dv_nrm)], inc=st_d, inc_n=16)

            pe_lg = [0] * 4
            for tt in range(4):
                tsl = slice(tt * 128, (tt + 1) * 128)
                for ci in range(DC):
                    P.op("pe", lambda e, ci=ci, tsl=tsl, tt=tt: e.matmul(
                        p_lg[:, tt * 64:tt * 64 + 8], nrm_v[:, ci, tsl], gw_v[:, ci, :],
                        start=(ci == 0), stop=False),
                        waits=([(dv_c, dv_nrm), (dv_c, rec_gl_free[0])] if (ci == 0 and tt == 0) else []))
                P.op("pe", lambda e, tt=tt: e.matmul(
                    p_lg[:, tt * 64:tt * 64 + 8], s_ones[0:1, 0:128],
                    s_gbrow[0:1, :], start=False, stop=True),
                    inc=pe_c)
                pe_lg[tt] = pe_c.v

            def gstep(build4, waits=(), inc=None):
                for tt in range(4):
                    P.op("dve", lambda e, tt=tt: build4(e, tt),
                         waits=(waits if tt == 0 else []),
                         inc=(inc if tt == 3 else None))
                P.drain("dve")
            g = gt_v
            gb_ = gtb_v
            gs = gsc_v
            gstep(lambda e, tt: e.tensor_copy(out=gb_[:, tt, 0:8],
                                              in_=p_lg[:, tt * 64:tt * 64 + 8]),
                  waits=[(pe_c, pe_lg[3])])
            gstep(lambda e, tt: e.tensor_copy(out=g[:, tt, 0:8], in_=gb_[:, tt, 0:8]))
            gstep(lambda e, tt: e.max(out=g[:, tt, 8:16], in_=g[:, tt, 0:8]))
            gstep(lambda e, tt: e.tensor_sub(gs[:, tt, 0:1], g[:, tt, 9:10], g[:, tt, 8:9]))
            for tt in range(4):
                P.op("dve", lambda e, tt=tt: e.tensor_scalar(
                    out=g[:, tt, 16:24], in0=g[:, tt, 0:8],
                    scalar1=g[:, tt, 8:9], scalar2=None, op0=ALU.is_equal))
            gstep(lambda e, tt: e.tensor_scalar(
                out=g[:, tt, 24:32], in0=g[:, tt, 0:8],
                scalar1=g[:, tt, 9:10], scalar2=None, op0=ALU.is_equal), inc=dv_c)
            dv_eq = dv_c.v
            for tt in range(4):
                P.op("act", lambda e, tt=tt: e.activation(
                    out=gs[:, tt, 4:5], in_=gs[:, tt, 0:1], func=AF.Sigmoid),
                    waits=([(dv_c, dv_eq)] if tt == 0 else []))
            for tt in range(4):
                P.op("act", lambda e, tt=tt: e.activation(
                    out=gs[:, tt, 3:4], in_=gs[:, tt, 0:1], func=AF.Sigmoid, scale=-1.0),
                    inc=(ac_c if tt == 3 else None))
            ac_z = ac_c.v
            P.wait("dve", ac_c, ac_z)
            gstep(lambda e, tt: e.tensor_tensor_scan(
                out=g[:, tt, 32:40], data0=g[:, tt, 16:24], data1=s_zero8[:],
                initial=0.0, op0=ALU.add, op1=ALU.add))
            gstep(lambda e, tt: e.tensor_scalar(
                out=g[:, tt, 40:48], in0=g[:, tt, 32:40],
                scalar1=1.0, scalar2=None, op0=ALU.is_equal))
            gstep(lambda e, tt: e.tensor_mul(g[:, tt, 48:56], g[:, tt, 40:48], g[:, tt, 16:24]))
            gstep(lambda e, tt: e.tensor_scalar(
                out=g[:, tt, 56:64], in0=g[:, tt, 48:56],
                scalar1=-1.0, scalar2=1.0, op0=ALU.mult, op1=ALU.add))
            gstep(lambda e, tt: e.tensor_mul(g[:, tt, 64:72], g[:, tt, 24:32], g[:, tt, 56:64]))
            gstep(lambda e, tt: e.tensor_tensor_scan(
                out=g[:, tt, 72:80], data0=g[:, tt, 64:72], data1=s_zero8[:],
                initial=0.0, op0=ALU.add, op1=ALU.add))
            gstep(lambda e, tt: e.tensor_scalar(
                out=g[:, tt, 80:88], in0=g[:, tt, 72:80],
                scalar1=1.0, scalar2=None, op0=ALU.is_equal))
            gstep(lambda e, tt: e.tensor_mul(g[:, tt, 88:96], g[:, tt, 80:88], g[:, tt, 64:72]))
            gstep(lambda e, tt: e.tensor_scalar(
                out=g[:, tt, 96:104], in0=g[:, tt, 48:56],
                scalar1=gs[:, tt, 3:4], scalar2=None, op0=ALU.mult))
            gstep(lambda e, tt: e.scalar_tensor_tensor(
                out=g[:, tt, 104:112], in0=g[:, tt, 88:96],
                scalar=gs[:, tt, 4:5], in1=g[:, tt, 96:104],
                op0=ALU.mult, op1=ALU.add), inc=dv_c)
            dv_comb = dv_c.v
            for tt in range(4):
                P.op("pe", lambda e, tt=tt: e.transpose(p_ct, g[:, tt, 104:112], s_id[:]),
                     waits=([(dv_c, dv_comb), (dv_c, rec_statc_free)] if tt == 0
                            else [(dv_c, dv_c.v)]),
                     inc=pe_c)
                pe_ct = pe_c.v
                P.op("dve", lambda e, tc4=tc4, tt=tt: e.tensor_copy(
                    out=c_ct[0:8, tc4 * 512 + tt * 128: tc4 * 512 + (tt + 1) * 128],
                    in_=p_ct), waits=[(pe_c, pe_ct)])
                P.drain("dve", inc=dv_c)
            dv_ct = dv_c.v
            rec_statc_free = dv_ct
            rec_gl_free[0] = max(rec_gl_free[0], dv_ct)
            if debug and tc4 == NQC - 1:
                P.op("pool", lambda e: e.dma_start(out=dbg["ct"], in_=c_ct[0:NE, :]),
                     waits=[(dv_c, dv_ct)], inc=st_d, inc_n=16)

            for e_ in range(_CNE):
                idx = tc4 * NE + e_
                wb = idx % 2
                wf_here = wf_ready[idx]
                P.op("pe", lambda e, e_=e_, ct_s=ct_s: e.matmul(
                    p_cb, s_sel[0:NE, e_ * 128:(e_ + 1) * 128], c_ct[0:NE, ct_s],
                    start=True, stop=True),
                    waits=[(dv_c, dv_ct), (ld_x1, sel_ld),
                           (dv_c, rec_mbrb2_free), (dv_c, rec_cb_free),
                           (ac_c, ac_c.v)],
                    inc=pe_c)
                pe_cb = pe_c.v
                P.op("act", lambda e: e.activation(out=c_cbs[:], in_=p_cb, func=AF.Copy),
                     waits=[(pe_c, pe_cb), (dv_c, rec_cb_free)], inc=ac_c)
                ac_cbs = ac_c.v
                rec_cbs_ac = ac_cbs
                dv_t1as = [0] * DC
                dv_lc2s = [0] * DC

                def emit_stage1(it):
                    par = it % 2
                    p_g, p_l = p_gl[par]
                    g_waits = [(ld_wf, wf_here), (dv_c, rec_gl_free[par])]
                    if par == 1 and it == 1:
                        g_waits.append((ac_c, ac_cbs))
                    for ci in range(DC):
                        P.op("pe", lambda e, it=it, ci=ci, p_g=p_g, wb=wb: e.matmul(
                            p_g, wf_v[wb][:, 0, ci, it * 128:(it + 1) * 128], nrm_v[:, ci, :],
                            start=(ci == 0), stop=(ci == DC - 1)),
                            waits=(g_waits if ci == 0 else []),
                            inc=(pe_c if ci == DC - 1 else None))
                    pe_g_here = pe_c.v
                    for ci in range(DC):
                        P.op("pe", lambda e, it=it, ci=ci, p_l=p_l, wb=wb: e.matmul(
                            p_l, wf_v[wb][:, 1, ci, it * 128:(it + 1) * 128], nrm_v[:, ci, :],
                            start=(ci == 0), stop=(ci == DC - 1)),
                            waits=([(ac_c, rec_pl_free[par])] if ci == 0 else []),
                            inc=(pe_c if ci == DC - 1 else None))
                    pe_l_here = pe_c.v
                    bofs = it * NE + e_
                    TA, SG, LC, AP_ = c_t1as[par], c_sigs[par], c_lcs[par], c_aps[par]
                    P.op("act", lambda e, bofs=bofs, p_g=p_g, SG=SG: e.activation(
                        out=SG[:], in_=p_g, func=AF.Sigmoid,
                        bias=s_b1gs[:, bofs:bofs + 1], scale=ALPHA),
                        waits=[(pe_c, pe_g_here), (dv_c, rec_sig_free[par])], inc=ac_c)
                    ac_sig = ac_c.v
                    P.op("act", lambda e, bofs=bofs, p_l=p_l, LC=LC: e.activation(
                        out=LC[:], in_=p_l, func=AF.Identity,
                        bias=s_b1l[:, bofs:bofs + 1]),
                        waits=[(pe_c, pe_l_here), (po_c, rec_lc_free[par])], inc=ac_c)
                    ac_lraw = ac_c.v
                    rec_pl_free[par] = ac_lraw
                    P.op("dve", lambda e, bofs=bofs, p_g=p_g, TA=TA, SG=SG: e.scalar_tensor_tensor(
                        out=TA[:], in0=p_g, scalar=s_b1g[:, bofs:bofs + 1],
                        in1=SG[:], op0=ALU.add, op1=ALU.mult),
                        waits=[(ac_c, ac_sig)], inc=dv_c)
                    dv_t1a = dv_c.v
                    dv_t1as[it] = dv_t1a
                    rec_gl_free[par] = dv_t1a
                    rec_sig_free[par] = dv_t1a
                    P.op("dve", lambda e, LC=LC: e.tensor_scalar(
                        out=LC[:], in0=LC[:], scalar1=-6.0, scalar2=8.0,
                        op0=ALU.max, op1=ALU.min),
                        waits=[(ac_c, ac_lraw)], inc=dv_c)
                    dv_lc2s[it] = dv_c.v

                def emit_stage2(it):
                    par = it % 2
                    TA, LC, AP_ = c_t1as[par], c_lcs[par], c_aps[par]
                    P.op("dve", lambda e, TA=TA, AP_=AP_: e.scalar_tensor_tensor(
                        out=AP_[:], in0=TA[:], scalar=CCAP, in1=c_cbs[:],
                        op0=ALU.min, op1=ALU.mult),
                        waits=[(ac_c, ac_cbs), (po_c, rec_ap_free[par])], inc=dv_c)
                    dv_z = dv_c.v
                    rec_ta_free[par] = dv_z
                    P.op("pool", lambda e, it=it, LC=LC, AP_=AP_: e.tensor_mul(
                        act_v[:, it, :], AP_[:], LC[:]),
                        waits=[(dv_c, dv_z), (dv_c, dv_lc2s[it])], inc=po_c)
                    rec_lc_free[par] = po_c.v
                    rec_ap_free[par] = po_c.v

                for it in range(DC + 1):
                    if it < DC:
                        emit_stage1(it)
                    if it >= 1:
                        P.drain("dve")
                        emit_stage2(it - 1)
                P.op("pool", lambda e: e.drain(), inc=po_c)
                po_act = po_c.v
                dv_act = dv_c.v
                rec_cb_free = dv_c.v
                for ht in range(DC):
                    hb = ht % 2
                    for ic in range(DC):
                        P.op("pe", lambda e, wb=wb, ht=ht, ic=ic, hb=hb: e.matmul(
                            p_p2[hb], wf_v[wb][:, 2, ic, ht * 128:(ht + 1) * 128],
                            act_v[:, ic, :],
                            start=(ic == 0), stop=(ic == DC - 1 and (e_ != NE - 1 or _CNE < NE))),
                            waits=([(po_c, po_act), (dv_c, rec_p2_free[hb])] if ic == 0 else []),
                            inc=(pe_c if (ic == DC - 1 and e_ != NE - 1) else None))
                    if e_ == NE - 1:
                        P.op("pe", lambda e, ht=ht, hb=hb, ct_s=ct_s: e.matmul(
                            p_p2[hb], s_b2[0:NE, ht * 128:(ht + 1) * 128], c_ct[0:NE, ct_s],
                            start=False, stop=True),
                            waits=[(ld_x1, sel_ld)] if ht == 0 else [], inc=pe_c)
                    pe_p2 = pe_c.v
                    if e_ == 0:
                        P.op("dve", lambda e, ht=ht, hb=hb: e.tensor_copy(
                            out=acc_v[:, ht, :], in_=p_p2[hb]),
                            waits=[(pe_c, pe_p2)], inc=dv_c)
                    else:
                        P.op("dve", lambda e, ht=ht, hb=hb: e.tensor_add(
                            acc_v[:, ht, :], acc_v[:, ht, :], p_p2[hb]),
                            waits=[(pe_c, pe_p2)], inc=dv_c)
                    rec_p2_free[hb] = dv_c.v
                P.drain("dve")
                rec_wf_free[wb] = pe_c.v
                if not _skipC and next_wf < len(wf_loads) and next_wf == idx + 2:
                    wf_ready[next_wf] = emit_wf_load(next_wf)
                    next_wf += 1
            pe_mlp2_end = pe_c.v
            if debug and tc4 == NQC - 1:
                P.op("pool", lambda e, qs=qs: e.dma_start(
                    out=dbg["acc"].rearrange("(c p) t -> p c t", p=128)[:, :, qs],
                    in_=acc_v), waits=[(dv_c, dv_c.v), (pe_c, pe_mlp2_end)], inc=st_d, inc_n=16)
            P.op("dve", lambda e: e.tensor_copy(out=c_act[:], in_=c_acc[:]),
                 waits=[(pe_c, pe_mlp2_end)])
            P.drain("dve")
            P.op("dve", lambda e: e.tensor_add(c_xb[:], c_xc[:], c_act[:]))
            P.drain("dve")
            P.op("dve", lambda e: e.tensor_add(s_x1[:], s_x1[:], c_xb[:]))
            P.drain("dve", inc=dv_c)
            dv_out = dv_c.v
            rec_x1_free = dv_out
            P.op("sp", lambda e, qs=qs: e.dma_start(out=out_fm_v[:, :, qs], in_=x1_v),
                 waits=[(dv_c, dv_out)], inc=st_o, inc_n=16)

        P.wait("sp", st_o, st_o.v)
        P.wait("sp", sp_k, sp_k.v)
        P.wait("sp", sp_v, sp_v.v)
        P.wait("sp", sp_o, sp_o.v)
        P.wait("sp", ld_c, ld_c.v)
        P.wait("sp", ld_w, ld_w.v)
        P.wait("sp", ld_wf, ld_wf.v)
        if debug:
            P.wait("sp", st_d, st_d.v)

        for s in P.sems.values():
            s.h = es.enter_context(nc.semaphore(s.name))

        blk = es.enter_context(nc.Block())

        @blk.tensor
        def _(t):
            for fn in P.ops["pe"]:
                fn(t)

        @blk.scalar
        def _(sc):
            for fn in P.ops["act"]:
                fn(sc)

        @blk.vector
        def _(v):
            for fn in P.ops["dve"]:
                fn(v)

        @blk.gpsimd
        def _(g):
            for fn in P.ops["pool"]:
                fn(g)

        @blk.sync
        def _(sy):
            for fn in P.ops["sp"]:
                fn(sy)

    return nc


def _sel_const():
    sel = np.zeros((NE, NE * 128), dtype=BF)
    for e in range(NE):
        sel[e, e * 128:(e + 1) * 128] = 1
    return sel


def make_inputs(inputs):
    x = np.asarray(inputs["L0_attn_x"])
    wq = np.asarray(inputs["L0_attn_Wq"]); bq = np.asarray(inputs["L0_attn_bq"])
    wk = np.asarray(inputs["L0_attn_Wk"]); bk = np.asarray(inputs["L0_attn_bk"])
    wv = np.asarray(inputs["L0_attn_Wv"]); bv = np.asarray(inputs["L0_attn_bv"])
    wo = np.asarray(inputs["L0_attn_Wo"]); bo = np.asarray(inputs["L0_attn_bo"])
    gamma = np.asarray(inputs["L0_ffn_norm_weight"])
    gw = np.asarray(inputs["L0_ffn_gate_weight"])
    gb = np.asarray(inputs["L0_ffn_gate_bias"])
    w1 = np.asarray(inputs["L0_ffn_mlp1_weight"])
    b1 = np.asarray(inputs["L0_ffn_mlp1_bias"])
    w2 = np.asarray(inputs["L0_ffn_mlp2_weight"])
    b2 = np.asarray(inputs["L0_ffn_mlp2_bias"])

    def chunked(b):
        return np.ascontiguousarray(b.reshape(DC, 128).T)

    def bias_pe(b):
        return np.ascontiguousarray(
            b.T.reshape(DC, 128, NE).transpose(1, 0, 2).reshape(128, DC * NE))

    shared = dict(
        wqT=np.ascontiguousarray(wq.T), wkT=np.ascontiguousarray(wk.T),
        wvT=np.ascontiguousarray(wv.T), woT=np.ascontiguousarray(wo.T),
        b_qkvo=np.ascontiguousarray(np.concatenate(
            [chunked(bq), chunked(bk), chunked(bv), chunked(bo)], axis=1)),
        bv_row=np.ascontiguousarray(bv.reshape(1, D)),
        gamma_in=chunked(gamma),
        wg_in=np.ascontiguousarray(gw.T),
        gb_row=np.ascontiguousarray(gb.reshape(1, NE)),
        w1gT=np.ascontiguousarray(w1[:, 0::2, :].transpose(0, 2, 1)),
        w1lT=np.ascontiguousarray(w1[:, 1::2, :].transpose(0, 2, 1)),
        w2T=np.ascontiguousarray(w2.transpose(0, 2, 1)),
        b1g_in=bias_pe(b1[:, 0::2]),
        b1l_in=bias_pe(b1[:, 1::2]),
        b2_in=np.ascontiguousarray(b2),
        ones_bf_in=np.ones((128, 128), dtype=BF),
        ident_in=np.eye(128, dtype=np.float32),
        sel_in=_sel_const(),
    )
    in_maps = []
    for c in range(8):
        b, h = c // 2, c % 2
        perm = np.concatenate([np.arange(h * T, S), np.arange(0, h * T)])
        cos_t, sin_t = _rope_tables(perm)
        xc = np.ascontiguousarray(x[b].T[:, perm])
        m = dict(shared)
        m.update(x_fm=xc, cos_in=cos_t, sin_in=sin_t)
        in_maps.append(m)
    return in_maps


def kernel(**inputs):
    in_maps = make_inputs(inputs)
    nc = build_program(debug=False)
    res = run_bass_kernel_spmd(nc, in_maps, core_ids=list(range(8)))
    out = np.empty((4, S, D), dtype=np.float32)
    for c in range(8):
        b, h = c // 2, c % 2
        out[b, h * T:(h + 1) * T, :] = np.asarray(res.results[c]["out_fm"]).T
    return out


# version 10
# speedup vs baseline: 1.1014x; 1.1014x over previous
import contextlib
import numpy as np
import ml_dtypes
import concourse.bass as bass
import concourse.mybir as mybir
from concourse.bass_utils import run_bass_kernel_spmd

F32 = mybir.dt.float32
BF16 = mybir.dt.bfloat16
AF = mybir.ActivationFunctionType
ALU = mybir.AluOpType
BF = ml_dtypes.bfloat16

D = 1024
S = 4096
T = 2048
HD = 64
NE = 8
DC = 8
ATC = 512
NATC = S // ATC
NQC = 4
EPS = 1e-5
ALPHA = 1.702
CCAP = float(np.float32(7.0) * np.float32(
    1.0 / (1.0 + np.exp(-np.float64(np.float32(ALPHA) * np.float32(7.0))))))

ENGS = ("pe", "act", "dve", "pool", "sp")


class Sem:
    def __init__(self, name):
        self.name = name
        self.h = None
        self.v = 0


class Plan:
    def __init__(self):
        self.ops = {e: [] for e in ENGS}
        self.sems = {}

    def sem(self, name):
        if name not in self.sems:
            self.sems[name] = Sem(name)
        return self.sems[name]

    def op(self, eng, build, waits=(), inc=None, inc_n=1):
        if inc is not None:
            inc.v += inc_n

        def fn(engine, waits=tuple(waits), build=build, inc=inc, inc_n=inc_n):
            for s, v in waits:
                if v > 0:
                    engine.wait_ge(s.h, v)
            ins = build(engine)
            if inc is not None:
                ins.then_inc(inc.h, inc_n)
        self.ops[eng].append(fn)

    def wait(self, eng, sem, val):
        def fn(engine, sem=sem, val=val):
            engine.wait_ge(sem.h, val)
        self.ops[eng].append(fn)

    def drain(self, eng, inc=None, inc_n=1):
        self.op(eng, lambda e: e.drain(), inc=inc, inc_n=inc_n)


def _rope_tables(perm):
    dim = np.float32(HD)
    inv_freq = (np.float32(1.0) / (np.float32(10000.0) ** (
        np.arange(0, HD, 2, dtype=np.float32) / dim))).astype(np.float32)
    pos = np.arange(S, dtype=np.float32)
    freqs = (pos[:, None] * inv_freq[None, :]).astype(np.float32)
    freqs = np.repeat(freqs, 2, axis=-1)
    cos_t = np.cos(freqs.astype(np.float64)).astype(np.float32)
    sin_t = np.sin(freqs.astype(np.float64)).astype(np.float32)
    sign = np.ones(HD, dtype=np.float32)
    sign[0::2] = -1.0
    sin_s = sin_t * sign[None, :]
    cos2 = np.concatenate([cos_t[perm].T, cos_t[perm].T], axis=0)
    sin2 = np.concatenate([sin_s[perm].T, sin_s[perm].T], axis=0)
    sinp = sin2.copy()
    sinp[0::2], sinp[1::2] = sin2[1::2], sin2[0::2]
    return np.ascontiguousarray(cos2), np.ascontiguousarray(sinp)


class Arena:

    def __init__(self, nc, es, name, nbytes):
        self.nc = nc
        self.name = name
        self.nbytes = nbytes
        self.backing = es.enter_context(
            nc.sbuf_tensor(name, [128, nbytes], mybir.dt.uint8))
        self.base = nc.lookup_mloc(self.backing).addr

    def tensor(self, name, shape, dtype, off):
        per_part = int(np.prod(shape[1:])) * mybir.dt.size(dtype)
        assert off + per_part <= self.nbytes, (self.name, name, off, per_part)
        assert (self.base + off) % 32 == 0, (name, self.base, off)
        h = self.nc.virtual_sbuf_tensor(name, shape, dtype)
        self.nc._tensor(h.name, shape, dtype, type="SB")
        mloc = self.nc.lookup_mloc(h)
        mloc.allocated = True
        mloc.addr = self.base + off
        return h


def build_program(debug=False, stop_after=None, c_lim=None):
    _LVL = {"pre": 0, "A": 1, "B": 2, "C": 3, None: 3}[stop_after]
    _CT4, _CNE = c_lim if c_lim is not None else (NQC, NE)
    nc = bass.Bass("TRN2", target_bir_lowering=False, debug=False)

    x_fm = nc.dram_tensor("x_fm", [D, S], F32, kind="ExternalInput").ap()
    cos_in = nc.dram_tensor("cos_in", [128, S], F32, kind="ExternalInput").ap()
    sin_in = nc.dram_tensor("sin_in", [128, S], F32, kind="ExternalInput").ap()
    wqT = nc.dram_tensor("wqT", [D, D], F32, kind="ExternalInput").ap()
    wkT = nc.dram_tensor("wkT", [D, D], F32, kind="ExternalInput").ap()
    wvT = nc.dram_tensor("wvT", [D, D], F32, kind="ExternalInput").ap()
    woT = nc.dram_tensor("woT", [D, D], F32, kind="ExternalInput").ap()
    b_qkvo = nc.dram_tensor("b_qkvo", [128, 4 * DC], F32, kind="ExternalInput").ap()
    bv_row = nc.dram_tensor("bv_row", [1, D], F32, kind="ExternalInput").ap()
    gamma_in = nc.dram_tensor("gamma_in", [128, DC], F32, kind="ExternalInput").ap()
    wg_in = nc.dram_tensor("wg_in", [D, NE], BF16, kind="ExternalInput").ap()
    gb_row = nc.dram_tensor("gb_row", [1, NE], BF16, kind="ExternalInput").ap()
    w1gT = nc.dram_tensor("w1gT", [NE, D, D], BF16, kind="ExternalInput").ap()
    w1lT = nc.dram_tensor("w1lT", [NE, D, D], BF16, kind="ExternalInput").ap()
    w2T = nc.dram_tensor("w2T", [NE, D, D], BF16, kind="ExternalInput").ap()
    b1g_in = nc.dram_tensor("b1g_in", [128, DC * NE], BF16, kind="ExternalInput").ap()
    b1l_in = nc.dram_tensor("b1l_in", [128, DC * NE], BF16, kind="ExternalInput").ap()
    b2_in = nc.dram_tensor("b2_in", [NE, D], BF16, kind="ExternalInput").ap()
    ones_bf_in = nc.dram_tensor("ones_bf_in", [128, 128], BF16, kind="ExternalInput").ap()
    ident_in = nc.dram_tensor("ident_in", [128, 128], F32, kind="ExternalInput").ap()
    sel_in = nc.dram_tensor("sel_in", [NE, NE * 128], BF16, kind="ExternalInput").ap()

    out_fm = nc.dram_tensor("out_fm", [D, T], F32, kind="ExternalOutput").ap()

    k_sp = nc.dram_tensor("k_sp", [DC, 128, S], BF16, kind="Internal").ap()
    v_sp = nc.dram_tensor("v_sp", [DC, S, 130], BF16, kind="Internal").ap()
    o_sp = nc.dram_tensor("o_sp", [D, T], F32, kind="Internal").ap()

    dbg = {}
    if debug:
        dbg["ln1"] = nc.dram_tensor("d_ln1", [D, S], BF16, kind="ExternalOutput").ap()
        dbg["q"] = nc.dram_tensor("d_q", [D, T], BF16, kind="ExternalOutput").ap()
        dbg["attn"] = nc.dram_tensor("d_attn", [D, T], BF16, kind="ExternalOutput").ap()
        dbg["x1"] = nc.dram_tensor("d_x1", [D, T], F32, kind="ExternalOutput").ap()
        dbg["ct"] = nc.dram_tensor("d_ct", [NE, T], BF16, kind="ExternalOutput").ap()
        dbg["normed"] = nc.dram_tensor("d_normed", [D, T], BF16, kind="ExternalOutput").ap()
        dbg["acc"] = nc.dram_tensor("d_acc", [D, T], F32, kind="ExternalOutput").ap()

    es = contextlib.ExitStack()
    with es:
        KB = 1024

        def sbuf(name, shape, dt):
            return es.enter_context(nc.sbuf_tensor(name, shape, dt))

        s_ones = sbuf("s_ones", [128, 128], BF16)
        s_id = sbuf("s_id", [128, 128], F32)
        s_bias = sbuf("s_bias", [128, 4 * DC], F32)
        s_gamma = sbuf("s_gamma", [128, DC], F32)
        s_gw = sbuf("s_gw", [128, DC * NE], BF16)
        s_gbrow = sbuf("s_gbrow", [1, NE], BF16)
        s_b1g = sbuf("s_b1g", [128, DC * NE], F32)
        s_b1l = sbuf("s_b1l", [128, DC * NE], F32)
        s_b1gs = sbuf("s_b1gs", [128, DC * NE], F32)
        s_zero8 = sbuf("s_zero8", [128, NE], F32)
        s_eps = sbuf("s_eps", [1, 8], F32)
        s_x1 = sbuf("s_x1", [128, DC * 512], F32)

        U1 = Arena(nc, es, "U1", 96 * KB)
        s_wq = U1.tensor("s_wq", [128, DC * D], BF16, 0)
        s_wk = U1.tensor("s_wk", [128, DC * D], BF16, 16 * KB)
        s_wv = U1.tensor("s_wv", [128, DC * D], BF16, 32 * KB)
        s_wo = U1.tensor("s_wo", [128, DC * D], BF16, 48 * KB)
        s_cos = U1.tensor("s_cos", [128, S], F32, 64 * KB)
        s_sin = U1.tensor("s_sin", [128, S], F32, 80 * KB)
        s_wf = [U1.tensor(f"s_wf{i}", [128, 3 * DC * D], BF16, i * 48 * KB)
                for i in range(2)]
        s_num = U1.tensor("s_num", [128, 1024], F32, 64 * KB)
        s_att2 = U1.tensor("s_att2", [128, DC * 512], BF16, 68 * KB)

        U2 = Arena(nc, es, "U2", 32 * KB)
        s_q = U2.tensor("s_q", [128, DC * T], BF16, 0)
        c_acc = U2.tensor("c_acc", [128, DC * 512], F32, 0)
        c_xb = U2.tensor("c_xb", [128, DC * 512], BF16, 16 * KB)
        c_sq = U2.tensor("c_sq", [128, DC * 512], BF16, 24 * KB)

        U3 = Arena(nc, es, "U3", 8 * KB)
        a_ln1 = U3.tensor("a_ln1", [128, DC * ATC], BF16, 0)
        s_att = U3.tensor("s_att", [128, DC * 512], BF16, 0)
        s_sel = U3.tensor("s_sel", [NE, NE * 128], BF16, 0)
        c_sig1 = U3.tensor("c_sig1", [128, 512], F32, 2 * KB)
        c_t1a1 = U3.tensor("c_t1a1", [128, 512], F32, 4 * KB)
        c_lc1 = U3.tensor("c_lc1", [128, 512], F32, 6 * KB)

        U4 = Arena(nc, es, "U4", 16 * KB)
        a_xin = U4.tensor("a_xin", [128, DC * ATC], F32, 0)
        s_kh = [U4.tensor(f"s_kh{i}", [128, S], BF16, i * 8 * KB) for i in range(2)]
        c_xc = U4.tensor("c_xc", [128, DC * 512], BF16, 0)
        c_nrm = U4.tensor("c_nrm", [128, DC * 512], BF16, 8 * KB)

        U5 = Arena(nc, es, "U5", 16672)
        a_kst = U5.tensor("a_kst", [128, DC * ATC], BF16, 0)
        a_vst = [U5.tensor(f"a_vst{i}", [128, 16 * 65], BF16, 8 * KB + 2112 * i)
                 for i in range(2)]
        a_scrA = U5.tensor("a_scrA", [128, 512], F32, 12416)
        a_scrB = U5.tensor("a_scrB", [128, 512], F32, 14464)
        s_vh = [U5.tensor(f"s_vh{i}", [128, 32 * 130], BF16, i * 8320) for i in range(2)]
        c_act = U5.tensor("c_act", [128, DC * 512], BF16, 0)
        c_sig = U5.tensor("c_sig", [128, 512], F32, 8 * KB)
        c_t1a = U5.tensor("c_t1a", [128, 512], F32, 10 * KB)
        c_lc = U5.tensor("c_lc", [128, 512], F32, 12 * KB)
        c_ap = U5.tensor("c_ap", [128, 512], F32, 14 * KB)

        U6 = Arena(nc, es, "U6", 16 * KB)
        a_sq = U6.tensor("a_sq", [128, DC * 256], BF16, 0)
        s_stat = U6.tensor("s_stat", [1, 4 * 512], F32, 4 * KB)
        s_statb = U6.tensor("s_statb", [1, 2 * 512], BF16, 12 * KB)
        s_bvrow = U6.tensor("s_bvrow", [1, D], BF16, 14 * KB)
        s_ptb = [U6.tensor(f"s_ptb{i}", [128, 2048], BF16, i * 4 * KB) for i in range(2)]
        s_rec = U6.tensor("s_rec", [1, 2 * 512], F32, 8 * KB)
        s_recb = U6.tensor("s_recb", [1, 2 * 512], BF16, 12 * KB)
        s_rb = U6.tensor("s_rb", [64, 2 * 512], BF16, 14 * KB)
        c_ap1 = U6.tensor("c_ap1", [128, 512], F32, 4 * KB)
        c_gt = U6.tensor("c_gt", [128, 4 * 14 * NE], F32, 0)
        c_gtb = U6.tensor("c_gtb", [128, 4 * 2 * NE], BF16, 1792)
        c_gsc = U6.tensor("c_gsc", [128, 4 * 8], F32, 1920)
        s_b2 = U6.tensor("s_b2", [NE, D], BF16, 2 * KB)
        c_cbs = U6.tensor("c_cbs", [128, 512], F32, 14 * KB)

        U7 = Arena(nc, es, "U7", 4 * KB)
        a_xb = U7.tensor("a_xb", [128, DC * 256], BF16, 0)
        c_ct = U7.tensor("c_ct", [NE, T], BF16, 0)

        PBS = es.enter_context(nc.psum_tensor("pbs", [128, 2048], F32))
        PB2 = es.enter_context(nc.psum_tensor("pb2", [128, 1024], F32))
        PB3 = es.enter_context(nc.psum_tensor("pb3", [128, 1024], F32))
        PB = [PBS[:, 0:1024], PBS[:, 1024:2048], PB2[:], PB3[:]]

        P = Plan()
        SM = [P.sem(n) for n in (
            "ld_c", "ld_w", "ld_x", "ld_kv", "ld_wf", "ld_x1",
            "pe_a", "pe_b", "pe_c", "ac_a", "ac_b", "ac_c",
            "dv_a", "dv_b", "dv_c", "po_a", "po_c",
            "sp_k", "sp_v", "sp_o", "st_o", "st_d")]
        (ld_c, ld_w, ld_x, ld_kv, ld_wf, ld_x1,
         pe_a, pe_b, pe_c, ac_a, ac_b, ac_c,
         dv_a, dv_b, dv_c, po_a, po_c,
         sp_k, sp_v, sp_o, st_o, st_d) = SM

        def cview(t, n):
            return t[:].rearrange("p (c n) -> p c n", c=DC)

        xin_v = cview(a_xin, ATC)
        axb_v = cview(a_xb, 256)
        asq_v = cview(a_sq, 256)
        ln1_v = cview(a_ln1, ATC)
        kst_v = cview(a_kst, ATC)
        vst_v = [t[:].rearrange("p (j c) -> p j c", j=16) for t in a_vst]
        q_v = s_q[:].rearrange("p (c n) -> p c n", c=DC)
        wq_v = cview(s_wq, D)
        wk_v = cview(s_wk, D)
        wv_v = cview(s_wv, D)
        wo_v = cview(s_wo, D)
        vh_v = [t[:].rearrange("p (k c) -> p k c", k=32) for t in s_vh]
        att_v = cview(s_att, 512)
        att2_v = cview(s_att2, 512)
        x1_v = cview(s_x1, 512)
        wf_v = [t[:].rearrange("p (w c n) -> p w c n", w=3, c=DC) for t in s_wf]
        c_sigs = [c_sig, c_sig1]
        c_t1as = [c_t1a, c_t1a1]
        c_lcs = [c_lc, c_lc1]
        c_aps = [c_ap, c_ap1]
        xc_v = cview(c_xc, 512)
        nrm_v = cview(c_nrm, 512)
        acc_v = cview(c_acc, 512)
        act_v = cview(c_act, 512)
        cxb_v = cview(c_xb, 512)
        csq_v = cview(c_sq, 512)
        gw_v = s_gw[:].rearrange("p (c n) -> p c n", c=DC)
        gt_v = c_gt[:].rearrange("p (i n) -> p i n", i=4)
        gtb_v = c_gtb[:].rearrange("p (i n) -> p i n", i=4)
        gsc_v = c_gsc[:].rearrange("p (i n) -> p i n", i=4)
        x_fm_v = x_fm.rearrange("(c p) t -> p c t", p=128)
        o_sp_v = o_sp.rearrange("(c p) t -> p c t", p=128)
        out_fm_v = out_fm.rearrange("(c p) t -> p c t", p=128)

        st_m = s_stat[0:1, 0:512]
        st_m2 = s_stat[0:1, 512:1024]
        st_ln = s_stat[0:1, 1024:1536]
        st_rstd = s_stat[0:1, 1536:2048]

        nc.const_aps.register(F32, float(EPS), s_eps[0:1, 0:1])
        P.op("pool", lambda e: e.dma_start(out=wq_v,
                                           in_=wqT.rearrange("(c p) n -> p c n", p=128)), inc=ld_w, inc_n=16)
        P.op("pool", lambda e: e.dma_start(out=wk_v,
                                           in_=wkT.rearrange("(c p) n -> p c n", p=128)), inc=ld_w, inc_n=16)
        P.op("pool", lambda e: e.dma_start(out=wv_v,
                                           in_=wvT.rearrange("(c p) n -> p c n", p=128)), inc=ld_w, inc_n=16)
        P.op("pool", lambda e: e.dma_start(out=wo_v,
                                           in_=woT.rearrange("(c p) n -> p c n", p=128)), inc=ld_w, inc_n=16)
        P.op("pool", lambda e: e.dma_start(out=s_bvrow[:], in_=bv_row), inc=ld_c, inc_n=16)
        P.op("pool", lambda e: e.dma_start(out=s_b1g[:], in_=b1g_in), inc=ld_c, inc_n=16)
        P.op("pool", lambda e: e.dma_start(out=s_b1l[:], in_=b1l_in), inc=ld_c, inc_n=16)
        P.op("sp", lambda e: e.dma_start(out=s_bias[:], in_=b_qkvo), inc=ld_c, inc_n=16)
        P.op("sp", lambda e: e.dma_start(out=s_gamma[:], in_=gamma_in), inc=ld_c, inc_n=16)
        P.op("sp", lambda e: e.dma_start(out=s_cos[:], in_=cos_in), inc=ld_c, inc_n=16)
        P.op("sp", lambda e: e.dma_start(out=s_sin[:], in_=sin_in), inc=ld_c, inc_n=16)
        P.op("sp", lambda e: e.dma_start(out=gw_v,
                                         in_=wg_in.rearrange("(c p) n -> p c n", p=128)), inc=ld_c, inc_n=16)
        P.op("sp", lambda e: e.dma_start(out=s_gbrow[:], in_=gb_row), inc=ld_c, inc_n=16)
        P.op("sp", lambda e: e.dma_start(out=s_ones[:], in_=ones_bf_in), inc=ld_c, inc_n=16)
        P.op("sp", lambda e: e.dma_start(out=s_id[:], in_=ident_in), inc=ld_c, inc_n=16)
        n_ldc = ld_c.v
        n_ldw = ld_w.v
        P.op("dve", lambda e: e.memset(s_zero8[:], 0.0), waits=[(ld_c, n_ldc)])
        P.op("dve", lambda e: e.memset(s_eps[:], float(EPS)))
        P.op("dve", lambda e: e.tensor_scalar_mul(s_b1gs[:], s_b1g[:], ALPHA))
        P.op("dve", lambda e: e.tensor_scalar_add(s_b1l[:], s_b1l[:], 1.0))
        P.drain("dve", inc=dv_a)
        P.wait("pe", ld_c, n_ldc)
        P.wait("pe", ld_w, n_ldw)
        P.wait("pool", ld_c, n_ldc)

        _skipA = _LVL < 1
        rec_xin_free = 0
        rec_ln1_free = 0
        rec_kst_free = 0
        rec_xb_a = 0
        rec_stats_free = 0
        rec_mbrb_ac = 0
        rec_pqk_free = [0, 0]
        rec_scrA_free = 0
        rec_pv_free = 0
        rec_vst_free = [0, 0]
        p_sum1 = PB[0][0:1, 0:512]
        p_sum2 = PB[0][0:1, 512:1024]
        p_mb = PB[1][:, 0:512]
        p_rb = PB[1][:, 512:1024]
        p_qks = [PB[2][:, 0:512], PB[3][:, 512:1024]]
        p_v = [PB[2][:, 512:1024], PB[3][:, 0:512]]

        for tc in range(NATC if not _skipA else 0):
            ts = slice(tc * ATC, (tc + 1) * ATC)
            own = tc < NATC // 2
            P.op("sp", lambda e, ts=ts: e.dma_start(out=xin_v, in_=x_fm_v[:, :, ts]),
                 waits=[(dv_a, rec_xin_free)], inc=ld_x, inc_n=16)
            ldx_here = ld_x.v
            for hf in range(2):
                hsl = slice(tc * ATC + hf * 256, tc * ATC + (hf + 1) * 256)
                lsl = slice(hf * 256, (hf + 1) * 256)
                P.op("dve", lambda e, lsl=lsl: e.tensor_copy(
                    out=a_xb[:], in_=a_xin[:].rearrange("p (c n) -> p c n", c=DC)[:, :, lsl]),
                    waits=[(ld_x, ldx_here), (pe_a, rec_xb_a)])
                P.drain("dve", inc=dv_a)
                dv_xb = dv_a.v
                P.op("act", lambda e: e.activation(out=a_sq[:], in_=a_xb[:], func=AF.Square),
                     waits=[(dv_a, dv_xb)], inc=ac_a)
                ac_sq = ac_a.v
                for ci in range(DC):
                    P.op("pe", lambda e, ci=ci, hf=hf: e.matmul(
                        p_sum1[:, hf * 256:(hf + 1) * 256], s_ones[:, 0:1], axb_v[:, ci, :],
                        start=(ci == 0), stop=(ci == DC - 1)),
                        waits=([(dv_a, dv_xb)] + ([(dv_a, rec_stats_free)] if hf == 0 else [])) if ci == 0 else [],
                        inc=(pe_a if ci == DC - 1 else None))
                for ci in range(DC):
                    P.op("pe", lambda e, ci=ci, hf=hf: e.matmul(
                        p_sum2[:, hf * 256:(hf + 1) * 256], s_ones[:, 0:1], asq_v[:, ci, :],
                        start=(ci == 0), stop=(ci == DC - 1)),
                        waits=([(ac_a, ac_sq)] if ci == 0 else []),
                        inc=(pe_a if ci == DC - 1 else None))
                rec_xb_a = pe_a.v
            pe_stats = pe_a.v
            P.op("dve", lambda e: e.tensor_scalar_mul(st_m, p_sum1, 1.0 / D),
                 waits=[(pe_a, pe_stats)])
            P.drain("dve")
            P.op("dve", lambda e: e.tensor_mul(st_m2, st_m, st_m))
            P.drain("dve")
            P.op("dve", lambda e: e.scalar_tensor_tensor(
                out=st_m2, in0=p_sum2, scalar=1.0 / D, in1=st_m2,
                op0=ALU.mult, op1=ALU.subtract))
            P.drain("dve", inc=dv_a)
            dv_var = dv_a.v
            rec_stats_free = dv_var
            P.op("act", lambda e: e.activation(out=st_ln, in_=st_m2, func=AF.Ln, bias=EPS),
                 waits=[(dv_a, dv_var)])
            P.op("act", lambda e: e.drain())
            P.op("act", lambda e: e.activation(out=st_rstd, in_=st_ln, func=AF.Exp, scale=-0.5),
                 inc=ac_a)
            ac_rstd = ac_a.v
            P.op("dve", lambda e: e.tensor_copy(out=s_statb[0:1, 0:512], in_=st_m),
                 waits=[(ac_a, ac_rstd)])
            P.op("dve", lambda e: e.tensor_copy(out=s_statb[0:1, 512:1024], in_=st_rstd))
            P.drain("dve", inc=dv_a)
            dv_statb = dv_a.v
            P.op("pe", lambda e: e.matmul(p_mb, s_ones[0:1, :], s_statb[0:1, 0:512],
                                          start=True, stop=True),
                 waits=[(dv_a, dv_statb), (ac_a, rec_mbrb_ac)])
            P.op("pe", lambda e: e.matmul(p_rb, s_ones[0:1, :], s_statb[0:1, 512:1024],
                                          start=True, stop=True), inc=pe_a)
            pe_bc = pe_a.v
            P.op("act", lambda e: e.activation(out=a_scrA[:], in_=p_mb, func=AF.Copy),
                 waits=[(pe_a, pe_bc), (po_a, rec_scrA_free)])
            P.op("act", lambda e: e.activation(out=a_scrB[:], in_=p_rb, func=AF.Copy),
                 inc=ac_a)
            ac_mbrb = ac_a.v
            rec_mbrb_ac = ac_mbrb
            for ci in range(DC):
                P.op("pool", lambda e, ci=ci: e.tensor_tensor(
                    out=xin_v[:, ci, :], in0=xin_v[:, ci, :], in1=a_scrA[:],
                    op=ALU.subtract),
                    waits=([(ac_a, ac_mbrb)] if ci == 0 else []))
            for ci in range(DC):
                P.op("pool", lambda e, ci=ci: e.tensor_mul(
                    ln1_v[:, ci, :], xin_v[:, ci, :], a_scrB[:]),
                    waits=([(pe_a, rec_ln1_free)] if ci == 0 else []),
                    inc=(po_a if ci == DC - 1 else None))
            po_ln1 = po_a.v
            P.wait("dve", po_a, po_ln1)
            P.drain("dve", inc=dv_a)
            dv_ln1 = dv_a.v
            rec_xin_free = dv_ln1
            if debug:
                P.op("pool", lambda e, ts=ts: e.dma_start(
                    out=dbg["ln1"].rearrange("(c p) t -> p c t", p=128)[:, :, ts],
                    in_=ln1_v), waits=[(dv_a, dv_ln1)], inc=st_d, inc_n=16)

            for which in (("q",) if own else ()) + ("k",):
                wsel = wq_v if which == "q" else wk_v
                boff = 0 if which == "q" else DC
                for do in range(DC):
                    par = do % 2
                    p_qk = p_qks[par]
                    for ci in range(DC):
                        P.op("pe", lambda e, wsel=wsel, do=do, ci=ci, p_qk=p_qk: e.matmul(
                            p_qk, wsel[:, ci, do * 128:(do + 1) * 128],
                            ln1_v[:, ci, :],
                            start=(ci == 0), stop=(ci == DC - 1)),
                            waits=([(dv_a, dv_ln1), (dv_a, rec_pqk_free[par])] if ci == 0 else []),
                            inc=(pe_a if ci == DC - 1 else None))
                    pe_proj = pe_a.v
                    if which == "q":
                        dst = q_v[:, do, ts]
                    else:
                        dst = kst_v[:, do, :]
                    bslice = s_bias[:, boff + do:boff + do + 1]
                    dwaits = [(pe_a, pe_proj)]
                    if which == "k" and do == 0:
                        dwaits.append((sp_k, rec_kst_free))
                    P.op("dve", lambda e, dst=dst, bslice=bslice, ts=ts, p_qk=p_qk: e.scalar_tensor_tensor(
                        out=dst, in0=p_qk, scalar=bslice, in1=s_cos[:, ts],
                        op0=ALU.add, op1=ALU.mult), waits=dwaits)
                    P.op("dve", lambda e, bslice=bslice, ts=ts, p_qk=p_qk: e.scalar_tensor_tensor(
                        out=a_scrB[:], in0=p_qk, scalar=bslice, in1=s_sin[:, ts],
                        op0=ALU.add, op1=ALU.mult), inc=dv_a)
                    rec_pqk_free[par] = dv_a.v
                    P.drain("dve")
                    mask = [i ^ 1 for i in range(32)]
                    P.op("dve", lambda e, mask=mask: e.stream_shuffle(
                        out=a_scrA[:], in_=a_scrB[:], mask=mask),
                        waits=[(po_a, rec_scrA_free)], inc=dv_a)
                    dv_shf = dv_a.v
                    P.op("pool", lambda e, dst=dst: e.tensor_add(dst, dst, a_scrA[:]),
                         waits=[(dv_a, dv_shf)], inc=po_a)
                    rec_scrA_free = po_a.v
                po_rope = po_a.v
                if which == "k":
                    for hp in range(DC):
                        P.op("pool", lambda e, hp=hp, ts=ts: e.dma_start(
                            out=k_sp[hp, :, ts], in_=kst_v[:, hp, :]),
                            inc=sp_k, inc_n=16)
                    rec_kst_free = sp_k.v
                elif debug:
                    P.op("pool", lambda e, ts=ts: e.dma_start(
                        out=dbg["q"].rearrange("(c p) t -> p c t", p=128)[:, :, ts],
                        in_=q_v[:, :, ts]), inc=st_d, inc_n=16)

            for tt in range(ATC // 128):
                vb = tt % 2
                tsl = slice(tt * 128, (tt + 1) * 128)
                for nv in range(2):
                    for ci in range(DC):
                        P.op("pe", lambda e, nv=nv, ci=ci, tsl=tsl: e.matmul(
                            p_v[nv], ln1_v[:, ci, tsl],
                            wv_v[:, ci, nv * 512:(nv + 1) * 512],
                            start=(ci == 0), stop=False),
                            waits=([(dv_a, rec_pv_free)] if (ci == 0 and nv == 0) else []))
                    P.op("pe", lambda e, nv=nv: e.matmul(
                        p_v[nv], s_ones[0:1, 0:128], s_bvrow[0:1, nv * 512:(nv + 1) * 512],
                        start=False, stop=True), inc=pe_a)
                pe_vp = pe_a.v
                P.op("dve", lambda e, vb=vb: e.tensor_copy(
                    out=vst_v[vb][:, 0:8, 0:64],
                    in_=p_v[0].rearrange("p (j c) -> p j c", j=8)),
                    waits=[(pe_a, pe_vp), (sp_v, rec_vst_free[vb])])
                P.op("dve", lambda e, vb=vb: e.tensor_copy(
                    out=vst_v[vb][:, 8:16, 0:64],
                    in_=p_v[1].rearrange("p (j c) -> p j c", j=8)))
                P.op("dve", lambda e, vb=vb: e.memset(vst_v[vb][:, :, 64:65], 1.0),
                     inc=dv_a)
                dv_vst = dv_a.v
                rec_pv_free = dv_vst
                tok0 = tc * ATC + tt * 128
                for hp in range(DC):
                    P.op("pool", lambda e, hp=hp, tok0=tok0, vb=vb: e.dma_start(
                        out=v_sp[hp, tok0:tok0 + 128, :],
                        in_=a_vst[vb][:, hp * 130:(hp + 1) * 130]),
                        waits=([(dv_a, dv_vst)] if hp == 0 else []),
                        inc=sp_v, inc_n=16)
                rec_vst_free[vb] = sp_v.v
            rec_ln1_free = pe_a.v

        a_dv_end = dv_a.v
        a_pe_end = pe_a.v
        a_po_end = po_a.v
        a_spk_end = sp_k.v
        a_spv_end = sp_v.v

        _skipB = _LVL < 2
        p_sc = [PB[0][:, 0:1024], PB[1][:, 0:1024]]
        p_pv = [PB[2][:, 0:512], PB[2][:, 512:1024]]
        p_rbc = PB[3][0:64, 0:512]
        p_wo = PB[3][:, 512:1024]
        att_b = [att_v, att2_v]
        pt_slot = [s_ptb[0][:, 0:1024], s_ptb[0][:, 1024:2048], s_ptb[1][:, 0:1024]]

        rec_kh_free = [0, 0]
        rec_vh_free = [0, 0]
        rec_exp_done = [0, 0]
        rec_slot_pv = [0, 0, 0]
        rec_slot_exp = [0, 0, 0]
        rec_num_evac = [0, 0]
        rec_pwo = [0]
        rec_rbc = [0]
        dv_att_q = [0] * NQC
        spill_q = [0] * NQC

        def emit_wo_mm(ci, wo_qc, wo_do):
            wv = att_b[wo_qc % 2]
            w_waits = []
            if ci == 0:
                w_waits = [(dv_b, dv_att_q[wo_qc]), (dv_b, rec_pwo[0])]
            P.op("pe", lambda e, ci=ci, wo_do=wo_do, wv=wv: e.matmul(
                p_wo, wo_v[:, ci, wo_do * 128:(wo_do + 1) * 128], wv[:, ci, :],
                start=(ci == 0), stop=(ci == DC - 1)),
                waits=w_waits, inc=(pe_b if ci == DC - 1 else None))

        def emit_wo_evac(wo_qc, wo_do):
            pe_wo = pe_b.v
            e_waits = [(pe_b, pe_wo)]
            if wo_do == 0 and wo_qc >= 1:
                e_waits.append((sp_o, spill_q[wo_qc - 1]))
            P.op("dve", lambda e, wo_do=wo_do: e.tensor_scalar(
                out=x1_v[:, wo_do, :], in0=p_wo,
                scalar1=s_bias[:, 3 * DC + wo_do:3 * DC + wo_do + 1],
                scalar2=None, op0=ALU.add),
                waits=e_waits, inc=dv_b)
            rec_pwo[0] = dv_b.v

        def emit_spill(wo_qc):
            wqs = slice(wo_qc * 512, (wo_qc + 1) * 512)
            dv_o = dv_b.v
            P.op("pool", lambda e, wqs=wqs: e.dma_start(
                out=o_sp_v[:, :, wqs], in_=x1_v),
                waits=[(dv_b, dv_o)], inc=sp_o, inc_n=16)
            spill_q[wo_qc] = sp_o.v

        def emit_rbc(pc, j):
            P.op("pe", lambda e, j=j: e.matmul(
                p_rbc, s_ones[0:1, 0:64], s_recb[0:1, j * 512:(j + 1) * 512],
                start=True, stop=True),
                waits=[(dv_b, pc["dv_recb"] if j == 0 else 0),
                       (dv_b, rec_rbc[0])],
                inc=pe_b)
            pe_rbc = pe_b.v
            P.op("dve", lambda e, j=j: e.tensor_copy(
                out=s_rb[0:64, j * 512:(j + 1) * 512], in_=p_rbc),
                waits=[(pe_b, pe_rbc)], inc=dv_b)
            rec_rbc[0] = dv_b.v

        def emit_chain_tail(pc):
            emit_rbc(pc, 1)
            P.drain("dve")
            for j in range(2):
                P.op("dve", lambda e, j=j, pc=pc: e.tensor_mul(
                    pc["av"][64 * j:64 * (j + 1), pc["hp"], :],
                    s_num[0:64, j * 512:(j + 1) * 512],
                    s_rb[0:64, j * 512:(j + 1) * 512]))
            P.drain("dve", inc=dv_b)
            if pc["hp"] == DC - 1:
                dv_att_q[pc["qc"]] = dv_b.v
                if debug:
                    dqs = slice(pc["qc"] * 512, (pc["qc"] + 1) * 512)
                    P.op("pool", lambda e, dqs=dqs, dav=pc["av"]: e.dma_start(
                        out=dbg["attn"].rearrange("(c p) t -> p c t", p=128)[:, :, dqs],
                        in_=dav), waits=[(dv_b, dv_att_q[pc["qc"]])],
                        inc=st_d, inc_n=16)

        pend_wo = None
        pend_chain = None
        for qc in range(NQC if not _skipB else 0):
            qs = slice(qc * 512, (qc + 1) * 512)
            av = att_b[qc % 2]
            for hp in range(DC):
                kb = hp % 2
                P.op("sp", lambda e, kb=kb, hp=hp: e.dma_start(
                    out=s_kh[kb][:], in_=k_sp[hp]),
                    waits=[(sp_k, a_spk_end), (dv_a, a_dv_end), (po_a, a_po_end),
                           (pe_b, rec_kh_free[kb])],
                    inc=ld_kv, inc_n=16)
                P.op("sp", lambda e, kb=kb, hp=hp: e.dma_start(
                    out=vh_v[kb], in_=v_sp[hp].rearrange("(k p) c -> p k c", p=128)),
                    waits=[(sp_v, a_spv_end), (sp_k, a_spk_end), (pe_b, rec_vh_free[kb])],
                    inc=ld_kv, inc_n=16)
                ldkv_here = ld_kv.v

                def emit_scores(kt):
                    par = kt % 2
                    sl = kt % 3
                    ksl = slice(kt * 128, (kt + 1) * 128)
                    sc_waits = [(ld_kv, ldkv_here), (ac_b, rec_exp_done[par])]
                    P.op("pe", lambda e, ksl=ksl, par=par, kb=kb, hp=hp, qs=qs: e.matmul(
                        p_sc[par][:, 0:512], s_kh[kb][0:64, ksl], q_v[0:64, hp, qs],
                        start=True, stop=True, tile_position=(0, 0)),
                        waits=sc_waits)
                    P.op("pe", lambda e, ksl=ksl, par=par, kb=kb, hp=hp, qs=qs: e.matmul(
                        p_sc[par][:, 512:1024], s_kh[kb][64:128, ksl], q_v[64:128, hp, qs],
                        start=True, stop=True, tile_position=(64, 0)),
                        inc=pe_b)
                    pe_sc_here = pe_b.v
                    P.op("act", lambda e, par=par, sl=sl: e.activation(
                        out=pt_slot[sl], in_=p_sc[par][:, 0:1024], func=AF.Exp, scale=0.125),
                        waits=[(pe_b, pe_sc_here), (pe_b, rec_slot_pv[sl])],
                        inc=ac_b)
                    rec_exp_done[par] = ac_b.v
                    rec_slot_exp[sl] = ac_b.v

                def emit_pv(kt):
                    sl = kt % 3
                    pv0_waits = [(ac_b, rec_slot_exp[sl])]
                    pv1_waits = []
                    if kt == 0:
                        pv0_waits.append((dv_b, rec_num_evac[0]))
                        pv1_waits = [(dv_b, rec_num_evac[1])]
                    P.op("pe", lambda e, kt=kt, sl=sl, kb=kb: e.matmul(
                        p_pv[0][0:65, :], vh_v[kb][:, kt, 0:65],
                        pt_slot[sl][:, 0:512],
                        start=(kt == 0), stop=(kt == 31)),
                        waits=pv0_waits)
                    P.op("pe", lambda e, kt=kt, sl=sl, kb=kb: e.matmul(
                        p_pv[1][0:65, :], vh_v[kb][:, kt, 65:130],
                        pt_slot[sl][:, 512:1024],
                        start=(kt == 0), stop=(kt == 31)),
                        waits=pv1_waits, inc=pe_b)
                    rec_slot_pv[sl] = pe_b.v

                for kt in range(33):
                    if kt < 32:
                        emit_scores(kt)
                    if kt >= 1:
                        emit_pv(kt - 1)
                    if kt < 32:
                        if pend_chain is not None and kt == 9:
                            emit_rbc(pend_chain, 0)
                        if pend_chain is not None and kt == 11:
                            emit_chain_tail(pend_chain)
                            pend_chain = None
                        if pend_wo is not None and 16 <= kt < 24:
                            emit_wo_mm(kt - 16, pend_wo, hp)
                            if kt == 23:
                                emit_wo_evac(pend_wo, hp)
                                if hp == DC - 1:
                                    emit_spill(pend_wo)
                                    pend_wo = None
                rec_kh_free[kb] = pe_b.v
                rec_vh_free[kb] = pe_b.v
                pe_pv_end = pe_b.v
                P.op("dve", lambda e: e.tensor_copy(
                    out=s_num[0:65, 0:512], in_=p_pv[0][0:65, :]),
                    waits=[(pe_b, pe_pv_end)], inc=dv_b)
                rec_num_evac[0] = dv_b.v
                P.op("dve", lambda e: e.tensor_copy(
                    out=s_num[0:65, 512:1024], in_=p_pv[1][0:65, :]), inc=dv_b)
                rec_num_evac[1] = dv_b.v
                P.drain("dve")
                P.op("dve", lambda e: e.reciprocal(
                    out=s_rec[0:1, 0:1024], in_=s_num[64:65, 0:1024]))
                P.drain("dve")
                P.op("dve", lambda e: e.tensor_copy(out=s_recb[:], in_=s_rec[:]),
                     inc=dv_b)
                pend_chain = dict(dv_recb=dv_b.v, av=av, hp=hp, qc=qc)
            pend_wo = qc
            if qc == NQC - 1:
                emit_rbc(pend_chain, 0)
                emit_chain_tail(pend_chain)
                pend_chain = None
                for do in range(DC):
                    for ci in range(DC):
                        emit_wo_mm(ci, qc, do)
                    emit_wo_evac(qc, do)
                emit_spill(qc)

        b_spo_end = sp_o.v
        b_dv_end = dv_b.v
        b_pe_end = pe_b.v
        b_ac_end = ac_b.v

        _skipC = _LVL < 3
        p_sum1c = PB[0][0:1, 0:512]
        p_sum2c = PB[0][0:1, 512:1024]
        p_mb2 = PB[1][:, 0:512]
        p_rb2 = PB[1][:, 512:1024]
        p_lg = PB[2][:, 0:512]
        p_ct = PB[0][0:8, 512:640]
        p_cb = PB[1][:, 0:512]
        p_gl = [(PB[2][:, 0:512], PB[2][:, 512:1024]),
                (PB[1][:, 0:512], PB[1][:, 512:1024])]
        p_p2 = [PB[3][:, 0:512], PB[3][:, 512:1024]]

        rec_wf_free = [0, 0]
        rec_statc_free = 0
        rec_mbrb2_free = 0
        rec_gl_free = [0, 0]
        rec_pl_free = [0, 0]
        rec_sig_free = [0, 0]
        rec_lc_free = [0, 0]
        rec_ap_free = [0, 0]
        rec_ta_free = [0, 0]
        rec_p2_free = [0, 0]
        rec_cb_free = 0
        rec_cbs_ac = 0
        rec_x1_free = 0

        wf_loads = [(tc4, e_) for tc4 in range(NQC) for e_ in range(NE)]

        def emit_wf_load(idx):
            tc4, e_ = wf_loads[idx]
            wb = idx % 2
            waits = [(pe_c, rec_wf_free[wb])]
            if idx == 0:
                waits.append((pe_a, a_pe_end))
            if idx == 1:
                waits.append((pe_b, b_pe_end))
                waits.append((dv_b, b_dv_end))
            for wi, wt in ((0, w1gT), (1, w1lT), (2, w2T)):
                P.op("sp", lambda e, wb=wb, wi=wi, wt=wt, e_=e_: e.dma_start(
                    out=wf_v[wb][:, wi, :, :],
                    in_=wt[e_].rearrange("(c p) n -> p c n", p=128)),
                    waits=(waits if wi == 0 else []), inc=ld_wf, inc_n=16)
            return ld_wf.v

        if not _skipC:
            P.op("sp", lambda e: e.dma_start(out=s_sel[0:NE, :], in_=sel_in),
                 waits=[(pe_b, b_pe_end), (ac_b, b_ac_end)], inc=ld_x1, inc_n=16)
            P.op("sp", lambda e: e.dma_start(out=s_b2[0:NE, :], in_=b2_in),
                 inc=ld_x1, inc_n=16)
            sel_ld = ld_x1.v
            wf_ready = [0] * len(wf_loads)
            wf_ready[0] = emit_wf_load(0)
            wf_ready[1] = emit_wf_load(1)
            next_wf = 2

        for tc4 in range(_CT4 if not _skipC else 0):
            qs = slice(tc4 * 512, (tc4 + 1) * 512)
            ct_s = slice(tc4 * 512, (tc4 + 1) * 512)
            P.op("sp", lambda e, qs=qs: e.dma_start(out=x1_v, in_=o_sp_v[:, :, qs]),
                 waits=[(sp_o, b_spo_end), (dv_c, rec_x1_free)],
                 inc=ld_x1, inc_n=16)
            P.op("pool", lambda e, qs=qs: e.dma_start(out=acc_v, in_=x_fm_v[:, :, qs]),
                 waits=[(pe_b, b_pe_end), (dv_c, rec_x1_free)], inc=ld_x1, inc_n=16)
            ldx1_here = ld_x1.v
            P.op("dve", lambda e: e.tensor_add(s_x1[:], s_x1[:], c_acc[:]),
                 waits=[(ld_x1, ldx1_here), (dv_b, b_dv_end)])
            P.drain("dve", inc=dv_c)
            dv_x1c = dv_c.v
            if debug:
                P.op("pool", lambda e, qs=qs: e.dma_start(
                    out=dbg["x1"].rearrange("(c p) t -> p c t", p=128)[:, :, qs],
                    in_=x1_v), waits=[(dv_c, dv_x1c)], inc=st_d, inc_n=16)
            P.op("dve", lambda e: e.tensor_copy(out=c_xb[:], in_=s_x1[:]))
            P.drain("dve", inc=dv_c)
            dv_xbc = dv_c.v
            P.op("act", lambda e: e.activation(out=c_sq[:], in_=c_xb[:], func=AF.Square),
                 waits=[(dv_c, dv_xbc)], inc=ac_c)
            ac_sqc = ac_c.v
            for ci in range(DC):
                P.op("pe", lambda e, ci=ci: e.matmul(
                    p_sum1c, s_ones[:, 0:1], cxb_v[:, ci, :],
                    start=(ci == 0), stop=(ci == DC - 1)),
                    waits=([(dv_c, dv_xbc), (dv_c, rec_statc_free), (pe_b, b_pe_end)] if ci == 0 else []),
                    inc=(pe_c if ci == DC - 1 else None))
            for ci in range(DC):
                P.op("pe", lambda e, ci=ci: e.matmul(
                    p_sum2c, s_ones[:, 0:1], csq_v[:, ci, :],
                    start=(ci == 0), stop=(ci == DC - 1)),
                    waits=([(ac_c, ac_sqc)] if ci == 0 else []),
                    inc=(pe_c if ci == DC - 1 else None))
            pe_statc = pe_c.v
            P.op("dve", lambda e: e.tensor_scalar_mul(st_m, p_sum1c, 1.0 / D),
                 waits=[(pe_c, pe_statc)])
            P.drain("dve")
            P.op("dve", lambda e: e.tensor_mul(st_m2, st_m, st_m))
            P.drain("dve")
            P.op("dve", lambda e: e.scalar_tensor_tensor(
                out=st_m2, in0=p_sum2c, scalar=1.0 / D, in1=st_m2,
                op0=ALU.mult, op1=ALU.subtract))
            P.drain("dve", inc=dv_c)
            dv_varc = dv_c.v
            P.op("act", lambda e: e.activation(out=st_ln, in_=st_m2, func=AF.Ln, bias=EPS),
                 waits=[(dv_c, dv_varc)])
            P.op("act", lambda e: e.drain())
            P.op("act", lambda e: e.activation(out=st_rstd, in_=st_ln, func=AF.Exp, scale=-0.5),
                 inc=ac_c)
            ac_rstdc = ac_c.v
            P.op("dve", lambda e: e.tensor_copy(out=s_statb[0:1, 0:512], in_=st_m),
                 waits=[(ac_c, ac_rstdc)])
            P.op("dve", lambda e: e.tensor_copy(out=s_statb[0:1, 512:1024], in_=st_rstd))
            P.drain("dve", inc=dv_c)
            dv_statbc = dv_c.v
            P.op("pe", lambda e: e.matmul(p_mb2, s_ones[0:1, :], s_statb[0:1, 0:512],
                                          start=True, stop=True),
                 waits=[(dv_c, dv_statbc), (dv_c, rec_mbrb2_free), (ac_c, rec_cbs_ac)])
            P.op("pe", lambda e: e.matmul(p_rb2, s_ones[0:1, :], s_statb[0:1, 512:1024],
                                          start=True, stop=True), inc=pe_c)
            pe_bc2 = pe_c.v
            for ci in range(DC):
                P.op("dve", lambda e, ci=ci: e.tensor_tensor(
                    out=acc_v[:, ci, :], in0=x1_v[:, ci, :], in1=p_mb2, op=ALU.subtract),
                    waits=([(pe_c, pe_bc2)] if ci == 0 else []))
            P.drain("dve")
            for ci in range(DC):
                P.op("dve", lambda e, ci=ci: e.tensor_mul(
                    xc_v[:, ci, :], acc_v[:, ci, :], p_rb2),
                    waits=([(pe_b, b_pe_end)] if ci == 0 else []))
            P.drain("dve", inc=dv_c)
            dv_xc = dv_c.v
            P.op("act", lambda e: e.activation(out=c_sq[:], in_=c_xc[:], func=AF.Square),
                 waits=[(dv_c, dv_xc)], inc=ac_c)
            ac_sq2 = ac_c.v
            for ci in range(DC):
                P.op("pe", lambda e, ci=ci: e.matmul(
                    p_sum1c, s_ones[:, 0:1], csq_v[:, ci, :],
                    start=(ci == 0), stop=(ci == DC - 1)),
                    waits=([(ac_c, ac_sq2), (dv_c, dv_varc)] if ci == 0 else []),
                    inc=(pe_c if ci == DC - 1 else None))
            pe_rms = pe_c.v
            P.op("dve", lambda e: e.tensor_scalar_mul(st_m, p_sum1c, 1.0 / D),
                 waits=[(pe_c, pe_rms)])
            P.drain("dve", inc=dv_c)
            dv_rmsv = dv_c.v
            P.op("act", lambda e: e.activation(out=st_ln, in_=st_m, func=AF.Ln, bias=EPS),
                 waits=[(dv_c, dv_rmsv)])
            P.op("act", lambda e: e.drain())
            P.op("act", lambda e: e.activation(out=st_rstd, in_=st_ln, func=AF.Exp, scale=-0.5),
                 inc=ac_c)
            ac_rms = ac_c.v
            P.op("dve", lambda e: e.tensor_copy(out=s_statb[0:1, 512:1024], in_=st_rstd),
                 waits=[(ac_c, ac_rms)])
            P.drain("dve", inc=dv_c)
            dv_rmsb = dv_c.v
            P.op("pe", lambda e: e.matmul(p_rb2, s_ones[0:1, :], s_statb[0:1, 512:1024],
                                          start=True, stop=True),
                 waits=[(dv_c, dv_rmsb), (dv_c, dv_xc)], inc=pe_c)
            pe_rmsb = pe_c.v
            for ci in range(DC):
                P.op("dve", lambda e, ci=ci: e.scalar_tensor_tensor(
                    out=nrm_v[:, ci, :], in0=xc_v[:, ci, :], scalar=s_gamma[:, ci:ci + 1],
                    in1=p_rb2, op0=ALU.mult, op1=ALU.mult),
                    waits=([(pe_c, pe_rmsb)] if ci == 0 else []))
            P.drain("dve", inc=dv_c)
            dv_nrm = dv_c.v
            rec_mbrb2_free = dv_nrm
            if debug:
                P.op("pool", lambda e, qs=qs: e.dma_start(
                    out=dbg["normed"].rearrange("(c p) t -> p c t", p=128)[:, :, qs],
                    in_=nrm_v), waits=[(dv_c, dv_nrm)], inc=st_d, inc_n=16)

            pe_lg = [0] * 4
            for tt in range(4):
                tsl = slice(tt * 128, (tt + 1) * 128)
                for ci in range(DC):
                    P.op("pe", lambda e, ci=ci, tsl=tsl, tt=tt: e.matmul(
                        p_lg[:, tt * 64:tt * 64 + 8], nrm_v[:, ci, tsl], gw_v[:, ci, :],
                        start=(ci == 0), stop=False),
                        waits=([(dv_c, dv_nrm), (dv_c, rec_gl_free[0])] if (ci == 0 and tt == 0) else []))
                P.op("pe", lambda e, tt=tt: e.matmul(
                    p_lg[:, tt * 64:tt * 64 + 8], s_ones[0:1, 0:128],
                    s_gbrow[0:1, :], start=False, stop=True),
                    inc=pe_c)
                pe_lg[tt] = pe_c.v

            def gstep(build4, waits=(), inc=None):
                for tt in range(4):
                    P.op("dve", lambda e, tt=tt: build4(e, tt),
                         waits=(waits if tt == 0 else []),
                         inc=(inc if tt == 3 else None))
                P.drain("dve")
            g = gt_v
            gb_ = gtb_v
            gs = gsc_v
            gstep(lambda e, tt: e.tensor_copy(out=gb_[:, tt, 0:8],
                                              in_=p_lg[:, tt * 64:tt * 64 + 8]),
                  waits=[(pe_c, pe_lg[3])])
            gstep(lambda e, tt: e.tensor_copy(out=g[:, tt, 0:8], in_=gb_[:, tt, 0:8]))
            gstep(lambda e, tt: e.max(out=g[:, tt, 8:16], in_=g[:, tt, 0:8]))
            gstep(lambda e, tt: e.tensor_sub(gs[:, tt, 0:1], g[:, tt, 9:10], g[:, tt, 8:9]))
            for tt in range(4):
                P.op("dve", lambda e, tt=tt: e.tensor_scalar(
                    out=g[:, tt, 16:24], in0=g[:, tt, 0:8],
                    scalar1=g[:, tt, 8:9], scalar2=None, op0=ALU.is_equal))
            gstep(lambda e, tt: e.tensor_scalar(
                out=g[:, tt, 24:32], in0=g[:, tt, 0:8],
                scalar1=g[:, tt, 9:10], scalar2=None, op0=ALU.is_equal), inc=dv_c)
            dv_eq = dv_c.v
            for tt in range(4):
                P.op("act", lambda e, tt=tt: e.activation(
                    out=gs[:, tt, 4:5], in_=gs[:, tt, 0:1], func=AF.Sigmoid),
                    waits=([(dv_c, dv_eq)] if tt == 0 else []))
            for tt in range(4):
                P.op("act", lambda e, tt=tt: e.activation(
                    out=gs[:, tt, 3:4], in_=gs[:, tt, 0:1], func=AF.Sigmoid, scale=-1.0),
                    inc=(ac_c if tt == 3 else None))
            ac_z = ac_c.v
            P.wait("dve", ac_c, ac_z)
            gstep(lambda e, tt: e.tensor_tensor_scan(
                out=g[:, tt, 32:40], data0=g[:, tt, 16:24], data1=s_zero8[:],
                initial=0.0, op0=ALU.add, op1=ALU.add))
            gstep(lambda e, tt: e.tensor_scalar(
                out=g[:, tt, 40:48], in0=g[:, tt, 32:40],
                scalar1=1.0, scalar2=None, op0=ALU.is_equal))
            gstep(lambda e, tt: e.tensor_mul(g[:, tt, 48:56], g[:, tt, 40:48], g[:, tt, 16:24]))
            gstep(lambda e, tt: e.tensor_scalar(
                out=g[:, tt, 56:64], in0=g[:, tt, 48:56],
                scalar1=-1.0, scalar2=1.0, op0=ALU.mult, op1=ALU.add))
            gstep(lambda e, tt: e.tensor_mul(g[:, tt, 64:72], g[:, tt, 24:32], g[:, tt, 56:64]))
            gstep(lambda e, tt: e.tensor_tensor_scan(
                out=g[:, tt, 72:80], data0=g[:, tt, 64:72], data1=s_zero8[:],
                initial=0.0, op0=ALU.add, op1=ALU.add))
            gstep(lambda e, tt: e.tensor_scalar(
                out=g[:, tt, 80:88], in0=g[:, tt, 72:80],
                scalar1=1.0, scalar2=None, op0=ALU.is_equal))
            gstep(lambda e, tt: e.tensor_mul(g[:, tt, 88:96], g[:, tt, 80:88], g[:, tt, 64:72]))
            gstep(lambda e, tt: e.tensor_scalar(
                out=g[:, tt, 96:104], in0=g[:, tt, 48:56],
                scalar1=gs[:, tt, 3:4], scalar2=None, op0=ALU.mult))
            gstep(lambda e, tt: e.scalar_tensor_tensor(
                out=g[:, tt, 104:112], in0=g[:, tt, 88:96],
                scalar=gs[:, tt, 4:5], in1=g[:, tt, 96:104],
                op0=ALU.mult, op1=ALU.add), inc=dv_c)
            dv_comb = dv_c.v
            for tt in range(4):
                P.op("pe", lambda e, tt=tt: e.transpose(p_ct, g[:, tt, 104:112], s_id[:]),
                     waits=([(dv_c, dv_comb), (dv_c, rec_statc_free)] if tt == 0
                            else [(dv_c, dv_c.v)]),
                     inc=pe_c)
                pe_ct = pe_c.v
                P.op("dve", lambda e, tc4=tc4, tt=tt: e.tensor_copy(
                    out=c_ct[0:8, tc4 * 512 + tt * 128: tc4 * 512 + (tt + 1) * 128],
                    in_=p_ct), waits=[(pe_c, pe_ct)])
                P.drain("dve", inc=dv_c)
            dv_ct = dv_c.v
            rec_statc_free = dv_ct
            rec_gl_free[0] = max(rec_gl_free[0], dv_ct)
            if debug and tc4 == NQC - 1:
                P.op("pool", lambda e: e.dma_start(out=dbg["ct"], in_=c_ct[0:NE, :]),
                     waits=[(dv_c, dv_ct)], inc=st_d, inc_n=16)

            for e_ in range(_CNE):
                idx = tc4 * NE + e_
                wb = idx % 2
                wf_here = wf_ready[idx]
                P.op("pe", lambda e, e_=e_, ct_s=ct_s: e.matmul(
                    p_cb, s_sel[0:NE, e_ * 128:(e_ + 1) * 128], c_ct[0:NE, ct_s],
                    start=True, stop=True),
                    waits=[(dv_c, dv_ct), (ld_x1, sel_ld),
                           (dv_c, rec_mbrb2_free), (dv_c, rec_cb_free),
                           (ac_c, ac_c.v)],
                    inc=pe_c)
                pe_cb = pe_c.v
                P.op("act", lambda e: e.activation(out=c_cbs[:], in_=p_cb, func=AF.Copy),
                     waits=[(pe_c, pe_cb), (dv_c, rec_cb_free)], inc=ac_c)
                ac_cbs = ac_c.v
                rec_cbs_ac = ac_cbs
                dv_t1as = [0] * DC
                dv_lc2s = [0] * DC

                def emit_stage1(it):
                    par = it % 2
                    p_g, p_l = p_gl[par]
                    g_waits = [(ld_wf, wf_here), (dv_c, rec_gl_free[par])]
                    if par == 1 and it == 1:
                        g_waits.append((ac_c, ac_cbs))
                    for ci in range(DC):
                        P.op("pe", lambda e, it=it, ci=ci, p_g=p_g, wb=wb: e.matmul(
                            p_g, wf_v[wb][:, 0, ci, it * 128:(it + 1) * 128], nrm_v[:, ci, :],
                            start=(ci == 0), stop=(ci == DC - 1)),
                            waits=(g_waits if ci == 0 else []),
                            inc=(pe_c if ci == DC - 1 else None))
                    pe_g_here = pe_c.v
                    for ci in range(DC):
                        P.op("pe", lambda e, it=it, ci=ci, p_l=p_l, wb=wb: e.matmul(
                            p_l, wf_v[wb][:, 1, ci, it * 128:(it + 1) * 128], nrm_v[:, ci, :],
                            start=(ci == 0), stop=(ci == DC - 1)),
                            waits=([(ac_c, rec_pl_free[par])] if ci == 0 else []),
                            inc=(pe_c if ci == DC - 1 else None))
                    pe_l_here = pe_c.v
                    bofs = it * NE + e_
                    TA, SG, LC, AP_ = c_t1as[par], c_sigs[par], c_lcs[par], c_aps[par]
                    P.op("act", lambda e, bofs=bofs, p_g=p_g, SG=SG: e.activation(
                        out=SG[:], in_=p_g, func=AF.Sigmoid,
                        bias=s_b1gs[:, bofs:bofs + 1], scale=ALPHA),
                        waits=[(pe_c, pe_g_here), (dv_c, rec_sig_free[par])], inc=ac_c)
                    ac_sig = ac_c.v
                    P.op("act", lambda e, bofs=bofs, p_l=p_l, LC=LC: e.activation(
                        out=LC[:], in_=p_l, func=AF.Identity,
                        bias=s_b1l[:, bofs:bofs + 1]),
                        waits=[(pe_c, pe_l_here), (po_c, rec_lc_free[par])], inc=ac_c)
                    ac_lraw = ac_c.v
                    rec_pl_free[par] = ac_lraw
                    P.op("dve", lambda e, bofs=bofs, p_g=p_g, TA=TA, SG=SG: e.scalar_tensor_tensor(
                        out=TA[:], in0=p_g, scalar=s_b1g[:, bofs:bofs + 1],
                        in1=SG[:], op0=ALU.add, op1=ALU.mult),
                        waits=[(ac_c, ac_sig)], inc=dv_c)
                    dv_t1a = dv_c.v
                    dv_t1as[it] = dv_t1a
                    rec_gl_free[par] = dv_t1a
                    rec_sig_free[par] = dv_t1a
                    P.op("dve", lambda e, LC=LC: e.tensor_scalar(
                        out=LC[:], in0=LC[:], scalar1=-6.0, scalar2=8.0,
                        op0=ALU.max, op1=ALU.min),
                        waits=[(ac_c, ac_lraw)], inc=dv_c)
                    dv_lc2s[it] = dv_c.v

                def emit_stage2(it):
                    par = it % 2
                    TA, LC, AP_ = c_t1as[par], c_lcs[par], c_aps[par]
                    P.op("dve", lambda e, TA=TA, AP_=AP_: e.scalar_tensor_tensor(
                        out=AP_[:], in0=TA[:], scalar=CCAP, in1=c_cbs[:],
                        op0=ALU.min, op1=ALU.mult),
                        waits=[(ac_c, ac_cbs), (po_c, rec_ap_free[par])], inc=dv_c)
                    dv_z = dv_c.v
                    rec_ta_free[par] = dv_z
                    P.op("pool", lambda e, it=it, LC=LC, AP_=AP_: e.tensor_mul(
                        act_v[:, it, :], AP_[:], LC[:]),
                        waits=[(dv_c, dv_z), (dv_c, dv_lc2s[it])], inc=po_c)
                    rec_lc_free[par] = po_c.v
                    rec_ap_free[par] = po_c.v

                for it in range(DC + 1):
                    if it < DC:
                        emit_stage1(it)
                    if it >= 1:
                        P.drain("dve")
                        emit_stage2(it - 1)
                P.op("pool", lambda e: e.drain(), inc=po_c)
                po_act = po_c.v
                dv_act = dv_c.v
                rec_cb_free = dv_c.v
                for ht in range(DC):
                    hb = ht % 2
                    for ic in range(DC):
                        P.op("pe", lambda e, wb=wb, ht=ht, ic=ic, hb=hb: e.matmul(
                            p_p2[hb], wf_v[wb][:, 2, ic, ht * 128:(ht + 1) * 128],
                            act_v[:, ic, :],
                            start=(ic == 0), stop=(ic == DC - 1 and (e_ != NE - 1 or _CNE < NE))),
                            waits=([(po_c, po_act), (dv_c, rec_p2_free[hb])] if ic == 0 else []),
                            inc=(pe_c if (ic == DC - 1 and e_ != NE - 1) else None))
                    if e_ == NE - 1:
                        P.op("pe", lambda e, ht=ht, hb=hb, ct_s=ct_s: e.matmul(
                            p_p2[hb], s_b2[0:NE, ht * 128:(ht + 1) * 128], c_ct[0:NE, ct_s],
                            start=False, stop=True),
                            waits=[(ld_x1, sel_ld)] if ht == 0 else [], inc=pe_c)
                    pe_p2 = pe_c.v
                    if e_ == 0:
                        P.op("dve", lambda e, ht=ht, hb=hb: e.tensor_copy(
                            out=acc_v[:, ht, :], in_=p_p2[hb]),
                            waits=[(pe_c, pe_p2)], inc=dv_c)
                    else:
                        P.op("dve", lambda e, ht=ht, hb=hb: e.tensor_add(
                            acc_v[:, ht, :], acc_v[:, ht, :], p_p2[hb]),
                            waits=[(pe_c, pe_p2)], inc=dv_c)
                    rec_p2_free[hb] = dv_c.v
                P.drain("dve")
                rec_wf_free[wb] = pe_c.v
                if not _skipC and next_wf < len(wf_loads) and next_wf == idx + 2:
                    wf_ready[next_wf] = emit_wf_load(next_wf)
                    next_wf += 1
            pe_mlp2_end = pe_c.v
            if debug and tc4 == NQC - 1:
                P.op("pool", lambda e, qs=qs: e.dma_start(
                    out=dbg["acc"].rearrange("(c p) t -> p c t", p=128)[:, :, qs],
                    in_=acc_v), waits=[(dv_c, dv_c.v), (pe_c, pe_mlp2_end)], inc=st_d, inc_n=16)
            P.op("dve", lambda e: e.tensor_copy(out=c_act[:], in_=c_acc[:]),
                 waits=[(pe_c, pe_mlp2_end)])
            P.drain("dve")
            P.op("dve", lambda e: e.tensor_add(c_xb[:], c_xc[:], c_act[:]))
            P.drain("dve")
            P.op("dve", lambda e: e.tensor_add(s_x1[:], s_x1[:], c_xb[:]))
            P.drain("dve", inc=dv_c)
            dv_out = dv_c.v
            rec_x1_free = dv_out
            P.op("sp", lambda e, qs=qs: e.dma_start(out=out_fm_v[:, :, qs], in_=x1_v),
                 waits=[(dv_c, dv_out)], inc=st_o, inc_n=16)

        P.wait("sp", st_o, st_o.v)
        P.wait("sp", sp_k, sp_k.v)
        P.wait("sp", sp_v, sp_v.v)
        P.wait("sp", sp_o, sp_o.v)
        P.wait("sp", ld_c, ld_c.v)
        P.wait("sp", ld_w, ld_w.v)
        P.wait("sp", ld_wf, ld_wf.v)
        if debug:
            P.wait("sp", st_d, st_d.v)

        for s in P.sems.values():
            s.h = es.enter_context(nc.semaphore(s.name))

        blk = es.enter_context(nc.Block())

        @blk.tensor
        def _(t):
            for fn in P.ops["pe"]:
                fn(t)

        @blk.scalar
        def _(sc):
            for fn in P.ops["act"]:
                fn(sc)

        @blk.vector
        def _(v):
            for fn in P.ops["dve"]:
                fn(v)

        @blk.gpsimd
        def _(g):
            for fn in P.ops["pool"]:
                fn(g)

        @blk.sync
        def _(sy):
            for fn in P.ops["sp"]:
                fn(sy)

    return nc


def _sel_const():
    sel = np.zeros((NE, NE * 128), dtype=BF)
    for e in range(NE):
        sel[e, e * 128:(e + 1) * 128] = 1
    return sel


def make_inputs(inputs):
    x = np.asarray(inputs["L0_attn_x"])
    wq = np.asarray(inputs["L0_attn_Wq"]); bq = np.asarray(inputs["L0_attn_bq"])
    wk = np.asarray(inputs["L0_attn_Wk"]); bk = np.asarray(inputs["L0_attn_bk"])
    wv = np.asarray(inputs["L0_attn_Wv"]); bv = np.asarray(inputs["L0_attn_bv"])
    wo = np.asarray(inputs["L0_attn_Wo"]); bo = np.asarray(inputs["L0_attn_bo"])
    gamma = np.asarray(inputs["L0_ffn_norm_weight"])
    gw = np.asarray(inputs["L0_ffn_gate_weight"])
    gb = np.asarray(inputs["L0_ffn_gate_bias"])
    w1 = np.asarray(inputs["L0_ffn_mlp1_weight"])
    b1 = np.asarray(inputs["L0_ffn_mlp1_bias"])
    w2 = np.asarray(inputs["L0_ffn_mlp2_weight"])
    b2 = np.asarray(inputs["L0_ffn_mlp2_bias"])

    def chunked(b):
        return np.ascontiguousarray(b.reshape(DC, 128).T)

    def bias_pe(b):
        return np.ascontiguousarray(
            b.T.reshape(DC, 128, NE).transpose(1, 0, 2).reshape(128, DC * NE))

    shared = dict(
        wqT=np.ascontiguousarray(wq.T), wkT=np.ascontiguousarray(wk.T),
        wvT=np.ascontiguousarray(wv.T), woT=np.ascontiguousarray(wo.T),
        b_qkvo=np.ascontiguousarray(np.concatenate(
            [chunked(bq), chunked(bk), chunked(bv), chunked(bo)], axis=1)),
        bv_row=np.ascontiguousarray(bv.reshape(1, D)),
        gamma_in=chunked(gamma),
        wg_in=np.ascontiguousarray(gw.T),
        gb_row=np.ascontiguousarray(gb.reshape(1, NE)),
        w1gT=np.ascontiguousarray(w1[:, 0::2, :].transpose(0, 2, 1)),
        w1lT=np.ascontiguousarray(w1[:, 1::2, :].transpose(0, 2, 1)),
        w2T=np.ascontiguousarray(w2.transpose(0, 2, 1)),
        b1g_in=bias_pe(b1[:, 0::2]),
        b1l_in=bias_pe(b1[:, 1::2]),
        b2_in=np.ascontiguousarray(b2),
        ones_bf_in=np.ones((128, 128), dtype=BF),
        ident_in=np.eye(128, dtype=np.float32),
        sel_in=_sel_const(),
    )
    in_maps = []
    for c in range(8):
        b, h = c // 2, c % 2
        perm = np.concatenate([np.arange(h * T, S), np.arange(0, h * T)])
        cos_t, sin_t = _rope_tables(perm)
        xc = np.ascontiguousarray(x[b].T[:, perm])
        m = dict(shared)
        m.update(x_fm=xc, cos_in=cos_t, sin_in=sin_t)
        in_maps.append(m)
    return in_maps


def kernel(**inputs):
    in_maps = make_inputs(inputs)
    nc = build_program(debug=False)
    res = run_bass_kernel_spmd(nc, in_maps, core_ids=list(range(8)))
    out = np.empty((4, S, D), dtype=np.float32)
    for c in range(8):
        b, h = c // 2, c % 2
        out[b, h * T:(h + 1) * T, :] = np.asarray(res.results[c]["out_fm"]).T
    return out


# version 11
# speedup vs baseline: 1.1537x; 1.0475x over previous
import contextlib
import numpy as np
import ml_dtypes
import concourse.bass as bass
import concourse.mybir as mybir
from concourse.bass_utils import run_bass_kernel_spmd

F32 = mybir.dt.float32
BF16 = mybir.dt.bfloat16
AF = mybir.ActivationFunctionType
ALU = mybir.AluOpType
BF = ml_dtypes.bfloat16

D = 1024
S = 4096
T = 2048
HD = 64
NE = 8
DC = 8
ATC = 512
NATC = S // ATC
NQC = 4
EPS = 1e-5
ALPHA = 1.702
CCAP = float(np.float32(7.0) * np.float32(
    1.0 / (1.0 + np.exp(-np.float64(np.float32(ALPHA) * np.float32(7.0))))))

ENGS = ("pe", "act", "dve", "pool", "sp")


class Sem:
    def __init__(self, name):
        self.name = name
        self.h = None
        self.v = 0


class Plan:
    def __init__(self):
        self.ops = {e: [] for e in ENGS}
        self.sems = {}

    def sem(self, name):
        if name not in self.sems:
            self.sems[name] = Sem(name)
        return self.sems[name]

    def op(self, eng, build, waits=(), inc=None, inc_n=1):
        if inc is not None:
            inc.v += inc_n

        def fn(engine, waits=tuple(waits), build=build, inc=inc, inc_n=inc_n):
            for s, v in waits:
                if v > 0:
                    engine.wait_ge(s.h, v)
            ins = build(engine)
            if inc is not None:
                ins.then_inc(inc.h, inc_n)
        self.ops[eng].append(fn)

    def wait(self, eng, sem, val):
        def fn(engine, sem=sem, val=val):
            engine.wait_ge(sem.h, val)
        self.ops[eng].append(fn)

    def drain(self, eng, inc=None, inc_n=1):
        self.op(eng, lambda e: e.drain(), inc=inc, inc_n=inc_n)


def _rope_tables(perm):
    dim = np.float32(HD)
    inv_freq = (np.float32(1.0) / (np.float32(10000.0) ** (
        np.arange(0, HD, 2, dtype=np.float32) / dim))).astype(np.float32)
    pos = np.arange(S, dtype=np.float32)
    freqs = (pos[:, None] * inv_freq[None, :]).astype(np.float32)
    freqs = np.repeat(freqs, 2, axis=-1)
    cos_t = np.cos(freqs.astype(np.float64)).astype(np.float32)
    sin_t = np.sin(freqs.astype(np.float64)).astype(np.float32)
    sign = np.ones(HD, dtype=np.float32)
    sign[0::2] = -1.0
    sin_s = sin_t * sign[None, :]
    cos2 = np.concatenate([cos_t[perm].T, cos_t[perm].T], axis=0)
    sin2 = np.concatenate([sin_s[perm].T, sin_s[perm].T], axis=0)
    sinp = sin2.copy()
    sinp[0::2], sinp[1::2] = sin2[1::2], sin2[0::2]
    return np.ascontiguousarray(cos2), np.ascontiguousarray(sinp)


class Arena:

    def __init__(self, nc, es, name, nbytes):
        self.nc = nc
        self.name = name
        self.nbytes = nbytes
        self.backing = es.enter_context(
            nc.sbuf_tensor(name, [128, nbytes], mybir.dt.uint8))
        self.base = nc.lookup_mloc(self.backing).addr

    def tensor(self, name, shape, dtype, off):
        per_part = int(np.prod(shape[1:])) * mybir.dt.size(dtype)
        assert off + per_part <= self.nbytes, (self.name, name, off, per_part)
        assert (self.base + off) % 32 == 0, (name, self.base, off)
        h = self.nc.virtual_sbuf_tensor(name, shape, dtype)
        self.nc._tensor(h.name, shape, dtype, type="SB")
        mloc = self.nc.lookup_mloc(h)
        mloc.allocated = True
        mloc.addr = self.base + off
        return h


def build_program(debug=False, stop_after=None, c_lim=None):
    _LVL = {"pre": 0, "A": 1, "B": 2, "C": 3, None: 3}[stop_after]
    _CT4, _CNE = c_lim if c_lim is not None else (NQC, NE)
    nc = bass.Bass("TRN2", target_bir_lowering=False, debug=False)

    x_fm = nc.dram_tensor("x_fm", [D, S], F32, kind="ExternalInput").ap()
    cos_in = nc.dram_tensor("cos_in", [128, S], F32, kind="ExternalInput").ap()
    sin_in = nc.dram_tensor("sin_in", [128, S], F32, kind="ExternalInput").ap()
    wqT = nc.dram_tensor("wqT", [D, D], F32, kind="ExternalInput").ap()
    wkT = nc.dram_tensor("wkT", [D, D], F32, kind="ExternalInput").ap()
    wvT = nc.dram_tensor("wvT", [D, D], F32, kind="ExternalInput").ap()
    woT = nc.dram_tensor("woT", [D, D], F32, kind="ExternalInput").ap()
    b_qkvo = nc.dram_tensor("b_qkvo", [128, 4 * DC], F32, kind="ExternalInput").ap()
    bv_row = nc.dram_tensor("bv_row", [1, D], F32, kind="ExternalInput").ap()
    gamma_in = nc.dram_tensor("gamma_in", [128, DC], F32, kind="ExternalInput").ap()
    wg_in = nc.dram_tensor("wg_in", [D, NE], BF16, kind="ExternalInput").ap()
    gb_row = nc.dram_tensor("gb_row", [1, NE], BF16, kind="ExternalInput").ap()
    w1gT = nc.dram_tensor("w1gT", [NE, D, D], BF16, kind="ExternalInput").ap()
    w1lT = nc.dram_tensor("w1lT", [NE, D, D], BF16, kind="ExternalInput").ap()
    w2T = nc.dram_tensor("w2T", [NE, D, D], BF16, kind="ExternalInput").ap()
    b1g_in = nc.dram_tensor("b1g_in", [128, DC * NE], BF16, kind="ExternalInput").ap()
    b1l_in = nc.dram_tensor("b1l_in", [128, DC * NE], BF16, kind="ExternalInput").ap()
    b2_in = nc.dram_tensor("b2_in", [NE, D], BF16, kind="ExternalInput").ap()
    ones_bf_in = nc.dram_tensor("ones_bf_in", [128, 128], BF16, kind="ExternalInput").ap()
    ident_in = nc.dram_tensor("ident_in", [128, 128], F32, kind="ExternalInput").ap()
    sel_in = nc.dram_tensor("sel_in", [NE, NE * 128], BF16, kind="ExternalInput").ap()

    out_fm = nc.dram_tensor("out_fm", [D, T], F32, kind="ExternalOutput").ap()

    k_sp = nc.dram_tensor("k_sp", [DC, 128, S], BF16, kind="Internal").ap()
    v_sp = nc.dram_tensor("v_sp", [DC, S, 130], BF16, kind="Internal").ap()
    o_sp = nc.dram_tensor("o_sp", [D, T], F32, kind="Internal").ap()

    dbg = {}
    if debug:
        dbg["ln1"] = nc.dram_tensor("d_ln1", [D, S], BF16, kind="ExternalOutput").ap()
        dbg["q"] = nc.dram_tensor("d_q", [D, T], BF16, kind="ExternalOutput").ap()
        dbg["attn"] = nc.dram_tensor("d_attn", [D, T], BF16, kind="ExternalOutput").ap()
        dbg["x1"] = nc.dram_tensor("d_x1", [D, T], F32, kind="ExternalOutput").ap()
        dbg["ct"] = nc.dram_tensor("d_ct", [NE, T], BF16, kind="ExternalOutput").ap()
        dbg["normed"] = nc.dram_tensor("d_normed", [D, T], BF16, kind="ExternalOutput").ap()
        dbg["acc"] = nc.dram_tensor("d_acc", [D, T], F32, kind="ExternalOutput").ap()

    es = contextlib.ExitStack()
    with es:
        KB = 1024

        def sbuf(name, shape, dt):
            return es.enter_context(nc.sbuf_tensor(name, shape, dt))

        s_ones = sbuf("s_ones", [128, 128], BF16)
        s_id = sbuf("s_id", [128, 128], F32)
        s_bias = sbuf("s_bias", [128, 4 * DC], F32)
        s_gamma = sbuf("s_gamma", [128, DC], F32)
        s_gw = sbuf("s_gw", [128, DC * NE], BF16)
        s_gbrow = sbuf("s_gbrow", [1, NE], BF16)
        s_b1g = sbuf("s_b1g", [128, DC * NE], F32)
        s_b1l = sbuf("s_b1l", [128, DC * NE], F32)
        s_b1gs = sbuf("s_b1gs", [128, DC * NE], F32)
        s_zero8 = sbuf("s_zero8", [128, NE], F32)
        s_eps = sbuf("s_eps", [1, 8], F32)
        s_x1 = sbuf("s_x1", [128, DC * 512], F32)

        U1 = Arena(nc, es, "U1", 96 * KB)
        s_wq = U1.tensor("s_wq", [128, DC * D], BF16, 0)
        s_wk = U1.tensor("s_wk", [128, DC * D], BF16, 16 * KB)
        s_wv = U1.tensor("s_wv", [128, DC * D], BF16, 32 * KB)
        s_wo = U1.tensor("s_wo", [128, DC * D], BF16, 48 * KB)
        s_cos = U1.tensor("s_cos", [128, S], F32, 64 * KB)
        s_sin = U1.tensor("s_sin", [128, S], F32, 80 * KB)
        s_wf = [U1.tensor(f"s_wf{i}", [128, 3 * DC * D], BF16, i * 48 * KB)
                for i in range(2)]
        s_num = U1.tensor("s_num", [128, 1024], F32, 64 * KB)
        s_att2 = U1.tensor("s_att2", [128, DC * 512], BF16, 68 * KB)

        U2 = Arena(nc, es, "U2", 32 * KB)
        s_q = U2.tensor("s_q", [128, DC * T], BF16, 0)
        c_acc = U2.tensor("c_acc", [128, DC * 512], F32, 0)
        c_xb = U2.tensor("c_xb", [128, DC * 512], BF16, 16 * KB)
        c_sq = U2.tensor("c_sq", [128, DC * 512], BF16, 24 * KB)

        U3 = Arena(nc, es, "U3", 8 * KB)
        a_ln1 = U3.tensor("a_ln1", [128, DC * ATC], BF16, 0)
        s_att = U3.tensor("s_att", [128, DC * 512], BF16, 0)
        s_sel = U3.tensor("s_sel", [NE, NE * 128], BF16, 0)
        c_sig1 = U3.tensor("c_sig1", [128, 512], F32, 2 * KB)
        c_t1a1 = U3.tensor("c_t1a1", [128, 512], F32, 4 * KB)
        c_lc1 = U3.tensor("c_lc1", [128, 512], F32, 6 * KB)

        U4 = Arena(nc, es, "U4", 16 * KB)
        a_xin = U4.tensor("a_xin", [128, DC * ATC], F32, 0)
        s_kh = [U4.tensor(f"s_kh{i}", [128, S], BF16, i * 8 * KB) for i in range(2)]
        c_xc = U4.tensor("c_xc", [128, DC * 512], BF16, 0)
        c_nrm = U4.tensor("c_nrm", [128, DC * 512], BF16, 8 * KB)

        U5 = Arena(nc, es, "U5", 16672)
        a_kst = U5.tensor("a_kst", [128, DC * ATC], BF16, 0)
        a_vst = [U5.tensor(f"a_vst{i}", [128, 16 * 65], BF16, 8 * KB + 2112 * i)
                 for i in range(2)]
        a_scrA = U5.tensor("a_scrA", [128, 512], F32, 12416)
        a_scrB = U5.tensor("a_scrB", [128, 512], F32, 14464)
        s_vh = [U5.tensor(f"s_vh{i}", [128, 32 * 130], BF16, i * 8320) for i in range(2)]
        c_act = U5.tensor("c_act", [128, DC * 512], BF16, 0)
        c_sig = U5.tensor("c_sig", [128, 512], F32, 8 * KB)
        c_t1a = U5.tensor("c_t1a", [128, 512], F32, 10 * KB)
        c_lc = U5.tensor("c_lc", [128, 512], F32, 12 * KB)
        c_ap = U5.tensor("c_ap", [128, 512], F32, 14 * KB)

        U6 = Arena(nc, es, "U6", 16 * KB)
        a_sq = U6.tensor("a_sq", [128, DC * 256], BF16, 0)
        s_stat = U6.tensor("s_stat", [1, 4 * 512], F32, 4 * KB)
        s_statb = U6.tensor("s_statb", [1, 2 * 512], BF16, 12 * KB)
        s_bvrow = U6.tensor("s_bvrow", [1, D], BF16, 14 * KB)
        s_ptb = [U6.tensor(f"s_ptb{i}", [128, 2048], BF16, i * 4 * KB) for i in range(2)]
        s_rec = U6.tensor("s_rec", [1, 2 * 512], F32, 8 * KB)
        s_recb = U6.tensor("s_recb", [1, 2 * 512], BF16, 12 * KB)
        s_rb = U6.tensor("s_rb", [64, 2 * 512], BF16, 14 * KB)
        c_ap1 = U6.tensor("c_ap1", [128, 512], F32, 4 * KB)
        c_gt = U6.tensor("c_gt", [128, 4 * 14 * NE], F32, 0)
        c_gtb = U6.tensor("c_gtb", [128, 4 * 2 * NE], BF16, 1792)
        c_gsc = U6.tensor("c_gsc", [128, 4 * 8], F32, 1920)
        s_b2 = U6.tensor("s_b2", [NE, D], BF16, 2 * KB)
        c_cbs = U6.tensor("c_cbs", [128, 512], F32, 14 * KB)

        U7 = Arena(nc, es, "U7", 4 * KB)
        a_xb = U7.tensor("a_xb", [128, DC * 256], BF16, 0)
        c_ct = U7.tensor("c_ct", [NE, T], BF16, 0)

        PBS = es.enter_context(nc.psum_tensor("pbs", [128, 2048], F32))
        PB2 = es.enter_context(nc.psum_tensor("pb2", [128, 1024], F32))
        PB3 = es.enter_context(nc.psum_tensor("pb3", [128, 1024], F32))
        PB = [PBS[:, 0:1024], PBS[:, 1024:2048], PB2[:], PB3[:]]

        P = Plan()
        SM = [P.sem(n) for n in (
            "ld_c", "ld_w", "ld_x", "ld_kv", "ld_wf", "ld_x1",
            "pe_a", "pe_b", "pe_c", "ac_a", "ac_b", "ac_c",
            "dv_a", "dv_b", "dv_c", "po_a", "po_c",
            "sp_k", "sp_v", "sp_o", "st_o", "st_d")]
        (ld_c, ld_w, ld_x, ld_kv, ld_wf, ld_x1,
         pe_a, pe_b, pe_c, ac_a, ac_b, ac_c,
         dv_a, dv_b, dv_c, po_a, po_c,
         sp_k, sp_v, sp_o, st_o, st_d) = SM

        def cview(t, n):
            return t[:].rearrange("p (c n) -> p c n", c=DC)

        xin_v = cview(a_xin, ATC)
        axb_v = cview(a_xb, 256)
        asq_v = cview(a_sq, 256)
        ln1_v = cview(a_ln1, ATC)
        kst_v = cview(a_kst, ATC)
        vst_v = [t[:].rearrange("p (j c) -> p j c", j=16) for t in a_vst]
        q_v = s_q[:].rearrange("p (c n) -> p c n", c=DC)
        wq_v = cview(s_wq, D)
        wk_v = cview(s_wk, D)
        wv_v = cview(s_wv, D)
        wo_v = cview(s_wo, D)
        vh_v = [t[:].rearrange("p (k c) -> p k c", k=32) for t in s_vh]
        att_v = cview(s_att, 512)
        att2_v = cview(s_att2, 512)
        x1_v = cview(s_x1, 512)
        wf_v = [t[:].rearrange("p (w c n) -> p w c n", w=3, c=DC) for t in s_wf]
        c_sigs = [c_sig, c_sig1]
        c_t1as = [c_t1a, c_t1a1]
        c_lcs = [c_lc, c_lc1]
        c_aps = [c_ap, c_ap1]
        xc_v = cview(c_xc, 512)
        nrm_v = cview(c_nrm, 512)
        acc_v = cview(c_acc, 512)
        act_v = cview(c_act, 512)
        cxb_v = cview(c_xb, 512)
        csq_v = cview(c_sq, 512)
        gw_v = s_gw[:].rearrange("p (c n) -> p c n", c=DC)
        gt_v = c_gt[:].rearrange("p (i n) -> p i n", i=4)
        gtb_v = c_gtb[:].rearrange("p (i n) -> p i n", i=4)
        gsc_v = c_gsc[:].rearrange("p (i n) -> p i n", i=4)
        x_fm_v = x_fm.rearrange("(c p) t -> p c t", p=128)
        o_sp_v = o_sp.rearrange("(c p) t -> p c t", p=128)
        out_fm_v = out_fm.rearrange("(c p) t -> p c t", p=128)

        st_m = s_stat[0:1, 0:512]
        st_m2 = s_stat[0:1, 512:1024]
        st_ln = s_stat[0:1, 1024:1536]
        st_rstd = s_stat[0:1, 1536:2048]

        nc.const_aps.register(F32, float(EPS), s_eps[0:1, 0:1])
        P.op("pool", lambda e: e.dma_start(out=wq_v,
                                           in_=wqT.rearrange("(c p) n -> p c n", p=128)), inc=ld_w, inc_n=16)
        P.op("pool", lambda e: e.dma_start(out=wk_v,
                                           in_=wkT.rearrange("(c p) n -> p c n", p=128)), inc=ld_w, inc_n=16)
        P.op("pool", lambda e: e.dma_start(out=wv_v,
                                           in_=wvT.rearrange("(c p) n -> p c n", p=128)), inc=ld_w, inc_n=16)
        P.op("pool", lambda e: e.dma_start(out=wo_v,
                                           in_=woT.rearrange("(c p) n -> p c n", p=128)), inc=ld_w, inc_n=16)
        P.op("pool", lambda e: e.dma_start(out=s_bvrow[:], in_=bv_row), inc=ld_c, inc_n=16)
        P.op("pool", lambda e: e.dma_start(out=s_b1g[:], in_=b1g_in), inc=ld_c, inc_n=16)
        P.op("pool", lambda e: e.dma_start(out=s_b1l[:], in_=b1l_in), inc=ld_c, inc_n=16)
        P.op("sp", lambda e: e.dma_start(out=s_bias[:], in_=b_qkvo), inc=ld_c, inc_n=16)
        P.op("sp", lambda e: e.dma_start(out=s_gamma[:], in_=gamma_in), inc=ld_c, inc_n=16)
        P.op("sp", lambda e: e.dma_start(out=s_cos[:], in_=cos_in), inc=ld_c, inc_n=16)
        P.op("sp", lambda e: e.dma_start(out=s_sin[:], in_=sin_in), inc=ld_c, inc_n=16)
        P.op("sp", lambda e: e.dma_start(out=gw_v,
                                         in_=wg_in.rearrange("(c p) n -> p c n", p=128)), inc=ld_c, inc_n=16)
        P.op("sp", lambda e: e.dma_start(out=s_gbrow[:], in_=gb_row), inc=ld_c, inc_n=16)
        P.op("sp", lambda e: e.dma_start(out=s_ones[:], in_=ones_bf_in), inc=ld_c, inc_n=16)
        P.op("sp", lambda e: e.dma_start(out=s_id[:], in_=ident_in), inc=ld_c, inc_n=16)
        n_ldc = ld_c.v
        n_ldw = ld_w.v
        P.op("dve", lambda e: e.memset(s_zero8[:], 0.0), waits=[(ld_c, n_ldc)])
        P.op("dve", lambda e: e.memset(s_eps[:], float(EPS)))
        P.op("dve", lambda e: e.tensor_scalar_mul(s_b1gs[:], s_b1g[:], ALPHA))
        P.op("dve", lambda e: e.tensor_scalar_add(s_b1l[:], s_b1l[:], 1.0))
        P.drain("dve", inc=dv_a)
        P.wait("pe", ld_c, n_ldc)
        P.wait("pe", ld_w, n_ldw)
        P.wait("pool", ld_c, n_ldc)

        _skipA = _LVL < 1
        rec_xin_free = 0
        rec_ln1_free = 0
        rec_kst_free = 0
        rec_xb_a = 0
        rec_stats_free = 0
        rec_mbrb_ac = 0
        rec_pqk_free = [0, 0]
        rec_scrA_free = 0
        rec_pv_free = 0
        rec_vst_free = [0, 0]
        p_sum1 = PB[0][0:1, 0:512]
        p_sum2 = PB[0][0:1, 512:1024]
        p_mb = PB[1][:, 0:512]
        p_rb = PB[1][:, 512:1024]
        p_qks = [PB[2][:, 0:512], PB[3][:, 512:1024]]
        p_v = [PB[2][:, 512:1024], PB[3][:, 0:512]]

        for tc in range(NATC if not _skipA else 0):
            ts = slice(tc * ATC, (tc + 1) * ATC)
            own = tc < NATC // 2
            P.op("sp", lambda e, ts=ts: e.dma_start(out=xin_v, in_=x_fm_v[:, :, ts]),
                 waits=[(dv_a, rec_xin_free)], inc=ld_x, inc_n=16)
            ldx_here = ld_x.v
            for hf in range(2):
                hsl = slice(tc * ATC + hf * 256, tc * ATC + (hf + 1) * 256)
                lsl = slice(hf * 256, (hf + 1) * 256)
                P.op("dve", lambda e, lsl=lsl: e.tensor_copy(
                    out=a_xb[:], in_=a_xin[:].rearrange("p (c n) -> p c n", c=DC)[:, :, lsl]),
                    waits=[(ld_x, ldx_here), (pe_a, rec_xb_a)])
                P.drain("dve", inc=dv_a)
                dv_xb = dv_a.v
                P.op("act", lambda e: e.activation(out=a_sq[:], in_=a_xb[:], func=AF.Square),
                     waits=[(dv_a, dv_xb)], inc=ac_a)
                ac_sq = ac_a.v
                for ci in range(DC):
                    P.op("pe", lambda e, ci=ci, hf=hf: e.matmul(
                        p_sum1[:, hf * 256:(hf + 1) * 256], s_ones[:, 0:1], axb_v[:, ci, :],
                        start=(ci == 0), stop=(ci == DC - 1)),
                        waits=([(dv_a, dv_xb)] + ([(dv_a, rec_stats_free)] if hf == 0 else [])) if ci == 0 else [],
                        inc=(pe_a if ci == DC - 1 else None))
                for ci in range(DC):
                    P.op("pe", lambda e, ci=ci, hf=hf: e.matmul(
                        p_sum2[:, hf * 256:(hf + 1) * 256], s_ones[:, 0:1], asq_v[:, ci, :],
                        start=(ci == 0), stop=(ci == DC - 1)),
                        waits=([(ac_a, ac_sq)] if ci == 0 else []),
                        inc=(pe_a if ci == DC - 1 else None))
                rec_xb_a = pe_a.v
            pe_stats = pe_a.v
            P.op("dve", lambda e: e.tensor_scalar_mul(st_m, p_sum1, 1.0 / D),
                 waits=[(pe_a, pe_stats)])
            P.drain("dve")
            P.op("dve", lambda e: e.tensor_mul(st_m2, st_m, st_m))
            P.drain("dve")
            P.op("dve", lambda e: e.scalar_tensor_tensor(
                out=st_m2, in0=p_sum2, scalar=1.0 / D, in1=st_m2,
                op0=ALU.mult, op1=ALU.subtract))
            P.drain("dve", inc=dv_a)
            dv_var = dv_a.v
            rec_stats_free = dv_var
            P.op("act", lambda e: e.activation(out=st_ln, in_=st_m2, func=AF.Ln, bias=EPS),
                 waits=[(dv_a, dv_var)])
            P.op("act", lambda e: e.drain())
            P.op("act", lambda e: e.activation(out=st_rstd, in_=st_ln, func=AF.Exp, scale=-0.5),
                 inc=ac_a)
            ac_rstd = ac_a.v
            P.op("dve", lambda e: e.tensor_copy(out=s_statb[0:1, 0:512], in_=st_m),
                 waits=[(ac_a, ac_rstd)])
            P.op("dve", lambda e: e.tensor_copy(out=s_statb[0:1, 512:1024], in_=st_rstd))
            P.drain("dve", inc=dv_a)
            dv_statb = dv_a.v
            P.op("pe", lambda e: e.matmul(p_mb, s_ones[0:1, :], s_statb[0:1, 0:512],
                                          start=True, stop=True),
                 waits=[(dv_a, dv_statb), (ac_a, rec_mbrb_ac)])
            P.op("pe", lambda e: e.matmul(p_rb, s_ones[0:1, :], s_statb[0:1, 512:1024],
                                          start=True, stop=True), inc=pe_a)
            pe_bc = pe_a.v
            P.op("act", lambda e: e.activation(out=a_scrA[:], in_=p_mb, func=AF.Copy),
                 waits=[(pe_a, pe_bc), (po_a, rec_scrA_free)])
            P.op("act", lambda e: e.activation(out=a_scrB[:], in_=p_rb, func=AF.Copy),
                 inc=ac_a)
            ac_mbrb = ac_a.v
            rec_mbrb_ac = ac_mbrb
            for ci in range(DC):
                P.op("pool", lambda e, ci=ci: e.tensor_tensor(
                    out=xin_v[:, ci, :], in0=xin_v[:, ci, :], in1=a_scrA[:],
                    op=ALU.subtract),
                    waits=([(ac_a, ac_mbrb)] if ci == 0 else []))
            for ci in range(DC):
                P.op("pool", lambda e, ci=ci: e.tensor_mul(
                    ln1_v[:, ci, :], xin_v[:, ci, :], a_scrB[:]),
                    waits=([(pe_a, rec_ln1_free)] if ci == 0 else []),
                    inc=(po_a if ci == DC - 1 else None))
            po_ln1 = po_a.v
            P.wait("dve", po_a, po_ln1)
            P.drain("dve", inc=dv_a)
            dv_ln1 = dv_a.v
            rec_xin_free = dv_ln1
            if debug:
                P.op("pool", lambda e, ts=ts: e.dma_start(
                    out=dbg["ln1"].rearrange("(c p) t -> p c t", p=128)[:, :, ts],
                    in_=ln1_v), waits=[(dv_a, dv_ln1)], inc=st_d, inc_n=16)

            for which in (("q",) if own else ()) + ("k",):
                wsel = wq_v if which == "q" else wk_v
                boff = 0 if which == "q" else DC
                for do in range(DC):
                    par = do % 2
                    p_qk = p_qks[par]
                    for ci in range(DC):
                        P.op("pe", lambda e, wsel=wsel, do=do, ci=ci, p_qk=p_qk: e.matmul(
                            p_qk, wsel[:, ci, do * 128:(do + 1) * 128],
                            ln1_v[:, ci, :],
                            start=(ci == 0), stop=(ci == DC - 1)),
                            waits=([(dv_a, dv_ln1), (dv_a, rec_pqk_free[par])] if ci == 0 else []),
                            inc=(pe_a if ci == DC - 1 else None))
                    pe_proj = pe_a.v
                    if which == "q":
                        dst = q_v[:, do, ts]
                    else:
                        dst = kst_v[:, do, :]
                    bslice = s_bias[:, boff + do:boff + do + 1]
                    dwaits = [(pe_a, pe_proj)]
                    if which == "k" and do == 0:
                        dwaits.append((sp_k, rec_kst_free))
                    P.op("dve", lambda e, dst=dst, bslice=bslice, ts=ts, p_qk=p_qk: e.scalar_tensor_tensor(
                        out=dst, in0=p_qk, scalar=bslice, in1=s_cos[:, ts],
                        op0=ALU.add, op1=ALU.mult), waits=dwaits)
                    P.op("dve", lambda e, bslice=bslice, ts=ts, p_qk=p_qk: e.scalar_tensor_tensor(
                        out=a_scrB[:], in0=p_qk, scalar=bslice, in1=s_sin[:, ts],
                        op0=ALU.add, op1=ALU.mult), inc=dv_a)
                    rec_pqk_free[par] = dv_a.v
                    P.drain("dve")
                    mask = [i ^ 1 for i in range(32)]
                    P.op("dve", lambda e, mask=mask: e.stream_shuffle(
                        out=a_scrA[:], in_=a_scrB[:], mask=mask),
                        waits=[(po_a, rec_scrA_free)], inc=dv_a)
                    dv_shf = dv_a.v
                    P.op("pool", lambda e, dst=dst: e.tensor_add(dst, dst, a_scrA[:]),
                         waits=[(dv_a, dv_shf)], inc=po_a)
                    rec_scrA_free = po_a.v
                po_rope = po_a.v
                if which == "k":
                    for hp in range(DC):
                        P.op("pool", lambda e, hp=hp, ts=ts: e.dma_start(
                            out=k_sp[hp, :, ts], in_=kst_v[:, hp, :]),
                            inc=sp_k, inc_n=16)
                    rec_kst_free = sp_k.v
                elif debug:
                    P.op("pool", lambda e, ts=ts: e.dma_start(
                        out=dbg["q"].rearrange("(c p) t -> p c t", p=128)[:, :, ts],
                        in_=q_v[:, :, ts]), inc=st_d, inc_n=16)

            for tt in range(ATC // 128):
                vb = tt % 2
                tsl = slice(tt * 128, (tt + 1) * 128)
                for nv in range(2):
                    for ci in range(DC):
                        P.op("pe", lambda e, nv=nv, ci=ci, tsl=tsl: e.matmul(
                            p_v[nv], ln1_v[:, ci, tsl],
                            wv_v[:, ci, nv * 512:(nv + 1) * 512],
                            start=(ci == 0), stop=False),
                            waits=([(dv_a, rec_pv_free)] if (ci == 0 and nv == 0) else []))
                    P.op("pe", lambda e, nv=nv: e.matmul(
                        p_v[nv], s_ones[0:1, 0:128], s_bvrow[0:1, nv * 512:(nv + 1) * 512],
                        start=False, stop=True), inc=pe_a)
                pe_vp = pe_a.v
                P.op("dve", lambda e, vb=vb: e.tensor_copy(
                    out=vst_v[vb][:, 0:8, 0:64],
                    in_=p_v[0].rearrange("p (j c) -> p j c", j=8)),
                    waits=[(pe_a, pe_vp), (sp_v, rec_vst_free[vb])])
                P.op("dve", lambda e, vb=vb: e.tensor_copy(
                    out=vst_v[vb][:, 8:16, 0:64],
                    in_=p_v[1].rearrange("p (j c) -> p j c", j=8)))
                P.op("dve", lambda e, vb=vb: e.memset(vst_v[vb][:, :, 64:65], 1.0),
                     inc=dv_a)
                dv_vst = dv_a.v
                rec_pv_free = dv_vst
                tok0 = tc * ATC + tt * 128
                for hp in range(DC):
                    P.op("pool", lambda e, hp=hp, tok0=tok0, vb=vb: e.dma_start(
                        out=v_sp[hp, tok0:tok0 + 128, :],
                        in_=a_vst[vb][:, hp * 130:(hp + 1) * 130]),
                        waits=([(dv_a, dv_vst)] if hp == 0 else []),
                        inc=sp_v, inc_n=16)
                rec_vst_free[vb] = sp_v.v
            rec_ln1_free = pe_a.v

        a_dv_end = dv_a.v
        a_pe_end = pe_a.v
        a_po_end = po_a.v
        a_spk_end = sp_k.v
        a_spv_end = sp_v.v

        _skipB = _LVL < 2
        p_sc = [PB[0][:, 0:1024], PB[1][:, 0:1024]]
        p_pv = [PB[2][:, 0:512], PB[2][:, 512:1024]]
        p_rbc = PB[3][0:64, 0:512]
        p_wo = PB[3][:, 512:1024]
        att_b = [att_v, att2_v]
        pt_slot = [s_ptb[0][:, 0:1024], s_ptb[0][:, 1024:2048], s_ptb[1][:, 0:1024]]

        rec_kh_free = [0, 0]
        rec_vh_free = [0, 0]
        rec_exp_done = [0, 0]
        rec_slot_pv = [0, 0, 0]
        rec_slot_exp = [0, 0, 0]
        rec_num_evac = [0, 0]
        rec_pwo = [0]
        rec_rbc = [0]
        dv_att_q = [0] * NQC
        spill_q = [0] * NQC

        def emit_wo_mm(ci, wo_qc, wo_do):
            wv = att_b[wo_qc % 2]
            w_waits = []
            if ci == 0:
                w_waits = [(dv_b, dv_att_q[wo_qc]), (dv_b, rec_pwo[0])]
            P.op("pe", lambda e, ci=ci, wo_do=wo_do, wv=wv: e.matmul(
                p_wo, wo_v[:, ci, wo_do * 128:(wo_do + 1) * 128], wv[:, ci, :],
                start=(ci == 0), stop=(ci == DC - 1)),
                waits=w_waits, inc=(pe_b if ci == DC - 1 else None))

        def emit_wo_evac(wo_qc, wo_do):
            pe_wo = pe_b.v
            e_waits = [(pe_b, pe_wo)]
            if wo_do == 0 and wo_qc >= 1:
                e_waits.append((sp_o, spill_q[wo_qc - 1]))
            P.op("dve", lambda e, wo_do=wo_do: e.tensor_scalar(
                out=x1_v[:, wo_do, :], in0=p_wo,
                scalar1=s_bias[:, 3 * DC + wo_do:3 * DC + wo_do + 1],
                scalar2=None, op0=ALU.add),
                waits=e_waits, inc=dv_b)
            rec_pwo[0] = dv_b.v

        def emit_spill(wo_qc):
            wqs = slice(wo_qc * 512, (wo_qc + 1) * 512)
            dv_o = dv_b.v
            P.op("pool", lambda e, wqs=wqs: e.dma_start(
                out=o_sp_v[:, :, wqs], in_=x1_v),
                waits=[(dv_b, dv_o)], inc=sp_o, inc_n=16)
            spill_q[wo_qc] = sp_o.v

        def emit_rbc(pc, j):
            P.op("pe", lambda e, j=j: e.matmul(
                p_rbc, s_ones[0:1, 0:64], s_recb[0:1, j * 512:(j + 1) * 512],
                start=True, stop=True),
                waits=[(dv_b, pc["dv_recb"] if j == 0 else 0),
                       (dv_b, rec_rbc[0])],
                inc=pe_b)
            pe_rbc = pe_b.v
            P.op("dve", lambda e, j=j: e.tensor_copy(
                out=s_rb[0:64, j * 512:(j + 1) * 512], in_=p_rbc),
                waits=[(pe_b, pe_rbc)], inc=dv_b)
            rec_rbc[0] = dv_b.v

        def emit_chain_tail(pc):
            emit_rbc(pc, 1)
            P.drain("dve")
            for j in range(2):
                P.op("dve", lambda e, j=j, pc=pc: e.tensor_mul(
                    pc["av"][64 * j:64 * (j + 1), pc["hp"], :],
                    s_num[0:64, j * 512:(j + 1) * 512],
                    s_rb[0:64, j * 512:(j + 1) * 512]))
            P.drain("dve", inc=dv_b)
            if pc["hp"] == DC - 1:
                dv_att_q[pc["qc"]] = dv_b.v
                if debug:
                    dqs = slice(pc["qc"] * 512, (pc["qc"] + 1) * 512)
                    P.op("pool", lambda e, dqs=dqs, dav=pc["av"]: e.dma_start(
                        out=dbg["attn"].rearrange("(c p) t -> p c t", p=128)[:, :, dqs],
                        in_=dav), waits=[(dv_b, dv_att_q[pc["qc"]])],
                        inc=st_d, inc_n=16)

        pend_wo = None
        pend_chain = None
        for qc in range(NQC if not _skipB else 0):
            qs = slice(qc * 512, (qc + 1) * 512)
            av = att_b[qc % 2]
            for hp in range(DC):
                kb = hp % 2
                P.op("sp", lambda e, kb=kb, hp=hp: e.dma_start(
                    out=s_kh[kb][:], in_=k_sp[hp]),
                    waits=[(sp_k, a_spk_end), (dv_a, a_dv_end), (po_a, a_po_end),
                           (pe_b, rec_kh_free[kb])],
                    inc=ld_kv, inc_n=16)
                P.op("sp", lambda e, kb=kb, hp=hp: e.dma_start(
                    out=vh_v[kb], in_=v_sp[hp].rearrange("(k p) c -> p k c", p=128)),
                    waits=[(sp_v, a_spv_end), (sp_k, a_spk_end), (pe_b, rec_vh_free[kb])],
                    inc=ld_kv, inc_n=16)
                ldkv_here = ld_kv.v

                def emit_scores(kt):
                    par = kt % 2
                    sl = kt % 3
                    ksl = slice(kt * 128, (kt + 1) * 128)
                    sc_waits = [(ld_kv, ldkv_here), (ac_b, rec_exp_done[par])]
                    P.op("pe", lambda e, ksl=ksl, par=par, kb=kb, hp=hp, qs=qs: e.matmul(
                        p_sc[par][:, 0:512], s_kh[kb][0:64, ksl], q_v[0:64, hp, qs],
                        start=True, stop=True, tile_position=(0, 0)),
                        waits=sc_waits)
                    P.op("pe", lambda e, ksl=ksl, par=par, kb=kb, hp=hp, qs=qs: e.matmul(
                        p_sc[par][:, 512:1024], s_kh[kb][64:128, ksl], q_v[64:128, hp, qs],
                        start=True, stop=True, tile_position=(64, 0)),
                        inc=pe_b)
                    pe_sc_here = pe_b.v
                    P.op("act", lambda e, par=par, sl=sl: e.activation(
                        out=pt_slot[sl], in_=p_sc[par][:, 0:1024], func=AF.Exp, scale=0.125),
                        waits=[(pe_b, pe_sc_here), (pe_b, rec_slot_pv[sl])],
                        inc=ac_b)
                    rec_exp_done[par] = ac_b.v
                    rec_slot_exp[sl] = ac_b.v

                def emit_pv(kt):
                    sl = kt % 3
                    pv0_waits = [(ac_b, rec_slot_exp[sl])]
                    pv1_waits = []
                    if kt == 0:
                        pv0_waits.append((dv_b, rec_num_evac[0]))
                        pv1_waits = [(dv_b, rec_num_evac[1])]
                    P.op("pe", lambda e, kt=kt, sl=sl, kb=kb: e.matmul(
                        p_pv[0][0:65, :], vh_v[kb][:, kt, 0:65],
                        pt_slot[sl][:, 0:512],
                        start=(kt == 0), stop=(kt == 31)),
                        waits=pv0_waits)
                    P.op("pe", lambda e, kt=kt, sl=sl, kb=kb: e.matmul(
                        p_pv[1][0:65, :], vh_v[kb][:, kt, 65:130],
                        pt_slot[sl][:, 512:1024],
                        start=(kt == 0), stop=(kt == 31)),
                        waits=pv1_waits, inc=pe_b)
                    rec_slot_pv[sl] = pe_b.v

                for kt in range(33):
                    if kt < 32:
                        emit_scores(kt)
                    if kt >= 1:
                        emit_pv(kt - 1)
                    if kt < 32:
                        if pend_chain is not None and kt == 9:
                            emit_rbc(pend_chain, 0)
                        if pend_chain is not None and kt == 11:
                            emit_chain_tail(pend_chain)
                            pend_chain = None
                        if pend_wo is not None and 16 <= kt < 24:
                            emit_wo_mm(kt - 16, pend_wo, hp)
                            if kt == 23:
                                emit_wo_evac(pend_wo, hp)
                                if hp == DC - 1:
                                    emit_spill(pend_wo)
                                    pend_wo = None
                rec_kh_free[kb] = pe_b.v
                rec_vh_free[kb] = pe_b.v
                pe_pv_end = pe_b.v
                P.op("dve", lambda e: e.tensor_copy(
                    out=s_num[0:65, 0:512], in_=p_pv[0][0:65, :]),
                    waits=[(pe_b, pe_pv_end)], inc=dv_b)
                rec_num_evac[0] = dv_b.v
                P.op("dve", lambda e: e.tensor_copy(
                    out=s_num[0:65, 512:1024], in_=p_pv[1][0:65, :]), inc=dv_b)
                rec_num_evac[1] = dv_b.v
                P.drain("dve")
                P.op("dve", lambda e: e.reciprocal(
                    out=s_rec[0:1, 0:1024], in_=s_num[64:65, 0:1024]))
                P.drain("dve")
                P.op("dve", lambda e: e.tensor_copy(out=s_recb[:], in_=s_rec[:]),
                     inc=dv_b)
                pend_chain = dict(dv_recb=dv_b.v, av=av, hp=hp, qc=qc)
            pend_wo = qc
            if qc == NQC - 1:
                emit_rbc(pend_chain, 0)
                emit_chain_tail(pend_chain)
                pend_chain = None
                for do in range(DC):
                    for ci in range(DC):
                        emit_wo_mm(ci, qc, do)
                    emit_wo_evac(qc, do)
                emit_spill(qc)

        b_spo_end = sp_o.v
        b_dv_end = dv_b.v
        b_pe_end = pe_b.v
        b_ac_end = ac_b.v

        _skipC = _LVL < 3
        p_sum1c = PB[0][0:1, 0:512]
        p_sum2c = PB[0][0:1, 512:1024]
        p_mb2 = PB[1][:, 0:512]
        p_rb2 = PB[1][:, 512:1024]
        p_lg = PB[2][:, 0:512]
        p_ct = PB[0][0:8, 512:640]
        p_cb = PB[0][:, 0:512]
        p_gl = [(PB[2][:, 0:512], PB[2][:, 512:1024]),
                (PB[1][:, 0:512], PB[1][:, 512:1024])]
        p_p2 = [PB[3][:, 0:512], PB[3][:, 512:1024]]

        rec_wf_free = [0, 0]
        rec_statc_free = 0
        rec_mbrb2_free = 0
        rec_gl_free = [0, 0]
        rec_pl_free = [0, 0]
        rec_sig_free = [0, 0]
        rec_lc_free = [0, 0]
        rec_ap_free = [0, 0]
        rec_ta_free = [0, 0]
        rec_p2_free = [0, 0]
        rec_cb_free = 0
        rec_cbs_ac = 0
        rec_x1_free = 0

        wf_loads = [(tc4, e_) for tc4 in range(NQC) for e_ in range(NE)]

        def emit_wf_load(idx):
            tc4, e_ = wf_loads[idx]
            wb = idx % 2
            waits = [(pe_c, rec_wf_free[wb])]
            if idx == 0:
                waits.append((pe_a, a_pe_end))
            if idx == 1:
                waits.append((pe_b, b_pe_end))
                waits.append((dv_b, b_dv_end))
            for wi, wt in ((0, w1gT), (1, w1lT), (2, w2T)):
                P.op("sp", lambda e, wb=wb, wi=wi, wt=wt, e_=e_: e.dma_start(
                    out=wf_v[wb][:, wi, :, :],
                    in_=wt[e_].rearrange("(c p) n -> p c n", p=128)),
                    waits=(waits if wi == 0 else []), inc=ld_wf, inc_n=16)
            return ld_wf.v

        if not _skipC:
            P.op("sp", lambda e: e.dma_start(out=s_sel[0:NE, :], in_=sel_in),
                 waits=[(pe_b, b_pe_end), (ac_b, b_ac_end)], inc=ld_x1, inc_n=16)
            P.op("sp", lambda e: e.dma_start(out=s_b2[0:NE, :], in_=b2_in),
                 inc=ld_x1, inc_n=16)
            sel_ld = ld_x1.v
            wf_ready = [0] * len(wf_loads)
            wf_ready[0] = emit_wf_load(0)
            wf_ready[1] = emit_wf_load(1)
            next_wf = [2]
        pend_mlp2 = None
        pe_mlp2_prev = [0]

        for tc4 in range(_CT4 if not _skipC else 0):
            qs = slice(tc4 * 512, (tc4 + 1) * 512)
            ct_s = slice(tc4 * 512, (tc4 + 1) * 512)
            P.op("sp", lambda e, qs=qs: e.dma_start(out=x1_v, in_=o_sp_v[:, :, qs]),
                 waits=[(sp_o, b_spo_end), (dv_c, rec_x1_free)],
                 inc=ld_x1, inc_n=16)
            P.op("pool", lambda e, qs=qs: e.dma_start(out=acc_v, in_=x_fm_v[:, :, qs]),
                 waits=[(pe_b, b_pe_end), (dv_c, rec_x1_free)], inc=ld_x1, inc_n=16)
            ldx1_here = ld_x1.v
            P.op("dve", lambda e: e.tensor_add(s_x1[:], s_x1[:], c_acc[:]),
                 waits=[(ld_x1, ldx1_here), (dv_b, b_dv_end)])
            P.drain("dve", inc=dv_c)
            dv_x1c = dv_c.v
            if debug:
                P.op("pool", lambda e, qs=qs: e.dma_start(
                    out=dbg["x1"].rearrange("(c p) t -> p c t", p=128)[:, :, qs],
                    in_=x1_v), waits=[(dv_c, dv_x1c)], inc=st_d, inc_n=16)
            P.op("dve", lambda e: e.tensor_copy(out=c_xb[:], in_=s_x1[:]))
            P.drain("dve", inc=dv_c)
            dv_xbc = dv_c.v
            P.op("act", lambda e: e.activation(out=c_sq[:], in_=c_xb[:], func=AF.Square),
                 waits=[(dv_c, dv_xbc)], inc=ac_c)
            ac_sqc = ac_c.v
            for ci in range(DC):
                P.op("pe", lambda e, ci=ci: e.matmul(
                    p_sum1c, s_ones[:, 0:1], cxb_v[:, ci, :],
                    start=(ci == 0), stop=(ci == DC - 1)),
                    waits=([(dv_c, dv_xbc), (dv_c, rec_statc_free), (pe_b, b_pe_end),
                            (ac_c, rec_cbs_ac)] if ci == 0 else []),
                    inc=(pe_c if ci == DC - 1 else None))
            for ci in range(DC):
                P.op("pe", lambda e, ci=ci: e.matmul(
                    p_sum2c, s_ones[:, 0:1], csq_v[:, ci, :],
                    start=(ci == 0), stop=(ci == DC - 1)),
                    waits=([(ac_c, ac_sqc)] if ci == 0 else []),
                    inc=(pe_c if ci == DC - 1 else None))
            pe_statc = pe_c.v
            P.op("dve", lambda e: e.tensor_scalar_mul(st_m, p_sum1c, 1.0 / D),
                 waits=[(pe_c, pe_statc)])
            P.drain("dve")
            P.op("dve", lambda e: e.tensor_mul(st_m2, st_m, st_m))
            P.drain("dve")
            P.op("dve", lambda e: e.scalar_tensor_tensor(
                out=st_m2, in0=p_sum2c, scalar=1.0 / D, in1=st_m2,
                op0=ALU.mult, op1=ALU.subtract))
            P.drain("dve", inc=dv_c)
            dv_varc = dv_c.v
            P.op("act", lambda e: e.activation(out=st_ln, in_=st_m2, func=AF.Ln, bias=EPS),
                 waits=[(dv_c, dv_varc)])
            P.op("act", lambda e: e.drain())
            P.op("act", lambda e: e.activation(out=st_rstd, in_=st_ln, func=AF.Exp, scale=-0.5),
                 inc=ac_c)
            ac_rstdc = ac_c.v
            P.op("dve", lambda e: e.tensor_copy(out=s_statb[0:1, 0:512], in_=st_m),
                 waits=[(ac_c, ac_rstdc)])
            P.op("dve", lambda e: e.tensor_copy(out=s_statb[0:1, 512:1024], in_=st_rstd))
            P.drain("dve", inc=dv_c)
            dv_statbc = dv_c.v
            P.op("pe", lambda e: e.matmul(p_mb2, s_ones[0:1, :], s_statb[0:1, 0:512],
                                          start=True, stop=True),
                 waits=[(dv_c, dv_statbc), (dv_c, rec_mbrb2_free), (ac_c, rec_cbs_ac)])
            P.op("pe", lambda e: e.matmul(p_rb2, s_ones[0:1, :], s_statb[0:1, 512:1024],
                                          start=True, stop=True), inc=pe_c)
            pe_bc2 = pe_c.v
            for ci in range(DC):
                P.op("dve", lambda e, ci=ci: e.tensor_tensor(
                    out=acc_v[:, ci, :], in0=x1_v[:, ci, :], in1=p_mb2, op=ALU.subtract),
                    waits=([(pe_c, pe_bc2)] if ci == 0 else []))
            P.drain("dve")
            for ci in range(DC):
                P.op("dve", lambda e, ci=ci: e.tensor_mul(
                    xc_v[:, ci, :], acc_v[:, ci, :], p_rb2),
                    waits=([(pe_b, b_pe_end)] if ci == 0 else []))
            P.drain("dve", inc=dv_c)
            dv_xc = dv_c.v
            P.op("act", lambda e: e.activation(out=c_sq[:], in_=c_xc[:], func=AF.Square),
                 waits=[(dv_c, dv_xc)], inc=ac_c)
            ac_sq2 = ac_c.v
            for ci in range(DC):
                P.op("pe", lambda e, ci=ci: e.matmul(
                    p_sum1c, s_ones[:, 0:1], csq_v[:, ci, :],
                    start=(ci == 0), stop=(ci == DC - 1)),
                    waits=([(ac_c, ac_sq2), (dv_c, dv_varc)] if ci == 0 else []),
                    inc=(pe_c if ci == DC - 1 else None))
            pe_rms = pe_c.v
            P.op("dve", lambda e: e.tensor_scalar_mul(st_m, p_sum1c, 1.0 / D),
                 waits=[(pe_c, pe_rms)])
            P.drain("dve", inc=dv_c)
            dv_rmsv = dv_c.v
            P.op("act", lambda e: e.activation(out=st_ln, in_=st_m, func=AF.Ln, bias=EPS),
                 waits=[(dv_c, dv_rmsv)])
            P.op("act", lambda e: e.drain())
            P.op("act", lambda e: e.activation(out=st_rstd, in_=st_ln, func=AF.Exp, scale=-0.5),
                 inc=ac_c)
            ac_rms = ac_c.v
            P.op("dve", lambda e: e.tensor_copy(out=s_statb[0:1, 512:1024], in_=st_rstd),
                 waits=[(ac_c, ac_rms)])
            P.drain("dve", inc=dv_c)
            dv_rmsb = dv_c.v
            P.op("pe", lambda e: e.matmul(p_rb2, s_ones[0:1, :], s_statb[0:1, 512:1024],
                                          start=True, stop=True),
                 waits=[(dv_c, dv_rmsb), (dv_c, dv_xc)], inc=pe_c)
            pe_rmsb = pe_c.v
            for ci in range(DC):
                P.op("dve", lambda e, ci=ci: e.scalar_tensor_tensor(
                    out=nrm_v[:, ci, :], in0=xc_v[:, ci, :], scalar=s_gamma[:, ci:ci + 1],
                    in1=p_rb2, op0=ALU.mult, op1=ALU.mult),
                    waits=([(pe_c, pe_rmsb)] if ci == 0 else []))
            P.drain("dve", inc=dv_c)
            dv_nrm = dv_c.v
            rec_mbrb2_free = dv_nrm
            if debug:
                P.op("pool", lambda e, qs=qs: e.dma_start(
                    out=dbg["normed"].rearrange("(c p) t -> p c t", p=128)[:, :, qs],
                    in_=nrm_v), waits=[(dv_c, dv_nrm)], inc=st_d, inc_n=16)

            pe_lg = [0] * 4
            for tt in range(4):
                tsl = slice(tt * 128, (tt + 1) * 128)
                for ci in range(DC):
                    P.op("pe", lambda e, ci=ci, tsl=tsl, tt=tt: e.matmul(
                        p_lg[:, tt * 64:tt * 64 + 8], nrm_v[:, ci, tsl], gw_v[:, ci, :],
                        start=(ci == 0), stop=False),
                        waits=([(dv_c, dv_nrm), (dv_c, rec_gl_free[0])] if (ci == 0 and tt == 0) else []))
                P.op("pe", lambda e, tt=tt: e.matmul(
                    p_lg[:, tt * 64:tt * 64 + 8], s_ones[0:1, 0:128],
                    s_gbrow[0:1, :], start=False, stop=True),
                    inc=pe_c)
                pe_lg[tt] = pe_c.v

            def gstep(build4, waits=(), inc=None):
                for tt in range(4):
                    P.op("dve", lambda e, tt=tt: build4(e, tt),
                         waits=(waits if tt == 0 else []),
                         inc=(inc if tt == 3 else None))
                P.drain("dve")
            g = gt_v
            gb_ = gtb_v
            gs = gsc_v
            gstep(lambda e, tt: e.tensor_copy(out=gb_[:, tt, 0:8],
                                              in_=p_lg[:, tt * 64:tt * 64 + 8]),
                  waits=[(pe_c, pe_lg[3])])
            gstep(lambda e, tt: e.tensor_copy(out=g[:, tt, 0:8], in_=gb_[:, tt, 0:8]))
            gstep(lambda e, tt: e.max(out=g[:, tt, 8:16], in_=g[:, tt, 0:8]))
            gstep(lambda e, tt: e.tensor_sub(gs[:, tt, 0:1], g[:, tt, 9:10], g[:, tt, 8:9]))
            for tt in range(4):
                P.op("dve", lambda e, tt=tt: e.tensor_scalar(
                    out=g[:, tt, 16:24], in0=g[:, tt, 0:8],
                    scalar1=g[:, tt, 8:9], scalar2=None, op0=ALU.is_equal))
            gstep(lambda e, tt: e.tensor_scalar(
                out=g[:, tt, 24:32], in0=g[:, tt, 0:8],
                scalar1=g[:, tt, 9:10], scalar2=None, op0=ALU.is_equal), inc=dv_c)
            dv_eq = dv_c.v
            for tt in range(4):
                P.op("act", lambda e, tt=tt: e.activation(
                    out=gs[:, tt, 4:5], in_=gs[:, tt, 0:1], func=AF.Sigmoid),
                    waits=([(dv_c, dv_eq)] if tt == 0 else []))
            for tt in range(4):
                P.op("act", lambda e, tt=tt: e.activation(
                    out=gs[:, tt, 3:4], in_=gs[:, tt, 0:1], func=AF.Sigmoid, scale=-1.0),
                    inc=(ac_c if tt == 3 else None))
            ac_z = ac_c.v
            P.wait("dve", ac_c, ac_z)
            gstep(lambda e, tt: e.tensor_tensor_scan(
                out=g[:, tt, 32:40], data0=g[:, tt, 16:24], data1=s_zero8[:],
                initial=0.0, op0=ALU.add, op1=ALU.add))
            gstep(lambda e, tt: e.tensor_scalar(
                out=g[:, tt, 40:48], in0=g[:, tt, 32:40],
                scalar1=1.0, scalar2=None, op0=ALU.is_equal))
            gstep(lambda e, tt: e.tensor_mul(g[:, tt, 48:56], g[:, tt, 40:48], g[:, tt, 16:24]))
            gstep(lambda e, tt: e.tensor_scalar(
                out=g[:, tt, 56:64], in0=g[:, tt, 48:56],
                scalar1=-1.0, scalar2=1.0, op0=ALU.mult, op1=ALU.add))
            gstep(lambda e, tt: e.tensor_mul(g[:, tt, 64:72], g[:, tt, 24:32], g[:, tt, 56:64]))
            gstep(lambda e, tt: e.tensor_tensor_scan(
                out=g[:, tt, 72:80], data0=g[:, tt, 64:72], data1=s_zero8[:],
                initial=0.0, op0=ALU.add, op1=ALU.add))
            gstep(lambda e, tt: e.tensor_scalar(
                out=g[:, tt, 80:88], in0=g[:, tt, 72:80],
                scalar1=1.0, scalar2=None, op0=ALU.is_equal))
            gstep(lambda e, tt: e.tensor_mul(g[:, tt, 88:96], g[:, tt, 80:88], g[:, tt, 64:72]))
            gstep(lambda e, tt: e.tensor_scalar(
                out=g[:, tt, 96:104], in0=g[:, tt, 48:56],
                scalar1=gs[:, tt, 3:4], scalar2=None, op0=ALU.mult))
            gstep(lambda e, tt: e.scalar_tensor_tensor(
                out=g[:, tt, 104:112], in0=g[:, tt, 88:96],
                scalar=gs[:, tt, 4:5], in1=g[:, tt, 96:104],
                op0=ALU.mult, op1=ALU.add), inc=dv_c)
            dv_comb = dv_c.v
            for tt in range(4):
                P.op("pe", lambda e, tt=tt: e.transpose(p_ct, g[:, tt, 104:112], s_id[:]),
                     waits=([(dv_c, dv_comb), (dv_c, rec_statc_free)] if tt == 0
                            else [(dv_c, dv_c.v)]),
                     inc=pe_c)
                pe_ct = pe_c.v
                P.op("dve", lambda e, tc4=tc4, tt=tt: e.tensor_copy(
                    out=c_ct[0:8, tc4 * 512 + tt * 128: tc4 * 512 + (tt + 1) * 128],
                    in_=p_ct), waits=[(pe_c, pe_ct)])
                P.drain("dve", inc=dv_c)
            dv_ct = dv_c.v
            rec_statc_free = dv_ct
            rec_gl_free[0] = max(rec_gl_free[0], dv_ct)
            if debug and tc4 == NQC - 1:
                P.op("pool", lambda e: e.dma_start(out=dbg["ct"], in_=c_ct[0:NE, :]),
                     waits=[(dv_c, dv_ct)], inc=st_d, inc_n=16)

            def emit_mlp2(pm):
                e_m, wb_m, po_m = pm["e_"], pm["wb"], pm["po_act"]
                for ht in range(DC):
                    hb = ht % 2
                    for ic in range(DC):
                        P.op("pe", lambda e, wb=wb_m, ht=ht, ic=ic, hb=hb, e_m=e_m: e.matmul(
                            p_p2[hb], wf_v[wb][:, 2, ic, ht * 128:(ht + 1) * 128],
                            act_v[:, ic, :],
                            start=(ic == 0), stop=(ic == DC - 1 and (e_m != NE - 1 or _CNE < NE))),
                            waits=([(po_c, po_m), (dv_c, rec_p2_free[hb])] if ic == 0 else []),
                            inc=(pe_c if (ic == DC - 1 and e_m != NE - 1) else None))
                    if e_m == NE - 1:
                        P.op("pe", lambda e, ht=ht, hb=hb, ct_s=ct_s: e.matmul(
                            p_p2[hb], s_b2[0:NE, ht * 128:(ht + 1) * 128], c_ct[0:NE, ct_s],
                            start=False, stop=True),
                            waits=[(ld_x1, sel_ld)] if ht == 0 else [], inc=pe_c)
                    pe_p2 = pe_c.v
                    if e_m == 0:
                        P.op("dve", lambda e, ht=ht, hb=hb: e.tensor_copy(
                            out=acc_v[:, ht, :], in_=p_p2[hb]),
                            waits=[(pe_c, pe_p2)], inc=dv_c)
                    else:
                        P.op("dve", lambda e, ht=ht, hb=hb: e.tensor_add(
                            acc_v[:, ht, :], acc_v[:, ht, :], p_p2[hb]),
                            waits=[(pe_c, pe_p2)], inc=dv_c)
                    rec_p2_free[hb] = dv_c.v
                P.drain("dve")
                rec_wf_free[wb_m] = pe_c.v
                pe_mlp2_prev[0] = pe_c.v
                idx_m = tc4 * NE + e_m
                nonlocal_next = idx_m + 2
                if not _skipC and next_wf[0] < len(wf_loads) and next_wf[0] == nonlocal_next:
                    wf_ready[next_wf[0]] = emit_wf_load(next_wf[0])
                    next_wf[0] += 1

            for e_ in range(_CNE):
                idx = tc4 * NE + e_
                wb = idx % 2
                wf_here = wf_ready[idx]
                dv_t1as = [0] * DC
                dv_lc2s = [0] * DC
                pe_gs = [0] * DC
                pe_ls = [0] * DC

                def emit_gl(it):
                    par = it % 2
                    p_g, p_l = p_gl[par]
                    g_waits = [(ld_wf, wf_here), (dv_c, rec_gl_free[par])]
                    if par == 1 and it == 1:
                        g_waits.append((dv_c, rec_mbrb2_free))
                    for ci in range(DC):
                        P.op("pe", lambda e, it=it, ci=ci, p_g=p_g, wb=wb: e.matmul(
                            p_g, wf_v[wb][:, 0, ci, it * 128:(it + 1) * 128], nrm_v[:, ci, :],
                            start=(ci == 0), stop=(ci == DC - 1)),
                            waits=(g_waits if ci == 0 else []),
                            inc=(pe_c if ci == DC - 1 else None))
                    pe_gs[it] = pe_c.v
                    for ci in range(DC):
                        P.op("pe", lambda e, it=it, ci=ci, p_l=p_l, wb=wb: e.matmul(
                            p_l, wf_v[wb][:, 1, ci, it * 128:(it + 1) * 128], nrm_v[:, ci, :],
                            start=(ci == 0), stop=(ci == DC - 1)),
                            waits=([(ac_c, rec_pl_free[par])] if ci == 0 else []),
                            inc=(pe_c if ci == DC - 1 else None))
                    pe_ls[it] = pe_c.v

                emit_gl(0)
                emit_gl(1)
                P.op("pe", lambda e, e_=e_, ct_s=ct_s: e.matmul(
                    p_cb, s_sel[0:NE, e_ * 128:(e_ + 1) * 128], c_ct[0:NE, ct_s],
                    start=True, stop=True),
                    waits=[(dv_c, dv_ct), (ld_x1, sel_ld), (ac_c, rec_cbs_ac)],
                    inc=pe_c)
                pe_cb = pe_c.v
                P.op("act", lambda e: e.activation(out=c_cbs[:], in_=p_cb, func=AF.Copy),
                     waits=[(pe_c, pe_cb), (dv_c, rec_cb_free)], inc=ac_c)
                ac_cbs = ac_c.v
                rec_cbs_ac = ac_cbs
                if pend_mlp2 is not None:
                    emit_mlp2(pend_mlp2)
                    pend_mlp2 = None

                def emit_stage1(it):
                    par = it % 2
                    p_g, p_l = p_gl[par]
                    pe_g_here = pe_gs[it]
                    pe_l_here = pe_ls[it]
                    bofs = it * NE + e_
                    TA, SG, LC, AP_ = c_t1as[par], c_sigs[par], c_lcs[par], c_aps[par]
                    P.op("act", lambda e, bofs=bofs, p_g=p_g, SG=SG: e.activation(
                        out=SG[:], in_=p_g, func=AF.Sigmoid,
                        bias=s_b1gs[:, bofs:bofs + 1], scale=ALPHA),
                        waits=[(pe_c, pe_g_here), (dv_c, rec_sig_free[par])], inc=ac_c)
                    ac_sig = ac_c.v
                    P.op("act", lambda e, bofs=bofs, p_l=p_l, LC=LC: e.activation(
                        out=LC[:], in_=p_l, func=AF.Identity,
                        bias=s_b1l[:, bofs:bofs + 1]),
                        waits=[(pe_c, pe_l_here), (po_c, rec_lc_free[par])], inc=ac_c)
                    ac_lraw = ac_c.v
                    rec_pl_free[par] = ac_lraw
                    P.op("dve", lambda e, bofs=bofs, p_g=p_g, TA=TA, SG=SG: e.scalar_tensor_tensor(
                        out=TA[:], in0=p_g, scalar=s_b1g[:, bofs:bofs + 1],
                        in1=SG[:], op0=ALU.add, op1=ALU.mult),
                        waits=[(ac_c, ac_sig)], inc=dv_c)
                    dv_t1a = dv_c.v
                    dv_t1as[it] = dv_t1a
                    rec_gl_free[par] = dv_t1a
                    rec_sig_free[par] = dv_t1a
                    P.op("dve", lambda e, LC=LC: e.tensor_scalar(
                        out=LC[:], in0=LC[:], scalar1=-6.0, scalar2=8.0,
                        op0=ALU.max, op1=ALU.min),
                        waits=[(ac_c, ac_lraw)], inc=dv_c)
                    dv_lc2s[it] = dv_c.v

                def emit_stage2(it):
                    par = it % 2
                    TA, LC, AP_ = c_t1as[par], c_lcs[par], c_aps[par]
                    P.op("dve", lambda e, TA=TA, AP_=AP_: e.scalar_tensor_tensor(
                        out=AP_[:], in0=TA[:], scalar=CCAP, in1=c_cbs[:],
                        op0=ALU.min, op1=ALU.mult),
                        waits=[(ac_c, ac_cbs), (po_c, rec_ap_free[par])], inc=dv_c)
                    dv_z = dv_c.v
                    rec_ta_free[par] = dv_z
                    P.op("pool", lambda e, it=it, LC=LC, AP_=AP_: e.tensor_mul(
                        act_v[:, it, :], AP_[:], LC[:]),
                        waits=[(dv_c, dv_z), (dv_c, dv_lc2s[it]),
                               (pe_c, pe_mlp2_prev[0])], inc=po_c)
                    rec_lc_free[par] = po_c.v
                    rec_ap_free[par] = po_c.v

                for it in range(DC + 1):
                    if it < DC:
                        emit_stage1(it)
                        if it + 2 < DC:
                            emit_gl(it + 2)
                    if it >= 1:
                        P.drain("dve")
                        emit_stage2(it - 1)
                P.op("pool", lambda e: e.drain(), inc=po_c)
                rec_cb_free = dv_c.v
                pend_mlp2 = dict(e_=e_, wb=wb, po_act=po_c.v)
            emit_mlp2(pend_mlp2)
            pend_mlp2 = None
            pe_mlp2_end = pe_c.v
            if debug and tc4 == NQC - 1:
                P.op("pool", lambda e, qs=qs: e.dma_start(
                    out=dbg["acc"].rearrange("(c p) t -> p c t", p=128)[:, :, qs],
                    in_=acc_v), waits=[(dv_c, dv_c.v), (pe_c, pe_mlp2_end)], inc=st_d, inc_n=16)
            P.op("dve", lambda e: e.tensor_copy(out=c_act[:], in_=c_acc[:]),
                 waits=[(pe_c, pe_mlp2_end)])
            P.drain("dve")
            P.op("dve", lambda e: e.tensor_add(c_xb[:], c_xc[:], c_act[:]))
            P.drain("dve")
            P.op("dve", lambda e: e.tensor_add(s_x1[:], s_x1[:], c_xb[:]))
            P.drain("dve", inc=dv_c)
            dv_out = dv_c.v
            rec_x1_free = dv_out
            P.op("sp", lambda e, qs=qs: e.dma_start(out=out_fm_v[:, :, qs], in_=x1_v),
                 waits=[(dv_c, dv_out)], inc=st_o, inc_n=16)

        P.wait("sp", st_o, st_o.v)
        P.wait("sp", sp_k, sp_k.v)
        P.wait("sp", sp_v, sp_v.v)
        P.wait("sp", sp_o, sp_o.v)
        P.wait("sp", ld_c, ld_c.v)
        P.wait("sp", ld_w, ld_w.v)
        P.wait("sp", ld_wf, ld_wf.v)
        if debug:
            P.wait("sp", st_d, st_d.v)

        for s in P.sems.values():
            s.h = es.enter_context(nc.semaphore(s.name))

        blk = es.enter_context(nc.Block())

        @blk.tensor
        def _(t):
            for fn in P.ops["pe"]:
                fn(t)

        @blk.scalar
        def _(sc):
            for fn in P.ops["act"]:
                fn(sc)

        @blk.vector
        def _(v):
            for fn in P.ops["dve"]:
                fn(v)

        @blk.gpsimd
        def _(g):
            for fn in P.ops["pool"]:
                fn(g)

        @blk.sync
        def _(sy):
            for fn in P.ops["sp"]:
                fn(sy)

    return nc


def _sel_const():
    sel = np.zeros((NE, NE * 128), dtype=BF)
    for e in range(NE):
        sel[e, e * 128:(e + 1) * 128] = 1
    return sel


def make_inputs(inputs):
    x = np.asarray(inputs["L0_attn_x"])
    wq = np.asarray(inputs["L0_attn_Wq"]); bq = np.asarray(inputs["L0_attn_bq"])
    wk = np.asarray(inputs["L0_attn_Wk"]); bk = np.asarray(inputs["L0_attn_bk"])
    wv = np.asarray(inputs["L0_attn_Wv"]); bv = np.asarray(inputs["L0_attn_bv"])
    wo = np.asarray(inputs["L0_attn_Wo"]); bo = np.asarray(inputs["L0_attn_bo"])
    gamma = np.asarray(inputs["L0_ffn_norm_weight"])
    gw = np.asarray(inputs["L0_ffn_gate_weight"])
    gb = np.asarray(inputs["L0_ffn_gate_bias"])
    w1 = np.asarray(inputs["L0_ffn_mlp1_weight"])
    b1 = np.asarray(inputs["L0_ffn_mlp1_bias"])
    w2 = np.asarray(inputs["L0_ffn_mlp2_weight"])
    b2 = np.asarray(inputs["L0_ffn_mlp2_bias"])

    def chunked(b):
        return np.ascontiguousarray(b.reshape(DC, 128).T)

    def bias_pe(b):
        return np.ascontiguousarray(
            b.T.reshape(DC, 128, NE).transpose(1, 0, 2).reshape(128, DC * NE))

    shared = dict(
        wqT=np.ascontiguousarray(wq.T), wkT=np.ascontiguousarray(wk.T),
        wvT=np.ascontiguousarray(wv.T), woT=np.ascontiguousarray(wo.T),
        b_qkvo=np.ascontiguousarray(np.concatenate(
            [chunked(bq), chunked(bk), chunked(bv), chunked(bo)], axis=1)),
        bv_row=np.ascontiguousarray(bv.reshape(1, D)),
        gamma_in=chunked(gamma),
        wg_in=np.ascontiguousarray(gw.T),
        gb_row=np.ascontiguousarray(gb.reshape(1, NE)),
        w1gT=np.ascontiguousarray(w1[:, 0::2, :].transpose(0, 2, 1)),
        w1lT=np.ascontiguousarray(w1[:, 1::2, :].transpose(0, 2, 1)),
        w2T=np.ascontiguousarray(w2.transpose(0, 2, 1)),
        b1g_in=bias_pe(b1[:, 0::2]),
        b1l_in=bias_pe(b1[:, 1::2]),
        b2_in=np.ascontiguousarray(b2),
        ones_bf_in=np.ones((128, 128), dtype=BF),
        ident_in=np.eye(128, dtype=np.float32),
        sel_in=_sel_const(),
    )
    in_maps = []
    for c in range(8):
        b, h = c // 2, c % 2
        perm = np.concatenate([np.arange(h * T, S), np.arange(0, h * T)])
        cos_t, sin_t = _rope_tables(perm)
        xc = np.ascontiguousarray(x[b].T[:, perm])
        m = dict(shared)
        m.update(x_fm=xc, cos_in=cos_t, sin_in=sin_t)
        in_maps.append(m)
    return in_maps


def kernel(**inputs):
    in_maps = make_inputs(inputs)
    nc = build_program(debug=False)
    res = run_bass_kernel_spmd(nc, in_maps, core_ids=list(range(8)))
    out = np.empty((4, S, D), dtype=np.float32)
    for c in range(8):
        b, h = c // 2, c % 2
        out[b, h * T:(h + 1) * T, :] = np.asarray(res.results[c]["out_fm"]).T
    return out


# version 12
# speedup vs baseline: 1.1856x; 1.0276x over previous
import contextlib
import numpy as np
import ml_dtypes
import concourse.bass as bass
import concourse.mybir as mybir
from concourse.bass_utils import run_bass_kernel_spmd

F32 = mybir.dt.float32
BF16 = mybir.dt.bfloat16
AF = mybir.ActivationFunctionType
ALU = mybir.AluOpType
BF = ml_dtypes.bfloat16

D = 1024
S = 4096
T = 2048
HD = 64
NE = 8
DC = 8
ATC = 512
NATC = S // ATC
NQC = 4
EPS = 1e-5
ALPHA = 1.702
CCAP = float(np.float32(7.0) * np.float32(
    1.0 / (1.0 + np.exp(-np.float64(np.float32(ALPHA) * np.float32(7.0))))))

ENGS = ("pe", "act", "dve", "pool", "sp")


class Sem:
    def __init__(self, name):
        self.name = name
        self.h = None
        self.v = 0


class Plan:
    def __init__(self):
        self.ops = {e: [] for e in ENGS}
        self.sems = {}

    def sem(self, name):
        if name not in self.sems:
            self.sems[name] = Sem(name)
        return self.sems[name]

    def op(self, eng, build, waits=(), inc=None, inc_n=1):
        if inc is not None:
            inc.v += inc_n

        def fn(engine, waits=tuple(waits), build=build, inc=inc, inc_n=inc_n):
            for s, v in waits:
                if v > 0:
                    engine.wait_ge(s.h, v)
            ins = build(engine)
            if inc is not None:
                ins.then_inc(inc.h, inc_n)
        self.ops[eng].append(fn)

    def wait(self, eng, sem, val):
        def fn(engine, sem=sem, val=val):
            engine.wait_ge(sem.h, val)
        self.ops[eng].append(fn)

    def drain(self, eng, inc=None, inc_n=1):
        self.op(eng, lambda e: e.drain(), inc=inc, inc_n=inc_n)


def _rope_tables(perm):
    dim = np.float32(HD)
    inv_freq = (np.float32(1.0) / (np.float32(10000.0) ** (
        np.arange(0, HD, 2, dtype=np.float32) / dim))).astype(np.float32)
    pos = np.arange(S, dtype=np.float32)
    freqs = (pos[:, None] * inv_freq[None, :]).astype(np.float32)
    freqs = np.repeat(freqs, 2, axis=-1)
    cos_t = np.cos(freqs.astype(np.float64)).astype(np.float32)
    sin_t = np.sin(freqs.astype(np.float64)).astype(np.float32)
    sign = np.ones(HD, dtype=np.float32)
    sign[0::2] = -1.0
    sin_s = sin_t * sign[None, :]
    cos2 = np.concatenate([cos_t[perm].T, cos_t[perm].T], axis=0)
    sin2 = np.concatenate([sin_s[perm].T, sin_s[perm].T], axis=0)
    sinp = sin2.copy()
    sinp[0::2], sinp[1::2] = sin2[1::2], sin2[0::2]
    return np.ascontiguousarray(cos2), np.ascontiguousarray(sinp)


class Arena:

    def __init__(self, nc, es, name, nbytes):
        self.nc = nc
        self.name = name
        self.nbytes = nbytes
        self.backing = es.enter_context(
            nc.sbuf_tensor(name, [128, nbytes], mybir.dt.uint8))
        self.base = nc.lookup_mloc(self.backing).addr

    def tensor(self, name, shape, dtype, off):
        per_part = int(np.prod(shape[1:])) * mybir.dt.size(dtype)
        assert off + per_part <= self.nbytes, (self.name, name, off, per_part)
        assert (self.base + off) % 32 == 0, (name, self.base, off)
        h = self.nc.virtual_sbuf_tensor(name, shape, dtype)
        self.nc._tensor(h.name, shape, dtype, type="SB")
        mloc = self.nc.lookup_mloc(h)
        mloc.allocated = True
        mloc.addr = self.base + off
        return h


def build_program(debug=False, stop_after=None, c_lim=None):
    _LVL = {"pre": 0, "A": 1, "B": 2, "C": 3, None: 3}[stop_after]
    _CT4, _CNE = c_lim if c_lim is not None else (NQC, NE)
    nc = bass.Bass("TRN2", target_bir_lowering=False, debug=False)

    x_fm = nc.dram_tensor("x_fm", [D, S], F32, kind="ExternalInput").ap()
    cos_in = nc.dram_tensor("cos_in", [128, S], F32, kind="ExternalInput").ap()
    sin_in = nc.dram_tensor("sin_in", [128, S], F32, kind="ExternalInput").ap()
    wqT = nc.dram_tensor("wqT", [D, D], F32, kind="ExternalInput").ap()
    wkT = nc.dram_tensor("wkT", [D, D], F32, kind="ExternalInput").ap()
    wvT = nc.dram_tensor("wvT", [D, D], F32, kind="ExternalInput").ap()
    woT = nc.dram_tensor("woT", [D, D], F32, kind="ExternalInput").ap()
    b_qkvo = nc.dram_tensor("b_qkvo", [128, 4 * DC], F32, kind="ExternalInput").ap()
    bv_row = nc.dram_tensor("bv_row", [1, D], F32, kind="ExternalInput").ap()
    gamma_in = nc.dram_tensor("gamma_in", [128, DC], F32, kind="ExternalInput").ap()
    wg_in = nc.dram_tensor("wg_in", [D, NE], BF16, kind="ExternalInput").ap()
    gb_row = nc.dram_tensor("gb_row", [1, NE], BF16, kind="ExternalInput").ap()
    w1gT = nc.dram_tensor("w1gT", [NE, D, D], BF16, kind="ExternalInput").ap()
    w1lT = nc.dram_tensor("w1lT", [NE, D, D], BF16, kind="ExternalInput").ap()
    w2T = nc.dram_tensor("w2T", [NE, D, D], BF16, kind="ExternalInput").ap()
    b1g_in = nc.dram_tensor("b1g_in", [128, DC * NE], BF16, kind="ExternalInput").ap()
    b1l_in = nc.dram_tensor("b1l_in", [128, DC * NE], BF16, kind="ExternalInput").ap()
    b2_in = nc.dram_tensor("b2_in", [NE, D], BF16, kind="ExternalInput").ap()
    ones_bf_in = nc.dram_tensor("ones_bf_in", [128, 128], BF16, kind="ExternalInput").ap()
    ident_in = nc.dram_tensor("ident_in", [128, 128], F32, kind="ExternalInput").ap()
    sel_in = nc.dram_tensor("sel_in", [NE, NE * 128], BF16, kind="ExternalInput").ap()

    out_fm = nc.dram_tensor("out_fm", [D, T], F32, kind="ExternalOutput").ap()

    k_sp = nc.dram_tensor("k_sp", [DC, 128, S], BF16, kind="Internal").ap()
    v_sp = nc.dram_tensor("v_sp", [DC, S, 130], BF16, kind="Internal").ap()
    o_sp = nc.dram_tensor("o_sp", [D, T], F32, kind="Internal").ap()

    dbg = {}
    if debug:
        dbg["ln1"] = nc.dram_tensor("d_ln1", [D, S], BF16, kind="ExternalOutput").ap()
        dbg["q"] = nc.dram_tensor("d_q", [D, T], BF16, kind="ExternalOutput").ap()
        dbg["attn"] = nc.dram_tensor("d_attn", [D, T], BF16, kind="ExternalOutput").ap()
        dbg["x1"] = nc.dram_tensor("d_x1", [D, T], F32, kind="ExternalOutput").ap()
        dbg["ct"] = nc.dram_tensor("d_ct", [NE, T], BF16, kind="ExternalOutput").ap()
        dbg["normed"] = nc.dram_tensor("d_normed", [D, T], BF16, kind="ExternalOutput").ap()
        dbg["acc"] = nc.dram_tensor("d_acc", [D, T], F32, kind="ExternalOutput").ap()

    es = contextlib.ExitStack()
    with es:
        KB = 1024

        def sbuf(name, shape, dt):
            return es.enter_context(nc.sbuf_tensor(name, shape, dt))

        s_ones = sbuf("s_ones", [128, 128], BF16)
        s_id = sbuf("s_id", [128, 128], F32)
        s_bias = sbuf("s_bias", [128, 4 * DC], F32)
        s_gamma = sbuf("s_gamma", [128, DC], F32)
        s_gw = sbuf("s_gw", [128, DC * NE], BF16)
        s_gbrow = sbuf("s_gbrow", [1, NE], BF16)
        s_b1g = sbuf("s_b1g", [128, DC * NE], F32)
        s_b1l = sbuf("s_b1l", [128, DC * NE], F32)
        s_b1gs = sbuf("s_b1gs", [128, DC * NE], F32)
        s_zero8 = sbuf("s_zero8", [128, NE], F32)
        s_eps = sbuf("s_eps", [1, 8], F32)
        s_x1 = sbuf("s_x1", [128, DC * 512], F32)

        U1 = Arena(nc, es, "U1", 96 * KB)
        s_wq = U1.tensor("s_wq", [128, DC * D], BF16, 0)
        s_wk = U1.tensor("s_wk", [128, DC * D], BF16, 16 * KB)
        s_wv = U1.tensor("s_wv", [128, DC * D], BF16, 32 * KB)
        s_wo = U1.tensor("s_wo", [128, DC * D], BF16, 48 * KB)
        s_cos = U1.tensor("s_cos", [128, S], F32, 64 * KB)
        s_sin = U1.tensor("s_sin", [128, S], F32, 80 * KB)
        s_wf = [U1.tensor(f"s_wf{i}", [128, 3 * DC * D], BF16, i * 48 * KB)
                for i in range(2)]
        s_num = U1.tensor("s_num", [128, 1024], F32, 64 * KB)
        s_att2 = U1.tensor("s_att2", [128, DC * 512], BF16, 68 * KB)

        U2 = Arena(nc, es, "U2", 32 * KB)
        s_q = U2.tensor("s_q", [128, DC * T], BF16, 0)
        c_acc = U2.tensor("c_acc", [128, DC * 512], F32, 0)
        c_xb = U2.tensor("c_xb", [128, DC * 512], BF16, 16 * KB)
        c_sq = U2.tensor("c_sq", [128, DC * 512], BF16, 24 * KB)

        U3 = Arena(nc, es, "U3", 8 * KB)
        a_ln1 = U3.tensor("a_ln1", [128, DC * ATC], BF16, 0)
        s_att = U3.tensor("s_att", [128, DC * 512], BF16, 0)
        s_sel = U3.tensor("s_sel", [NE, NE * 128], BF16, 0)
        c_sig1 = U3.tensor("c_sig1", [128, 512], F32, 2 * KB)
        c_t1a1 = U3.tensor("c_t1a1", [128, 512], F32, 4 * KB)
        c_lc1 = U3.tensor("c_lc1", [128, 512], F32, 6 * KB)

        U4 = Arena(nc, es, "U4", 16 * KB)
        a_xin = U4.tensor("a_xin", [128, DC * ATC], F32, 0)
        s_kh = [U4.tensor(f"s_kh{i}", [128, S], BF16, i * 8 * KB) for i in range(2)]
        c_xc = U4.tensor("c_xc", [128, DC * 512], BF16, 0)
        c_nrm = U4.tensor("c_nrm", [128, DC * 512], BF16, 8 * KB)

        U5 = Arena(nc, es, "U5", 16672)
        a_kst = U5.tensor("a_kst", [128, DC * ATC], BF16, 0)
        a_vst = [U5.tensor(f"a_vst{i}", [128, 16 * 65], BF16, 8 * KB + 2112 * i)
                 for i in range(2)]
        a_scrA = U5.tensor("a_scrA", [128, 512], F32, 12416)
        a_scrB = U5.tensor("a_scrB", [128, 512], F32, 14464)
        s_vh = [U5.tensor(f"s_vh{i}", [128, 32 * 130], BF16, i * 8320) for i in range(2)]
        c_act = U5.tensor("c_act", [128, DC * 512], BF16, 0)
        c_sig = U5.tensor("c_sig", [128, 512], F32, 8 * KB)
        c_t1a = U5.tensor("c_t1a", [128, 512], F32, 10 * KB)
        c_lc = U5.tensor("c_lc", [128, 512], F32, 12 * KB)
        c_ap = U5.tensor("c_ap", [128, 512], F32, 14 * KB)

        U6 = Arena(nc, es, "U6", 16 * KB)
        a_sq = U6.tensor("a_sq", [128, DC * 256], BF16, 0)
        s_stat = U6.tensor("s_stat", [1, 4 * 512], F32, 4 * KB)
        s_statb = U6.tensor("s_statb", [1, 2 * 512], BF16, 12 * KB)
        s_bvrow = U6.tensor("s_bvrow", [1, D], BF16, 14 * KB)
        s_ptb = [U6.tensor(f"s_ptb{i}", [128, 2048], BF16, i * 4 * KB) for i in range(2)]
        s_rec = U6.tensor("s_rec", [1, 2 * 512], F32, 8 * KB)
        s_recb = U6.tensor("s_recb", [1, 2 * 512], BF16, 12 * KB)
        s_rb = U6.tensor("s_rb", [64, 2 * 512], BF16, 14 * KB)
        c_ap1 = U6.tensor("c_ap1", [128, 512], F32, 4 * KB)
        c_gt = U6.tensor("c_gt", [128, 4 * 14 * NE], F32, 0)
        c_gtb = U6.tensor("c_gtb", [128, 4 * 2 * NE], BF16, 1792)
        c_gsc = U6.tensor("c_gsc", [128, 4 * 8], F32, 1920)
        s_b2 = U6.tensor("s_b2", [NE, D], BF16, 2 * KB)
        c_cbs = U6.tensor("c_cbs", [128, 512], F32, 14 * KB)

        U7 = Arena(nc, es, "U7", 4 * KB)
        a_xb = U7.tensor("a_xb", [128, DC * 256], BF16, 0)
        c_ct = U7.tensor("c_ct", [NE, T], BF16, 0)

        PBS = es.enter_context(nc.psum_tensor("pbs", [128, 2048], F32))
        PB2 = es.enter_context(nc.psum_tensor("pb2", [128, 1024], F32))
        PB3 = es.enter_context(nc.psum_tensor("pb3", [128, 1024], F32))
        PB = [PBS[:, 0:1024], PBS[:, 1024:2048], PB2[:], PB3[:]]

        P = Plan()
        SM = [P.sem(n) for n in (
            "ld_c", "ld_w", "ld_x", "ld_kv", "ld_wf", "ld_x1",
            "pe_a", "pe_b", "pe_c", "ac_a", "ac_b", "ac_c",
            "dv_a", "dv_b", "dv_c", "po_a", "po_c",
            "sp_k", "sp_v", "sp_o", "st_o", "st_d")]
        (ld_c, ld_w, ld_x, ld_kv, ld_wf, ld_x1,
         pe_a, pe_b, pe_c, ac_a, ac_b, ac_c,
         dv_a, dv_b, dv_c, po_a, po_c,
         sp_k, sp_v, sp_o, st_o, st_d) = SM

        def cview(t, n):
            return t[:].rearrange("p (c n) -> p c n", c=DC)

        xin_v = cview(a_xin, ATC)
        axb_v = cview(a_xb, 256)
        asq_v = cview(a_sq, 256)
        ln1_v = cview(a_ln1, ATC)
        kst_v = cview(a_kst, ATC)
        vst_v = [t[:].rearrange("p (j c) -> p j c", j=16) for t in a_vst]
        q_v = s_q[:].rearrange("p (c n) -> p c n", c=DC)
        wq_v = cview(s_wq, D)
        wk_v = cview(s_wk, D)
        wv_v = cview(s_wv, D)
        wo_v = cview(s_wo, D)
        vh_v = [t[:].rearrange("p (k c) -> p k c", k=32) for t in s_vh]
        att_v = cview(s_att, 512)
        att2_v = cview(s_att2, 512)
        x1_v = cview(s_x1, 512)
        wf_v = [t[:].rearrange("p (w c n) -> p w c n", w=3, c=DC) for t in s_wf]
        c_sigs = [c_sig, c_sig1]
        c_t1as = [c_t1a, c_t1a1]
        c_lcs = [c_lc, c_lc1]
        c_aps = [c_ap, c_ap1]
        xc_v = cview(c_xc, 512)
        nrm_v = cview(c_nrm, 512)
        acc_v = cview(c_acc, 512)
        act_v = cview(c_act, 512)
        cxb_v = cview(c_xb, 512)
        csq_v = cview(c_sq, 512)
        gw_v = s_gw[:].rearrange("p (c n) -> p c n", c=DC)
        gt_v = c_gt[:].rearrange("p (i n) -> p i n", i=4)
        gtb_v = c_gtb[:].rearrange("p (i n) -> p i n", i=4)
        gsc_v = c_gsc[:].rearrange("p (i n) -> p i n", i=4)
        x_fm_v = x_fm.rearrange("(c p) t -> p c t", p=128)
        o_sp_v = o_sp.rearrange("(c p) t -> p c t", p=128)
        out_fm_v = out_fm.rearrange("(c p) t -> p c t", p=128)

        st_m = s_stat[0:1, 0:512]
        st_m2 = s_stat[0:1, 512:1024]
        st_ln = s_stat[0:1, 1024:1536]
        st_rstd = s_stat[0:1, 1536:2048]

        nc.const_aps.register(F32, float(EPS), s_eps[0:1, 0:1])
        P.op("pool", lambda e: e.dma_start(out=wq_v,
                                           in_=wqT.rearrange("(c p) n -> p c n", p=128)), inc=ld_w, inc_n=16)
        P.op("pool", lambda e: e.dma_start(out=wk_v,
                                           in_=wkT.rearrange("(c p) n -> p c n", p=128)), inc=ld_w, inc_n=16)
        P.op("pool", lambda e: e.dma_start(out=wv_v,
                                           in_=wvT.rearrange("(c p) n -> p c n", p=128)), inc=ld_w, inc_n=16)
        P.op("pool", lambda e: e.dma_start(out=wo_v,
                                           in_=woT.rearrange("(c p) n -> p c n", p=128)), inc=ld_w, inc_n=16)
        P.op("pool", lambda e: e.dma_start(out=s_bvrow[:], in_=bv_row), inc=ld_c, inc_n=16)
        P.op("pool", lambda e: e.dma_start(out=s_b1g[:], in_=b1g_in), inc=ld_c, inc_n=16)
        P.op("pool", lambda e: e.dma_start(out=s_b1l[:], in_=b1l_in), inc=ld_c, inc_n=16)
        P.op("sp", lambda e: e.dma_start(out=s_bias[:], in_=b_qkvo), inc=ld_c, inc_n=16)
        P.op("sp", lambda e: e.dma_start(out=s_gamma[:], in_=gamma_in), inc=ld_c, inc_n=16)
        P.op("sp", lambda e: e.dma_start(out=s_cos[:], in_=cos_in), inc=ld_c, inc_n=16)
        P.op("sp", lambda e: e.dma_start(out=s_sin[:], in_=sin_in), inc=ld_c, inc_n=16)
        P.op("sp", lambda e: e.dma_start(out=gw_v,
                                         in_=wg_in.rearrange("(c p) n -> p c n", p=128)), inc=ld_c, inc_n=16)
        P.op("sp", lambda e: e.dma_start(out=s_gbrow[:], in_=gb_row), inc=ld_c, inc_n=16)
        P.op("sp", lambda e: e.dma_start(out=s_ones[:], in_=ones_bf_in), inc=ld_c, inc_n=16)
        P.op("sp", lambda e: e.dma_start(out=s_id[:], in_=ident_in), inc=ld_c, inc_n=16)
        n_ldc = ld_c.v
        n_ldw = ld_w.v
        P.op("dve", lambda e: e.memset(s_zero8[:], 0.0), waits=[(ld_c, n_ldc)])
        P.op("dve", lambda e: e.memset(s_eps[:], float(EPS)))
        P.op("dve", lambda e: e.tensor_scalar_mul(s_b1gs[:], s_b1g[:], ALPHA))
        P.op("dve", lambda e: e.tensor_scalar_add(s_b1l[:], s_b1l[:], 1.0))
        P.drain("dve", inc=dv_a)
        P.wait("pe", ld_c, n_ldc)
        P.wait("pe", ld_w, n_ldw)
        P.wait("pool", ld_c, n_ldc)

        _skipA = _LVL < 1
        rec_xin_free = 0
        rec_ln1_free = 0
        rec_kst_free = 0
        rec_xb_a = 0
        rec_stats_free = 0
        rec_mbrb_ac = 0
        rec_pqk_free = [0, 0]
        rec_scrA_free = 0
        rec_pv_free = 0
        rec_vst_free = [0, 0]
        p_sum1 = PB[0][0:1, 0:512]
        p_sum2 = PB[0][0:1, 512:1024]
        p_mb = PB[1][:, 0:512]
        p_rb = PB[1][:, 512:1024]
        p_qks = [PB[2][:, 0:512], PB[3][:, 512:1024]]
        p_v = [PB[2][:, 512:1024], PB[3][:, 0:512]]

        for tc in range(NATC if not _skipA else 0):
            ts = slice(tc * ATC, (tc + 1) * ATC)
            own = tc < NATC // 2
            P.op("sp", lambda e, ts=ts: e.dma_start(out=xin_v, in_=x_fm_v[:, :, ts]),
                 waits=[(dv_a, rec_xin_free)], inc=ld_x, inc_n=16)
            ldx_here = ld_x.v
            for hf in range(2):
                hsl = slice(tc * ATC + hf * 256, tc * ATC + (hf + 1) * 256)
                lsl = slice(hf * 256, (hf + 1) * 256)
                P.op("dve", lambda e, lsl=lsl: e.tensor_copy(
                    out=a_xb[:], in_=a_xin[:].rearrange("p (c n) -> p c n", c=DC)[:, :, lsl]),
                    waits=[(ld_x, ldx_here), (pe_a, rec_xb_a)])
                P.drain("dve", inc=dv_a)
                dv_xb = dv_a.v
                P.op("act", lambda e: e.activation(out=a_sq[:], in_=a_xb[:], func=AF.Square),
                     waits=[(dv_a, dv_xb)], inc=ac_a)
                ac_sq = ac_a.v
                for ci in range(DC):
                    P.op("pe", lambda e, ci=ci, hf=hf: e.matmul(
                        p_sum1[:, hf * 256:(hf + 1) * 256], s_ones[:, 0:1], axb_v[:, ci, :],
                        start=(ci == 0), stop=(ci == DC - 1)),
                        waits=([(dv_a, dv_xb)] + ([(dv_a, rec_stats_free)] if hf == 0 else [])) if ci == 0 else [],
                        inc=(pe_a if ci == DC - 1 else None))
                for ci in range(DC):
                    P.op("pe", lambda e, ci=ci, hf=hf: e.matmul(
                        p_sum2[:, hf * 256:(hf + 1) * 256], s_ones[:, 0:1], asq_v[:, ci, :],
                        start=(ci == 0), stop=(ci == DC - 1)),
                        waits=([(ac_a, ac_sq)] if ci == 0 else []),
                        inc=(pe_a if ci == DC - 1 else None))
                rec_xb_a = pe_a.v
            pe_stats = pe_a.v
            P.op("dve", lambda e: e.tensor_scalar_mul(st_m, p_sum1, 1.0 / D),
                 waits=[(pe_a, pe_stats)])
            P.drain("dve")
            P.op("dve", lambda e: e.tensor_mul(st_m2, st_m, st_m))
            P.drain("dve")
            P.op("dve", lambda e: e.scalar_tensor_tensor(
                out=st_m2, in0=p_sum2, scalar=1.0 / D, in1=st_m2,
                op0=ALU.mult, op1=ALU.subtract))
            P.drain("dve", inc=dv_a)
            dv_var = dv_a.v
            rec_stats_free = dv_var
            P.op("act", lambda e: e.activation(out=st_ln, in_=st_m2, func=AF.Ln, bias=EPS),
                 waits=[(dv_a, dv_var)])
            P.op("act", lambda e: e.drain())
            P.op("act", lambda e: e.activation(out=st_rstd, in_=st_ln, func=AF.Exp, scale=-0.5),
                 inc=ac_a)
            ac_rstd = ac_a.v
            P.op("dve", lambda e: e.tensor_copy(out=s_statb[0:1, 0:512], in_=st_m),
                 waits=[(ac_a, ac_rstd)])
            P.op("dve", lambda e: e.tensor_copy(out=s_statb[0:1, 512:1024], in_=st_rstd))
            P.drain("dve", inc=dv_a)
            dv_statb = dv_a.v
            P.op("pe", lambda e: e.matmul(p_mb, s_ones[0:1, :], s_statb[0:1, 0:512],
                                          start=True, stop=True),
                 waits=[(dv_a, dv_statb), (ac_a, rec_mbrb_ac)])
            P.op("pe", lambda e: e.matmul(p_rb, s_ones[0:1, :], s_statb[0:1, 512:1024],
                                          start=True, stop=True), inc=pe_a)
            pe_bc = pe_a.v
            P.op("act", lambda e: e.activation(out=a_scrA[:], in_=p_mb, func=AF.Copy),
                 waits=[(pe_a, pe_bc), (po_a, rec_scrA_free)])
            P.op("act", lambda e: e.activation(out=a_scrB[:], in_=p_rb, func=AF.Copy),
                 inc=ac_a)
            ac_mbrb = ac_a.v
            rec_mbrb_ac = ac_mbrb
            for ci in range(DC):
                P.op("pool", lambda e, ci=ci: e.tensor_tensor(
                    out=xin_v[:, ci, :], in0=xin_v[:, ci, :], in1=a_scrA[:],
                    op=ALU.subtract),
                    waits=([(ac_a, ac_mbrb)] if ci == 0 else []))
            for ci in range(DC):
                P.op("pool", lambda e, ci=ci: e.tensor_mul(
                    ln1_v[:, ci, :], xin_v[:, ci, :], a_scrB[:]),
                    waits=([(pe_a, rec_ln1_free)] if ci == 0 else []),
                    inc=(po_a if ci == DC - 1 else None))
            po_ln1 = po_a.v
            P.wait("dve", po_a, po_ln1)
            P.drain("dve", inc=dv_a)
            dv_ln1 = dv_a.v
            rec_xin_free = dv_ln1
            if debug:
                P.op("pool", lambda e, ts=ts: e.dma_start(
                    out=dbg["ln1"].rearrange("(c p) t -> p c t", p=128)[:, :, ts],
                    in_=ln1_v), waits=[(dv_a, dv_ln1)], inc=st_d, inc_n=16)

            for which in (("q",) if own else ()) + ("k",):
                wsel = wq_v if which == "q" else wk_v
                boff = 0 if which == "q" else DC
                for do in range(DC):
                    par = do % 2
                    p_qk = p_qks[par]
                    for ci in range(DC):
                        P.op("pe", lambda e, wsel=wsel, do=do, ci=ci, p_qk=p_qk: e.matmul(
                            p_qk, wsel[:, ci, do * 128:(do + 1) * 128],
                            ln1_v[:, ci, :],
                            start=(ci == 0), stop=(ci == DC - 1)),
                            waits=([(dv_a, dv_ln1), (dv_a, rec_pqk_free[par])] if ci == 0 else []),
                            inc=(pe_a if ci == DC - 1 else None))
                    pe_proj = pe_a.v
                    if which == "q":
                        dst = q_v[:, do, ts]
                    else:
                        dst = kst_v[:, do, :]
                    bslice = s_bias[:, boff + do:boff + do + 1]
                    dwaits = [(pe_a, pe_proj)]
                    if which == "k" and do == 0:
                        dwaits.append((sp_k, rec_kst_free))
                    P.op("dve", lambda e, dst=dst, bslice=bslice, ts=ts, p_qk=p_qk: e.scalar_tensor_tensor(
                        out=dst, in0=p_qk, scalar=bslice, in1=s_cos[:, ts],
                        op0=ALU.add, op1=ALU.mult), waits=dwaits)
                    P.op("dve", lambda e, bslice=bslice, ts=ts, p_qk=p_qk: e.scalar_tensor_tensor(
                        out=a_scrB[:], in0=p_qk, scalar=bslice, in1=s_sin[:, ts],
                        op0=ALU.add, op1=ALU.mult), inc=dv_a)
                    rec_pqk_free[par] = dv_a.v
                    P.drain("dve")
                    mask = [i ^ 1 for i in range(32)]
                    P.op("dve", lambda e, mask=mask: e.stream_shuffle(
                        out=a_scrA[:], in_=a_scrB[:], mask=mask),
                        waits=[(po_a, rec_scrA_free)], inc=dv_a)
                    dv_shf = dv_a.v
                    P.op("pool", lambda e, dst=dst: e.tensor_add(dst, dst, a_scrA[:]),
                         waits=[(dv_a, dv_shf)], inc=po_a)
                    rec_scrA_free = po_a.v
                po_rope = po_a.v
                if which == "k":
                    for hp in range(DC):
                        P.op("pool", lambda e, hp=hp, ts=ts: e.dma_start(
                            out=k_sp[hp, :, ts], in_=kst_v[:, hp, :]),
                            inc=sp_k, inc_n=16)
                    rec_kst_free = sp_k.v
                elif debug:
                    P.op("pool", lambda e, ts=ts: e.dma_start(
                        out=dbg["q"].rearrange("(c p) t -> p c t", p=128)[:, :, ts],
                        in_=q_v[:, :, ts]), inc=st_d, inc_n=16)

            for tt in range(ATC // 128):
                vb = tt % 2
                tsl = slice(tt * 128, (tt + 1) * 128)
                for nv in range(2):
                    for ci in range(DC):
                        P.op("pe", lambda e, nv=nv, ci=ci, tsl=tsl: e.matmul(
                            p_v[nv], ln1_v[:, ci, tsl],
                            wv_v[:, ci, nv * 512:(nv + 1) * 512],
                            start=(ci == 0), stop=False),
                            waits=([(dv_a, rec_pv_free)] if (ci == 0 and nv == 0) else []))
                    P.op("pe", lambda e, nv=nv: e.matmul(
                        p_v[nv], s_ones[0:1, 0:128], s_bvrow[0:1, nv * 512:(nv + 1) * 512],
                        start=False, stop=True), inc=pe_a)
                pe_vp = pe_a.v
                P.op("dve", lambda e, vb=vb: e.tensor_copy(
                    out=vst_v[vb][:, 0:8, 0:64],
                    in_=p_v[0].rearrange("p (j c) -> p j c", j=8)),
                    waits=[(pe_a, pe_vp), (sp_v, rec_vst_free[vb])])
                P.op("dve", lambda e, vb=vb: e.tensor_copy(
                    out=vst_v[vb][:, 8:16, 0:64],
                    in_=p_v[1].rearrange("p (j c) -> p j c", j=8)))
                P.op("dve", lambda e, vb=vb: e.memset(vst_v[vb][:, :, 64:65], 1.0),
                     inc=dv_a)
                dv_vst = dv_a.v
                rec_pv_free = dv_vst
                tok0 = tc * ATC + tt * 128
                for hp in range(DC):
                    P.op("pool", lambda e, hp=hp, tok0=tok0, vb=vb: e.dma_start(
                        out=v_sp[hp, tok0:tok0 + 128, :],
                        in_=a_vst[vb][:, hp * 130:(hp + 1) * 130]),
                        waits=([(dv_a, dv_vst)] if hp == 0 else []),
                        inc=sp_v, inc_n=16)
                rec_vst_free[vb] = sp_v.v
            rec_ln1_free = pe_a.v

        a_dv_end = dv_a.v
        a_pe_end = pe_a.v
        a_po_end = po_a.v
        a_spk_end = sp_k.v
        a_spv_end = sp_v.v

        _skipB = _LVL < 2
        p_sc = [PB[0][:, 0:1024], PB[1][:, 0:1024]]
        p_pv = [PB[2][:, 0:512], PB[2][:, 512:1024]]
        p_rbc = PB[3][0:64, 0:512]
        p_wo = PB[3][:, 512:1024]
        att_b = [att_v, att2_v]
        pt_slot = [s_ptb[0][:, 0:1024], s_ptb[0][:, 1024:2048], s_ptb[1][:, 0:1024]]

        rec_kh_free = [0, 0]
        rec_vh_free = [0, 0]
        rec_exp_done = [0, 0]
        rec_slot_pv = [0, 0, 0]
        rec_slot_exp = [0, 0, 0]
        rec_num_evac = [0, 0]
        rec_pwo = [0]
        rec_rbc = [0]
        dv_att_q = [0] * NQC
        spill_q = [0] * NQC

        def emit_wo_mm(ci, wo_qc, wo_do):
            wv = att_b[wo_qc % 2]
            w_waits = []
            if ci == 0:
                w_waits = [(dv_b, dv_att_q[wo_qc]), (dv_b, rec_pwo[0])]
            P.op("pe", lambda e, ci=ci, wo_do=wo_do, wv=wv: e.matmul(
                p_wo, wo_v[:, ci, wo_do * 128:(wo_do + 1) * 128], wv[:, ci, :],
                start=(ci == 0), stop=(ci == DC - 1)),
                waits=w_waits, inc=(pe_b if ci == DC - 1 else None))

        def emit_wo_evac(wo_qc, wo_do):
            pe_wo = pe_b.v
            e_waits = [(pe_b, pe_wo)]
            if wo_do == 0 and wo_qc >= 1:
                e_waits.append((sp_o, spill_q[wo_qc - 1]))
            P.op("dve", lambda e, wo_do=wo_do: e.tensor_scalar(
                out=x1_v[:, wo_do, :], in0=p_wo,
                scalar1=s_bias[:, 3 * DC + wo_do:3 * DC + wo_do + 1],
                scalar2=None, op0=ALU.add),
                waits=e_waits, inc=dv_b)
            rec_pwo[0] = dv_b.v

        def emit_spill(wo_qc):
            wqs = slice(wo_qc * 512, (wo_qc + 1) * 512)
            dv_o = dv_b.v
            P.op("pool", lambda e, wqs=wqs: e.dma_start(
                out=o_sp_v[:, :, wqs], in_=x1_v),
                waits=[(dv_b, dv_o)], inc=sp_o, inc_n=16)
            spill_q[wo_qc] = sp_o.v

        def emit_rbc(pc, j):
            P.op("pe", lambda e, j=j: e.matmul(
                p_rbc, s_ones[0:1, 0:64], s_recb[0:1, j * 512:(j + 1) * 512],
                start=True, stop=True),
                waits=[(dv_b, pc["dv_recb"] if j == 0 else 0),
                       (dv_b, rec_rbc[0])],
                inc=pe_b)
            pe_rbc = pe_b.v
            P.op("dve", lambda e, j=j: e.tensor_copy(
                out=s_rb[0:64, j * 512:(j + 1) * 512], in_=p_rbc),
                waits=[(pe_b, pe_rbc)], inc=dv_b)
            rec_rbc[0] = dv_b.v

        def emit_chain_tail(pc):
            emit_rbc(pc, 1)
            P.drain("dve")
            for j in range(2):
                P.op("dve", lambda e, j=j, pc=pc: e.tensor_mul(
                    pc["av"][64 * j:64 * (j + 1), pc["hp"], :],
                    s_num[0:64, j * 512:(j + 1) * 512],
                    s_rb[0:64, j * 512:(j + 1) * 512]))
            P.drain("dve", inc=dv_b)
            if pc["hp"] == DC - 1:
                dv_att_q[pc["qc"]] = dv_b.v
                if debug:
                    dqs = slice(pc["qc"] * 512, (pc["qc"] + 1) * 512)
                    P.op("pool", lambda e, dqs=dqs, dav=pc["av"]: e.dma_start(
                        out=dbg["attn"].rearrange("(c p) t -> p c t", p=128)[:, :, dqs],
                        in_=dav), waits=[(dv_b, dv_att_q[pc["qc"]])],
                        inc=st_d, inc_n=16)

        pend_wo = None
        pend_chain = None
        for qc in range(NQC if not _skipB else 0):
            qs = slice(qc * 512, (qc + 1) * 512)
            av = att_b[qc % 2]
            for hp in range(DC):
                kb = hp % 2
                P.op("sp", lambda e, kb=kb, hp=hp: e.dma_start(
                    out=s_kh[kb][:], in_=k_sp[hp]),
                    waits=[(sp_k, a_spk_end), (dv_a, a_dv_end), (po_a, a_po_end),
                           (pe_b, rec_kh_free[kb])],
                    inc=ld_kv, inc_n=16)
                P.op("sp", lambda e, kb=kb, hp=hp: e.dma_start(
                    out=vh_v[kb], in_=v_sp[hp].rearrange("(k p) c -> p k c", p=128)),
                    waits=[(sp_v, a_spv_end), (sp_k, a_spk_end), (pe_b, rec_vh_free[kb])],
                    inc=ld_kv, inc_n=16)
                ldkv_here = ld_kv.v

                def emit_scores(kt):
                    par = kt % 2
                    sl = kt % 3
                    ksl = slice(kt * 128, (kt + 1) * 128)
                    sc_waits = [(ld_kv, ldkv_here), (ac_b, rec_exp_done[par])]
                    P.op("pe", lambda e, ksl=ksl, par=par, kb=kb, hp=hp, qs=qs: e.matmul(
                        p_sc[par][:, 0:512], s_kh[kb][0:64, ksl], q_v[0:64, hp, qs],
                        start=True, stop=True, tile_position=(0, 0)),
                        waits=sc_waits)
                    P.op("pe", lambda e, ksl=ksl, par=par, kb=kb, hp=hp, qs=qs: e.matmul(
                        p_sc[par][:, 512:1024], s_kh[kb][64:128, ksl], q_v[64:128, hp, qs],
                        start=True, stop=True, tile_position=(64, 0)),
                        inc=pe_b)
                    pe_sc_here = pe_b.v
                    P.op("act", lambda e, par=par, sl=sl: e.activation(
                        out=pt_slot[sl], in_=p_sc[par][:, 0:1024], func=AF.Exp, scale=0.125),
                        waits=[(pe_b, pe_sc_here), (pe_b, rec_slot_pv[sl])],
                        inc=ac_b)
                    rec_exp_done[par] = ac_b.v
                    rec_slot_exp[sl] = ac_b.v

                def emit_pv(kt):
                    sl = kt % 3
                    pv0_waits = [(ac_b, rec_slot_exp[sl])]
                    pv1_waits = []
                    if kt == 0:
                        pv0_waits.append((dv_b, rec_num_evac[0]))
                        pv1_waits = [(dv_b, rec_num_evac[1])]
                    P.op("pe", lambda e, kt=kt, sl=sl, kb=kb: e.matmul(
                        p_pv[0][0:65, :], vh_v[kb][:, kt, 0:65],
                        pt_slot[sl][:, 0:512],
                        start=(kt == 0), stop=(kt == 31)),
                        waits=pv0_waits)
                    P.op("pe", lambda e, kt=kt, sl=sl, kb=kb: e.matmul(
                        p_pv[1][0:65, :], vh_v[kb][:, kt, 65:130],
                        pt_slot[sl][:, 512:1024],
                        start=(kt == 0), stop=(kt == 31)),
                        waits=pv1_waits, inc=pe_b)
                    rec_slot_pv[sl] = pe_b.v

                for kt in range(33):
                    if kt < 32:
                        emit_scores(kt)
                    if kt >= 1:
                        emit_pv(kt - 1)
                    if kt < 32:
                        if pend_chain is not None and kt == 9:
                            emit_rbc(pend_chain, 0)
                        if pend_chain is not None and kt == 11:
                            emit_chain_tail(pend_chain)
                            pend_chain = None
                        if pend_wo is not None and 16 <= kt < 24:
                            emit_wo_mm(kt - 16, pend_wo, hp)
                            if kt == 23:
                                emit_wo_evac(pend_wo, hp)
                                if hp == DC - 1:
                                    emit_spill(pend_wo)
                                    pend_wo = None
                rec_kh_free[kb] = pe_b.v
                rec_vh_free[kb] = pe_b.v
                pe_pv_end = pe_b.v
                P.op("dve", lambda e: e.tensor_copy(
                    out=s_num[0:65, 0:512], in_=p_pv[0][0:65, :]),
                    waits=[(pe_b, pe_pv_end)], inc=dv_b)
                rec_num_evac[0] = dv_b.v
                P.op("dve", lambda e: e.tensor_copy(
                    out=s_num[0:65, 512:1024], in_=p_pv[1][0:65, :]), inc=dv_b)
                rec_num_evac[1] = dv_b.v
                P.drain("dve")
                P.op("dve", lambda e: e.reciprocal(
                    out=s_rec[0:1, 0:1024], in_=s_num[64:65, 0:1024]))
                P.drain("dve")
                P.op("dve", lambda e: e.tensor_copy(out=s_recb[:], in_=s_rec[:]),
                     inc=dv_b)
                pend_chain = dict(dv_recb=dv_b.v, av=av, hp=hp, qc=qc)
            pend_wo = qc
            if qc == NQC - 1:
                emit_rbc(pend_chain, 0)
                emit_chain_tail(pend_chain)
                pend_chain = None
                for do in range(DC):
                    for ci in range(DC):
                        emit_wo_mm(ci, qc, do)
                    emit_wo_evac(qc, do)
                emit_spill(qc)

        b_spo_end = sp_o.v
        b_dv_end = dv_b.v
        b_pe_end = pe_b.v
        b_ac_end = ac_b.v

        _skipC = _LVL < 3
        p_sum1c = PB[0][0:1, 0:512]
        p_sum2c = PB[0][0:1, 512:1024]
        p_mb2 = PB[1][:, 0:512]
        p_rb2 = PB[1][:, 512:1024]
        p_lg = PB[2][:, 0:512]
        p_ct = PB[0][0:8, 512:640]
        p_cb = PB[0][:, 0:512]
        p_gl = [(PB[2][:, 0:512], PB[2][:, 512:1024]),
                (PB[1][:, 0:512], PB[1][:, 512:1024])]
        p_p2 = [PB[3][:, 0:512], PB[3][:, 512:1024]]

        rec_wf_free = [0, 0]
        rec_statc_free = 0
        rec_mbrb2_free = 0
        rec_gl_free = [0, 0]
        rec_pl_free = [0, 0]
        rec_sig_free = [0, 0]
        rec_lc_free = [0, 0]
        rec_ap_free = [0, 0]
        rec_ta_free = [0, 0]
        rec_p2_free = [0, 0]
        rec_cb_free = 0
        rec_cbs_ac = 0
        rec_x1_free = 0

        wf_loads = [(tc4, e_) for tc4 in range(NQC) for e_ in range(NE)]

        def emit_wf_load(idx):
            tc4, e_ = wf_loads[idx]
            wb = idx % 2
            waits = [(pe_c, rec_wf_free[wb])]
            if idx == 0:
                waits.append((pe_a, a_pe_end))
            if idx == 1:
                waits.append((pe_b, b_pe_end))
                waits.append((dv_b, b_dv_end))
            for wi, wt in ((0, w1gT), (1, w1lT), (2, w2T)):
                P.op("sp", lambda e, wb=wb, wi=wi, wt=wt, e_=e_: e.dma_start(
                    out=wf_v[wb][:, wi, :, :],
                    in_=wt[e_].rearrange("(c p) n -> p c n", p=128)),
                    waits=(waits if wi == 0 else []), inc=ld_wf, inc_n=16)
            return ld_wf.v

        if not _skipC:
            P.op("sp", lambda e: e.dma_start(out=s_sel[0:NE, :], in_=sel_in),
                 waits=[(pe_b, b_pe_end), (ac_b, b_ac_end)], inc=ld_x1, inc_n=16)
            P.op("sp", lambda e: e.dma_start(out=s_b2[0:NE, :], in_=b2_in),
                 inc=ld_x1, inc_n=16)
            sel_ld = ld_x1.v
            wf_ready = [0] * len(wf_loads)
            wf_ready[0] = emit_wf_load(0)
            wf_ready[1] = emit_wf_load(1)
            next_wf = [2]
        pend_mlp2 = None
        pe_mlp2_prev = [0]
        out_done = [0]
        rec_x1f = [0]
        x1_ld = [0] * NQC

        def emit_x1_load(tcl):
            lqs = slice(tcl * 512, (tcl + 1) * 512)
            P.op("sp", lambda e, lqs=lqs: e.dma_start(out=x1_v, in_=o_sp_v[:, :, lqs]),
                 waits=[(sp_o, b_spo_end), (dv_c, rec_x1f[0])],
                 inc=ld_x1, inc_n=16)
            ld_o = ld_x1.v
            P.op("pool", lambda e, lqs=lqs: e.dma_start(
                out=s_x1[:], in_=x_fm_v[:, :, lqs], accum_op=ALU.add),
                waits=[(ld_x1, ld_o), (pe_b, b_pe_end)], inc=ld_x1, inc_n=16)
            x1_ld[tcl] = ld_x1.v

        for tc4 in range(_CT4 if not _skipC else 0):
            qs = slice(tc4 * 512, (tc4 + 1) * 512)
            ct_s = slice(tc4 * 512, (tc4 + 1) * 512)
            if tc4 == 0:
                emit_x1_load(0)
            ldx1_here = x1_ld[tc4]
            if debug:
                P.op("pool", lambda e, qs=qs: e.dma_start(
                    out=dbg["x1"].rearrange("(c p) t -> p c t", p=128)[:, :, qs],
                    in_=x1_v), waits=[(ld_x1, ldx1_here)], inc=st_d, inc_n=16)
            P.op("dve", lambda e: e.tensor_copy(out=c_xb[:], in_=s_x1[:]),
                 waits=[(ld_x1, ldx1_here), (dv_b, b_dv_end)])
            P.drain("dve", inc=dv_c)
            dv_xbc = dv_c.v
            P.op("act", lambda e: e.activation(out=c_sq[:], in_=c_xb[:], func=AF.Square),
                 waits=[(dv_c, dv_xbc)], inc=ac_c)
            ac_sqc = ac_c.v
            for ci in range(DC):
                P.op("pe", lambda e, ci=ci: e.matmul(
                    p_sum1c, s_ones[:, 0:1], cxb_v[:, ci, :],
                    start=(ci == 0), stop=(ci == DC - 1)),
                    waits=([(dv_c, dv_xbc), (dv_c, rec_statc_free), (pe_b, b_pe_end),
                            (ac_c, rec_cbs_ac)] if ci == 0 else []),
                    inc=(pe_c if ci == DC - 1 else None))
            for ci in range(DC):
                P.op("pe", lambda e, ci=ci: e.matmul(
                    p_sum2c, s_ones[:, 0:1], csq_v[:, ci, :],
                    start=(ci == 0), stop=(ci == DC - 1)),
                    waits=([(ac_c, ac_sqc)] if ci == 0 else []),
                    inc=(pe_c if ci == DC - 1 else None))
            pe_statc = pe_c.v
            P.op("dve", lambda e: e.tensor_scalar_mul(st_m, p_sum1c, 1.0 / D),
                 waits=[(pe_c, pe_statc)])
            P.drain("dve")
            P.op("dve", lambda e: e.tensor_mul(st_m2, st_m, st_m))
            P.drain("dve")
            P.op("dve", lambda e: e.scalar_tensor_tensor(
                out=st_m2, in0=p_sum2c, scalar=1.0 / D, in1=st_m2,
                op0=ALU.mult, op1=ALU.subtract))
            P.drain("dve", inc=dv_c)
            dv_varc = dv_c.v
            P.op("act", lambda e: e.activation(out=st_ln, in_=st_m2, func=AF.Ln, bias=EPS),
                 waits=[(dv_c, dv_varc)])
            P.op("act", lambda e: e.drain())
            P.op("act", lambda e: e.activation(out=st_rstd, in_=st_ln, func=AF.Exp, scale=-0.5),
                 inc=ac_c)
            ac_rstdc = ac_c.v
            P.op("dve", lambda e: e.tensor_copy(out=s_statb[0:1, 0:512], in_=st_m),
                 waits=[(ac_c, ac_rstdc)])
            P.op("dve", lambda e: e.tensor_copy(out=s_statb[0:1, 512:1024], in_=st_rstd))
            P.drain("dve", inc=dv_c)
            dv_statbc = dv_c.v
            P.op("pe", lambda e: e.matmul(p_mb2, s_ones[0:1, :], s_statb[0:1, 0:512],
                                          start=True, stop=True),
                 waits=[(dv_c, dv_statbc), (dv_c, rec_mbrb2_free), (ac_c, rec_cbs_ac)])
            P.op("pe", lambda e: e.matmul(p_rb2, s_ones[0:1, :], s_statb[0:1, 512:1024],
                                          start=True, stop=True), inc=pe_c)
            pe_bc2 = pe_c.v
            for ci in range(DC):
                P.op("dve", lambda e, ci=ci: e.tensor_tensor(
                    out=acc_v[:, ci, :], in0=x1_v[:, ci, :], in1=p_mb2, op=ALU.subtract),
                    waits=([(pe_c, pe_bc2), (st_o, out_done[0])] if ci == 0 else []))
            P.drain("dve")
            for ci in range(DC):
                P.op("dve", lambda e, ci=ci: e.tensor_mul(
                    xc_v[:, ci, :], acc_v[:, ci, :], p_rb2),
                    waits=([(pe_b, b_pe_end)] if ci == 0 else []))
            P.drain("dve", inc=dv_c)
            dv_xc = dv_c.v
            P.op("dve", lambda e: e.tensor_tensor(
                out=c_acc[:], in0=s_x1[:], in1=c_xc[:], op=ALU.add))
            P.drain("dve", inc=dv_c)
            rec_x1f[0] = dv_c.v
            if tc4 + 1 < _CT4:
                emit_x1_load(tc4 + 1)
            P.op("act", lambda e: e.activation(out=c_sq[:], in_=c_xc[:], func=AF.Square),
                 waits=[(dv_c, dv_xc)], inc=ac_c)
            ac_sq2 = ac_c.v
            for ci in range(DC):
                P.op("pe", lambda e, ci=ci: e.matmul(
                    p_sum1c, s_ones[:, 0:1], csq_v[:, ci, :],
                    start=(ci == 0), stop=(ci == DC - 1)),
                    waits=([(ac_c, ac_sq2), (dv_c, dv_varc)] if ci == 0 else []),
                    inc=(pe_c if ci == DC - 1 else None))
            pe_rms = pe_c.v
            P.op("dve", lambda e: e.tensor_scalar_mul(st_m, p_sum1c, 1.0 / D),
                 waits=[(pe_c, pe_rms)])
            P.drain("dve", inc=dv_c)
            dv_rmsv = dv_c.v
            P.op("act", lambda e: e.activation(out=st_ln, in_=st_m, func=AF.Ln, bias=EPS),
                 waits=[(dv_c, dv_rmsv)])
            P.op("act", lambda e: e.drain())
            P.op("act", lambda e: e.activation(out=st_rstd, in_=st_ln, func=AF.Exp, scale=-0.5),
                 inc=ac_c)
            ac_rms = ac_c.v
            P.op("dve", lambda e: e.tensor_copy(out=s_statb[0:1, 512:1024], in_=st_rstd),
                 waits=[(ac_c, ac_rms)])
            P.drain("dve", inc=dv_c)
            dv_rmsb = dv_c.v
            P.op("pe", lambda e: e.matmul(p_rb2, s_ones[0:1, :], s_statb[0:1, 512:1024],
                                          start=True, stop=True),
                 waits=[(dv_c, dv_rmsb), (dv_c, dv_xc)], inc=pe_c)
            pe_rmsb = pe_c.v
            for ci in range(DC):
                P.op("dve", lambda e, ci=ci: e.scalar_tensor_tensor(
                    out=nrm_v[:, ci, :], in0=xc_v[:, ci, :], scalar=s_gamma[:, ci:ci + 1],
                    in1=p_rb2, op0=ALU.mult, op1=ALU.mult),
                    waits=([(pe_c, pe_rmsb)] if ci == 0 else []))
            P.drain("dve", inc=dv_c)
            dv_nrm = dv_c.v
            rec_mbrb2_free = dv_nrm
            if debug:
                P.op("pool", lambda e, qs=qs: e.dma_start(
                    out=dbg["normed"].rearrange("(c p) t -> p c t", p=128)[:, :, qs],
                    in_=nrm_v), waits=[(dv_c, dv_nrm)], inc=st_d, inc_n=16)

            pe_lg = [0] * 4
            for tt in range(4):
                tsl = slice(tt * 128, (tt + 1) * 128)
                for ci in range(DC):
                    P.op("pe", lambda e, ci=ci, tsl=tsl, tt=tt: e.matmul(
                        p_lg[:, tt * 64:tt * 64 + 8], nrm_v[:, ci, tsl], gw_v[:, ci, :],
                        start=(ci == 0), stop=False),
                        waits=([(dv_c, dv_nrm), (dv_c, rec_gl_free[0])] if (ci == 0 and tt == 0) else []))
                P.op("pe", lambda e, tt=tt: e.matmul(
                    p_lg[:, tt * 64:tt * 64 + 8], s_ones[0:1, 0:128],
                    s_gbrow[0:1, :], start=False, stop=True),
                    inc=pe_c)
                pe_lg[tt] = pe_c.v

            def gstep(build4, waits=(), inc=None):
                for tt in range(4):
                    P.op("dve", lambda e, tt=tt: build4(e, tt),
                         waits=(waits if tt == 0 else []),
                         inc=(inc if tt == 3 else None))
                P.drain("dve")
            g = gt_v
            gb_ = gtb_v
            gs = gsc_v
            gstep(lambda e, tt: e.tensor_copy(out=gb_[:, tt, 0:8],
                                              in_=p_lg[:, tt * 64:tt * 64 + 8]),
                  waits=[(pe_c, pe_lg[3])])
            gstep(lambda e, tt: e.tensor_copy(out=g[:, tt, 0:8], in_=gb_[:, tt, 0:8]))
            gstep(lambda e, tt: e.max(out=g[:, tt, 8:16], in_=g[:, tt, 0:8]))
            gstep(lambda e, tt: e.tensor_sub(gs[:, tt, 0:1], g[:, tt, 9:10], g[:, tt, 8:9]))
            for tt in range(4):
                P.op("dve", lambda e, tt=tt: e.tensor_scalar(
                    out=g[:, tt, 16:24], in0=g[:, tt, 0:8],
                    scalar1=g[:, tt, 8:9], scalar2=None, op0=ALU.is_equal))
            gstep(lambda e, tt: e.tensor_scalar(
                out=g[:, tt, 24:32], in0=g[:, tt, 0:8],
                scalar1=g[:, tt, 9:10], scalar2=None, op0=ALU.is_equal), inc=dv_c)
            dv_eq = dv_c.v
            for tt in range(4):
                P.op("act", lambda e, tt=tt: e.activation(
                    out=gs[:, tt, 4:5], in_=gs[:, tt, 0:1], func=AF.Sigmoid),
                    waits=([(dv_c, dv_eq)] if tt == 0 else []))
            for tt in range(4):
                P.op("act", lambda e, tt=tt: e.activation(
                    out=gs[:, tt, 3:4], in_=gs[:, tt, 0:1], func=AF.Sigmoid, scale=-1.0),
                    inc=(ac_c if tt == 3 else None))
            ac_z = ac_c.v
            P.wait("dve", ac_c, ac_z)
            gstep(lambda e, tt: e.tensor_tensor_scan(
                out=g[:, tt, 32:40], data0=g[:, tt, 16:24], data1=s_zero8[:],
                initial=0.0, op0=ALU.add, op1=ALU.add))
            gstep(lambda e, tt: e.tensor_scalar(
                out=g[:, tt, 40:48], in0=g[:, tt, 32:40],
                scalar1=1.0, scalar2=None, op0=ALU.is_equal))
            gstep(lambda e, tt: e.tensor_mul(g[:, tt, 48:56], g[:, tt, 40:48], g[:, tt, 16:24]))
            gstep(lambda e, tt: e.tensor_scalar(
                out=g[:, tt, 56:64], in0=g[:, tt, 48:56],
                scalar1=-1.0, scalar2=1.0, op0=ALU.mult, op1=ALU.add))
            gstep(lambda e, tt: e.tensor_mul(g[:, tt, 64:72], g[:, tt, 24:32], g[:, tt, 56:64]))
            gstep(lambda e, tt: e.tensor_tensor_scan(
                out=g[:, tt, 72:80], data0=g[:, tt, 64:72], data1=s_zero8[:],
                initial=0.0, op0=ALU.add, op1=ALU.add))
            gstep(lambda e, tt: e.tensor_scalar(
                out=g[:, tt, 80:88], in0=g[:, tt, 72:80],
                scalar1=1.0, scalar2=None, op0=ALU.is_equal))
            gstep(lambda e, tt: e.tensor_mul(g[:, tt, 88:96], g[:, tt, 80:88], g[:, tt, 64:72]))
            gstep(lambda e, tt: e.tensor_scalar(
                out=g[:, tt, 96:104], in0=g[:, tt, 48:56],
                scalar1=gs[:, tt, 3:4], scalar2=None, op0=ALU.mult))
            gstep(lambda e, tt: e.scalar_tensor_tensor(
                out=g[:, tt, 104:112], in0=g[:, tt, 88:96],
                scalar=gs[:, tt, 4:5], in1=g[:, tt, 96:104],
                op0=ALU.mult, op1=ALU.add), inc=dv_c)
            dv_comb = dv_c.v
            for tt in range(4):
                P.op("pe", lambda e, tt=tt: e.transpose(p_ct, g[:, tt, 104:112], s_id[:]),
                     waits=([(dv_c, dv_comb), (dv_c, rec_statc_free)] if tt == 0
                            else [(dv_c, dv_c.v)]),
                     inc=pe_c)
                pe_ct = pe_c.v
                P.op("dve", lambda e, tc4=tc4, tt=tt: e.tensor_copy(
                    out=c_ct[0:8, tc4 * 512 + tt * 128: tc4 * 512 + (tt + 1) * 128],
                    in_=p_ct), waits=[(pe_c, pe_ct)])
                P.drain("dve", inc=dv_c)
            dv_ct = dv_c.v
            rec_statc_free = dv_ct
            rec_gl_free[0] = max(rec_gl_free[0], dv_ct)
            if debug and tc4 == NQC - 1:
                P.op("pool", lambda e: e.dma_start(out=dbg["ct"], in_=c_ct[0:NE, :]),
                     waits=[(dv_c, dv_ct)], inc=st_d, inc_n=16)

            def emit_mlp2(pm):
                e_m, wb_m, po_m = pm["e_"], pm["wb"], pm["po_act"]
                for ht in range(DC):
                    hb = ht % 2
                    for ic in range(DC):
                        P.op("pe", lambda e, wb=wb_m, ht=ht, ic=ic, hb=hb, e_m=e_m: e.matmul(
                            p_p2[hb], wf_v[wb][:, 2, ic, ht * 128:(ht + 1) * 128],
                            act_v[:, ic, :],
                            start=(ic == 0), stop=(ic == DC - 1 and (e_m != NE - 1 or _CNE < NE))),
                            waits=([(po_c, po_m), (dv_c, rec_p2_free[hb])] if ic == 0 else []),
                            inc=(pe_c if (ic == DC - 1 and e_m != NE - 1) else None))
                    if e_m == NE - 1:
                        P.op("pe", lambda e, ht=ht, hb=hb, ct_s=ct_s: e.matmul(
                            p_p2[hb], s_b2[0:NE, ht * 128:(ht + 1) * 128], c_ct[0:NE, ct_s],
                            start=False, stop=True),
                            waits=[(ld_x1, sel_ld)] if ht == 0 else [], inc=pe_c)
                    pe_p2 = pe_c.v
                    P.op("dve", lambda e, ht=ht, hb=hb: e.tensor_add(
                        acc_v[:, ht, :], acc_v[:, ht, :], p_p2[hb]),
                        waits=[(pe_c, pe_p2)], inc=dv_c)
                    rec_p2_free[hb] = dv_c.v
                P.drain("dve")
                rec_wf_free[wb_m] = pe_c.v
                pe_mlp2_prev[0] = pe_c.v
                idx_m = tc4 * NE + e_m
                nonlocal_next = idx_m + 2
                if not _skipC and next_wf[0] < len(wf_loads) and next_wf[0] == nonlocal_next:
                    wf_ready[next_wf[0]] = emit_wf_load(next_wf[0])
                    next_wf[0] += 1

            for e_ in range(_CNE):
                idx = tc4 * NE + e_
                wb = idx % 2
                wf_here = wf_ready[idx]
                dv_t1as = [0] * DC
                dv_lc2s = [0] * DC
                pe_gs = [0] * DC
                pe_ls = [0] * DC

                def emit_gl(it):
                    par = it % 2
                    p_g, p_l = p_gl[par]
                    g_waits = [(ld_wf, wf_here), (dv_c, rec_gl_free[par])]
                    if par == 1 and it == 1:
                        g_waits.append((dv_c, rec_mbrb2_free))
                    for ci in range(DC):
                        P.op("pe", lambda e, it=it, ci=ci, p_g=p_g, wb=wb: e.matmul(
                            p_g, wf_v[wb][:, 0, ci, it * 128:(it + 1) * 128], nrm_v[:, ci, :],
                            start=(ci == 0), stop=(ci == DC - 1)),
                            waits=(g_waits if ci == 0 else []),
                            inc=(pe_c if ci == DC - 1 else None))
                    pe_gs[it] = pe_c.v
                    for ci in range(DC):
                        P.op("pe", lambda e, it=it, ci=ci, p_l=p_l, wb=wb: e.matmul(
                            p_l, wf_v[wb][:, 1, ci, it * 128:(it + 1) * 128], nrm_v[:, ci, :],
                            start=(ci == 0), stop=(ci == DC - 1)),
                            waits=([(ac_c, rec_pl_free[par])] if ci == 0 else []),
                            inc=(pe_c if ci == DC - 1 else None))
                    pe_ls[it] = pe_c.v

                emit_gl(0)
                emit_gl(1)
                P.op("pe", lambda e, e_=e_, ct_s=ct_s: e.matmul(
                    p_cb, s_sel[0:NE, e_ * 128:(e_ + 1) * 128], c_ct[0:NE, ct_s],
                    start=True, stop=True),
                    waits=[(dv_c, dv_ct), (ld_x1, sel_ld), (ac_c, rec_cbs_ac)],
                    inc=pe_c)
                pe_cb = pe_c.v
                P.op("act", lambda e: e.activation(out=c_cbs[:], in_=p_cb, func=AF.Copy),
                     waits=[(pe_c, pe_cb), (dv_c, rec_cb_free)], inc=ac_c)
                ac_cbs = ac_c.v
                rec_cbs_ac = ac_cbs
                if pend_mlp2 is not None:
                    emit_mlp2(pend_mlp2)
                    pend_mlp2 = None

                def emit_stage1(it):
                    par = it % 2
                    p_g, p_l = p_gl[par]
                    pe_g_here = pe_gs[it]
                    pe_l_here = pe_ls[it]
                    bofs = it * NE + e_
                    TA, SG, LC, AP_ = c_t1as[par], c_sigs[par], c_lcs[par], c_aps[par]
                    P.op("act", lambda e, bofs=bofs, p_g=p_g, SG=SG: e.activation(
                        out=SG[:], in_=p_g, func=AF.Sigmoid,
                        bias=s_b1gs[:, bofs:bofs + 1], scale=ALPHA),
                        waits=[(pe_c, pe_g_here), (dv_c, rec_sig_free[par])], inc=ac_c)
                    ac_sig = ac_c.v
                    P.op("act", lambda e, bofs=bofs, p_l=p_l, LC=LC: e.activation(
                        out=LC[:], in_=p_l, func=AF.Identity,
                        bias=s_b1l[:, bofs:bofs + 1]),
                        waits=[(pe_c, pe_l_here), (po_c, rec_lc_free[par])], inc=ac_c)
                    ac_lraw = ac_c.v
                    rec_pl_free[par] = ac_lraw
                    P.op("dve", lambda e, bofs=bofs, p_g=p_g, TA=TA, SG=SG: e.scalar_tensor_tensor(
                        out=TA[:], in0=p_g, scalar=s_b1g[:, bofs:bofs + 1],
                        in1=SG[:], op0=ALU.add, op1=ALU.mult),
                        waits=[(ac_c, ac_sig)], inc=dv_c)
                    dv_t1a = dv_c.v
                    dv_t1as[it] = dv_t1a
                    rec_gl_free[par] = dv_t1a
                    rec_sig_free[par] = dv_t1a
                    P.op("dve", lambda e, LC=LC: e.tensor_scalar(
                        out=LC[:], in0=LC[:], scalar1=-6.0, scalar2=8.0,
                        op0=ALU.max, op1=ALU.min),
                        waits=[(ac_c, ac_lraw)], inc=dv_c)
                    dv_lc2s[it] = dv_c.v

                def emit_stage2(it):
                    par = it % 2
                    TA, LC, AP_ = c_t1as[par], c_lcs[par], c_aps[par]
                    P.op("dve", lambda e, TA=TA, AP_=AP_: e.scalar_tensor_tensor(
                        out=AP_[:], in0=TA[:], scalar=CCAP, in1=c_cbs[:],
                        op0=ALU.min, op1=ALU.mult),
                        waits=[(ac_c, ac_cbs), (po_c, rec_ap_free[par])], inc=dv_c)
                    dv_z = dv_c.v
                    rec_ta_free[par] = dv_z
                    P.op("pool", lambda e, it=it, LC=LC, AP_=AP_: e.tensor_mul(
                        act_v[:, it, :], AP_[:], LC[:]),
                        waits=[(dv_c, dv_z), (dv_c, dv_lc2s[it]),
                               (pe_c, pe_mlp2_prev[0])], inc=po_c)
                    rec_lc_free[par] = po_c.v
                    rec_ap_free[par] = po_c.v

                for it in range(DC + 1):
                    if it < DC:
                        emit_stage1(it)
                        if it + 2 < DC:
                            emit_gl(it + 2)
                    if it >= 1:
                        P.drain("dve")
                        emit_stage2(it - 1)
                P.op("pool", lambda e: e.drain(), inc=po_c)
                rec_cb_free = dv_c.v
                pend_mlp2 = dict(e_=e_, wb=wb, po_act=po_c.v)
            emit_mlp2(pend_mlp2)
            pend_mlp2 = None
            pe_mlp2_end = pe_c.v
            if debug and tc4 == NQC - 1:
                P.op("pool", lambda e, qs=qs: e.dma_start(
                    out=dbg["acc"].rearrange("(c p) t -> p c t", p=128)[:, :, qs],
                    in_=acc_v), waits=[(dv_c, dv_c.v), (pe_c, pe_mlp2_end)], inc=st_d, inc_n=16)
            P.drain("dve", inc=dv_c)
            dv_out = dv_c.v
            P.op("sp", lambda e, qs=qs: e.dma_start(out=out_fm_v[:, :, qs], in_=acc_v),
                 waits=[(dv_c, dv_out)], inc=st_o, inc_n=16)
            out_done[0] = st_o.v

        P.wait("sp", st_o, st_o.v)
        P.wait("sp", sp_k, sp_k.v)
        P.wait("sp", sp_v, sp_v.v)
        P.wait("sp", sp_o, sp_o.v)
        P.wait("sp", ld_c, ld_c.v)
        P.wait("sp", ld_w, ld_w.v)
        P.wait("sp", ld_wf, ld_wf.v)
        if debug:
            P.wait("sp", st_d, st_d.v)

        for s in P.sems.values():
            s.h = es.enter_context(nc.semaphore(s.name))

        blk = es.enter_context(nc.Block())

        @blk.tensor
        def _(t):
            for fn in P.ops["pe"]:
                fn(t)

        @blk.scalar
        def _(sc):
            for fn in P.ops["act"]:
                fn(sc)

        @blk.vector
        def _(v):
            for fn in P.ops["dve"]:
                fn(v)

        @blk.gpsimd
        def _(g):
            for fn in P.ops["pool"]:
                fn(g)

        @blk.sync
        def _(sy):
            for fn in P.ops["sp"]:
                fn(sy)

    return nc


def _sel_const():
    sel = np.zeros((NE, NE * 128), dtype=BF)
    for e in range(NE):
        sel[e, e * 128:(e + 1) * 128] = 1
    return sel


def make_inputs(inputs):
    x = np.asarray(inputs["L0_attn_x"])
    wq = np.asarray(inputs["L0_attn_Wq"]); bq = np.asarray(inputs["L0_attn_bq"])
    wk = np.asarray(inputs["L0_attn_Wk"]); bk = np.asarray(inputs["L0_attn_bk"])
    wv = np.asarray(inputs["L0_attn_Wv"]); bv = np.asarray(inputs["L0_attn_bv"])
    wo = np.asarray(inputs["L0_attn_Wo"]); bo = np.asarray(inputs["L0_attn_bo"])
    gamma = np.asarray(inputs["L0_ffn_norm_weight"])
    gw = np.asarray(inputs["L0_ffn_gate_weight"])
    gb = np.asarray(inputs["L0_ffn_gate_bias"])
    w1 = np.asarray(inputs["L0_ffn_mlp1_weight"])
    b1 = np.asarray(inputs["L0_ffn_mlp1_bias"])
    w2 = np.asarray(inputs["L0_ffn_mlp2_weight"])
    b2 = np.asarray(inputs["L0_ffn_mlp2_bias"])

    def chunked(b):
        return np.ascontiguousarray(b.reshape(DC, 128).T)

    def bias_pe(b):
        return np.ascontiguousarray(
            b.T.reshape(DC, 128, NE).transpose(1, 0, 2).reshape(128, DC * NE))

    shared = dict(
        wqT=np.ascontiguousarray(wq.T), wkT=np.ascontiguousarray(wk.T),
        wvT=np.ascontiguousarray(wv.T), woT=np.ascontiguousarray(wo.T),
        b_qkvo=np.ascontiguousarray(np.concatenate(
            [chunked(bq), chunked(bk), chunked(bv), chunked(bo)], axis=1)),
        bv_row=np.ascontiguousarray(bv.reshape(1, D)),
        gamma_in=chunked(gamma),
        wg_in=np.ascontiguousarray(gw.T),
        gb_row=np.ascontiguousarray(gb.reshape(1, NE)),
        w1gT=np.ascontiguousarray(w1[:, 0::2, :].transpose(0, 2, 1)),
        w1lT=np.ascontiguousarray(w1[:, 1::2, :].transpose(0, 2, 1)),
        w2T=np.ascontiguousarray(w2.transpose(0, 2, 1)),
        b1g_in=bias_pe(b1[:, 0::2]),
        b1l_in=bias_pe(b1[:, 1::2]),
        b2_in=np.ascontiguousarray(b2),
        ones_bf_in=np.ones((128, 128), dtype=BF),
        ident_in=np.eye(128, dtype=np.float32),
        sel_in=_sel_const(),
    )
    in_maps = []
    for c in range(8):
        b, h = c // 2, c % 2
        perm = np.concatenate([np.arange(h * T, S), np.arange(0, h * T)])
        cos_t, sin_t = _rope_tables(perm)
        xc = np.ascontiguousarray(x[b].T[:, perm])
        m = dict(shared)
        m.update(x_fm=xc, cos_in=cos_t, sin_in=sin_t)
        in_maps.append(m)
    return in_maps


def kernel(**inputs):
    in_maps = make_inputs(inputs)
    nc = build_program(debug=False)
    res = run_bass_kernel_spmd(nc, in_maps, core_ids=list(range(8)))
    out = np.empty((4, S, D), dtype=np.float32)
    for c in range(8):
        b, h = c // 2, c % 2
        out[b, h * T:(h + 1) * T, :] = np.asarray(res.results[c]["out_fm"]).T
    return out


# version 15
# speedup vs baseline: 1.2021x; 1.0139x over previous
import contextlib
import numpy as np
import ml_dtypes
import concourse.bass as bass
import concourse.mybir as mybir
from concourse.bass_utils import run_bass_kernel_spmd

F32 = mybir.dt.float32
BF16 = mybir.dt.bfloat16
AF = mybir.ActivationFunctionType
ALU = mybir.AluOpType
BF = ml_dtypes.bfloat16

D = 1024
S = 4096
T = 2048
HD = 64
NE = 8
DC = 8
ATC = 512
NATC = S // ATC
NQC = 4
EPS = 1e-5
ALPHA = 1.702
CCAP = float(np.float32(7.0) * np.float32(
    1.0 / (1.0 + np.exp(-np.float64(np.float32(ALPHA) * np.float32(7.0))))))

ENGS = ("pe", "act", "dve", "pool", "sp")


class Sem:
    def __init__(self, name):
        self.name = name
        self.h = None
        self.v = 0


class Plan:
    def __init__(self):
        self.ops = {e: [] for e in ENGS}
        self.sems = {}

    def sem(self, name):
        if name not in self.sems:
            self.sems[name] = Sem(name)
        return self.sems[name]

    def op(self, eng, build, waits=(), inc=None, inc_n=1):
        if inc is not None:
            inc.v += inc_n

        def fn(engine, waits=tuple(waits), build=build, inc=inc, inc_n=inc_n):
            for s, v in waits:
                if v > 0:
                    engine.wait_ge(s.h, v)
            ins = build(engine)
            if inc is not None:
                ins.then_inc(inc.h, inc_n)
        self.ops[eng].append(fn)

    def wait(self, eng, sem, val):
        def fn(engine, sem=sem, val=val):
            engine.wait_ge(sem.h, val)
        self.ops[eng].append(fn)

    def drain(self, eng, inc=None, inc_n=1):
        self.op(eng, lambda e: e.drain(), inc=inc, inc_n=inc_n)


def _rope_tables(perm):
    dim = np.float32(HD)
    inv_freq = (np.float32(1.0) / (np.float32(10000.0) ** (
        np.arange(0, HD, 2, dtype=np.float32) / dim))).astype(np.float32)
    pos = np.arange(S, dtype=np.float32)
    freqs = (pos[:, None] * inv_freq[None, :]).astype(np.float32)
    freqs = np.repeat(freqs, 2, axis=-1)
    cos_t = np.cos(freqs.astype(np.float64)).astype(np.float32)
    sin_t = np.sin(freqs.astype(np.float64)).astype(np.float32)
    sign = np.ones(HD, dtype=np.float32)
    sign[0::2] = -1.0
    sin_s = sin_t * sign[None, :]
    cos2 = np.concatenate([cos_t[perm].T, cos_t[perm].T], axis=0)
    sin2 = np.concatenate([sin_s[perm].T, sin_s[perm].T], axis=0)
    sinp = sin2.copy()
    sinp[0::2], sinp[1::2] = sin2[1::2], sin2[0::2]
    return np.ascontiguousarray(cos2), np.ascontiguousarray(sinp)


class Arena:

    def __init__(self, nc, es, name, nbytes):
        self.nc = nc
        self.name = name
        self.nbytes = nbytes
        self.backing = es.enter_context(
            nc.sbuf_tensor(name, [128, nbytes], mybir.dt.uint8))
        self.base = nc.lookup_mloc(self.backing).addr

    def tensor(self, name, shape, dtype, off):
        per_part = int(np.prod(shape[1:])) * mybir.dt.size(dtype)
        assert off + per_part <= self.nbytes, (self.name, name, off, per_part)
        assert (self.base + off) % 32 == 0, (name, self.base, off)
        h = self.nc.virtual_sbuf_tensor(name, shape, dtype)
        self.nc._tensor(h.name, shape, dtype, type="SB")
        mloc = self.nc.lookup_mloc(h)
        mloc.allocated = True
        mloc.addr = self.base + off
        return h


def build_program(debug=False, stop_after=None, c_lim=None):
    _LVL = {"pre": 0, "A": 1, "B": 2, "C": 3, None: 3}[stop_after]
    _CT4, _CNE = c_lim if c_lim is not None else (NQC, NE)
    nc = bass.Bass("TRN2", target_bir_lowering=False, debug=False)

    x_fm = nc.dram_tensor("x_fm", [D, S], F32, kind="ExternalInput").ap()
    cos_in = nc.dram_tensor("cos_in", [128, S], F32, kind="ExternalInput").ap()
    sin_in = nc.dram_tensor("sin_in", [128, S], F32, kind="ExternalInput").ap()
    wqT = nc.dram_tensor("wqT", [D, D], F32, kind="ExternalInput").ap()
    wkT = nc.dram_tensor("wkT", [D, D], F32, kind="ExternalInput").ap()
    wvT = nc.dram_tensor("wvT", [D, D], F32, kind="ExternalInput").ap()
    woT = nc.dram_tensor("woT", [D, D], F32, kind="ExternalInput").ap()
    b_qkvo = nc.dram_tensor("b_qkvo", [128, 4 * DC], F32, kind="ExternalInput").ap()
    bv_row = nc.dram_tensor("bv_row", [1, D], F32, kind="ExternalInput").ap()
    gamma_in = nc.dram_tensor("gamma_in", [128, DC], F32, kind="ExternalInput").ap()
    wg_in = nc.dram_tensor("wg_in", [D, NE], BF16, kind="ExternalInput").ap()
    gb_row = nc.dram_tensor("gb_row", [1, NE], BF16, kind="ExternalInput").ap()
    w1gT = nc.dram_tensor("w1gT", [NE, D, D], BF16, kind="ExternalInput").ap()
    w1lT = nc.dram_tensor("w1lT", [NE, D, D], BF16, kind="ExternalInput").ap()
    w2T = nc.dram_tensor("w2T", [NE, D, D], BF16, kind="ExternalInput").ap()
    b1g_in = nc.dram_tensor("b1g_in", [128, DC * NE], BF16, kind="ExternalInput").ap()
    b1l_in = nc.dram_tensor("b1l_in", [128, DC * NE], BF16, kind="ExternalInput").ap()
    b2_in = nc.dram_tensor("b2_in", [NE, D], BF16, kind="ExternalInput").ap()
    ones_bf_in = nc.dram_tensor("ones_bf_in", [128, 128], BF16, kind="ExternalInput").ap()
    ident_in = nc.dram_tensor("ident_in", [128, 128], F32, kind="ExternalInput").ap()
    sel_in = nc.dram_tensor("sel_in", [NE, NE * 128], BF16, kind="ExternalInput").ap()

    out_fm = nc.dram_tensor("out_fm", [D, T], F32, kind="ExternalOutput").ap()

    k_sp = nc.dram_tensor("k_sp", [DC, 128, S], BF16, kind="Internal").ap()
    v_sp = nc.dram_tensor("v_sp", [DC, S, 130], BF16, kind="Internal").ap()
    o_sp = nc.dram_tensor("o_sp", [D, T], F32, kind="Internal").ap()

    dbg = {}
    if debug:
        dbg["ln1"] = nc.dram_tensor("d_ln1", [D, S], BF16, kind="ExternalOutput").ap()
        dbg["q"] = nc.dram_tensor("d_q", [D, T], BF16, kind="ExternalOutput").ap()
        dbg["attn"] = nc.dram_tensor("d_attn", [D, T], BF16, kind="ExternalOutput").ap()
        dbg["x1"] = nc.dram_tensor("d_x1", [D, T], F32, kind="ExternalOutput").ap()
        dbg["ct"] = nc.dram_tensor("d_ct", [NE, T], BF16, kind="ExternalOutput").ap()
        dbg["normed"] = nc.dram_tensor("d_normed", [D, T], BF16, kind="ExternalOutput").ap()
        dbg["acc"] = nc.dram_tensor("d_acc", [D, T], F32, kind="ExternalOutput").ap()

    es = contextlib.ExitStack()
    with es:
        KB = 1024

        def sbuf(name, shape, dt):
            return es.enter_context(nc.sbuf_tensor(name, shape, dt))

        s_ones = sbuf("s_ones", [128, 128], BF16)
        s_id = sbuf("s_id", [128, 128], F32)
        s_bias = sbuf("s_bias", [128, 4 * DC], F32)
        s_gamma = sbuf("s_gamma", [128, DC], F32)
        s_gw = sbuf("s_gw", [128, DC * NE], BF16)
        s_gbrow = sbuf("s_gbrow", [1, NE], BF16)
        s_b1g = sbuf("s_b1g", [128, DC * NE], F32)
        s_b1l = sbuf("s_b1l", [128, DC * NE], F32)
        s_b1gs = sbuf("s_b1gs", [128, DC * NE], F32)
        s_zero8 = sbuf("s_zero8", [128, NE], F32)
        s_eps = sbuf("s_eps", [1, 8], F32)
        s_x1 = sbuf("s_x1", [128, DC * 512], F32)

        U1 = Arena(nc, es, "U1", 96 * KB)
        s_wq = U1.tensor("s_wq", [128, DC * D], BF16, 0)
        s_wk = U1.tensor("s_wk", [128, DC * D], BF16, 16 * KB)
        s_wv = U1.tensor("s_wv", [128, DC * D], BF16, 32 * KB)
        s_wo = U1.tensor("s_wo", [128, DC * D], BF16, 48 * KB)
        s_cos = U1.tensor("s_cos", [128, S], F32, 64 * KB)
        s_sin = U1.tensor("s_sin", [128, S], F32, 80 * KB)
        s_wf = [U1.tensor(f"s_wf{i}", [128, 3 * DC * D], BF16, i * 48 * KB)
                for i in range(2)]
        s_num = U1.tensor("s_num", [128, 1024], F32, 64 * KB)
        s_att2 = U1.tensor("s_att2", [128, DC * 512], BF16, 68 * KB)

        U2 = Arena(nc, es, "U2", 32 * KB)
        s_q = U2.tensor("s_q", [128, DC * T], BF16, 0)
        c_acc = U2.tensor("c_acc", [128, DC * 512], F32, 0)
        c_xb = U2.tensor("c_xb", [128, DC * 512], BF16, 16 * KB)
        c_sq = U2.tensor("c_sq", [128, DC * 512], BF16, 24 * KB)

        U3 = Arena(nc, es, "U3", 8 * KB)
        a_ln1 = U3.tensor("a_ln1", [128, DC * ATC], BF16, 0)
        s_att = U3.tensor("s_att", [128, DC * 512], BF16, 0)
        s_sel = U3.tensor("s_sel", [NE, NE * 128], BF16, 0)
        c_sig1 = U3.tensor("c_sig1", [128, 512], F32, 2 * KB)
        c_t1a1 = U3.tensor("c_t1a1", [128, 512], F32, 4 * KB)
        c_lc1 = U3.tensor("c_lc1", [128, 512], F32, 6 * KB)

        U4 = Arena(nc, es, "U4", 16 * KB)
        a_xin = U4.tensor("a_xin", [128, DC * ATC], F32, 0)
        s_kh = [U4.tensor(f"s_kh{i}", [128, S], BF16, i * 8 * KB) for i in range(2)]
        c_xc = U4.tensor("c_xc", [128, DC * 512], BF16, 0)
        c_nrm = U4.tensor("c_nrm", [128, DC * 512], BF16, 8 * KB)

        U5 = Arena(nc, es, "U5", 16672)
        a_kst = U5.tensor("a_kst", [128, DC * ATC], BF16, 0)
        a_vst = [U5.tensor(f"a_vst{i}", [128, 16 * 65], BF16, 8 * KB + 2112 * i)
                 for i in range(2)]
        a_scrA = U5.tensor("a_scrA", [128, 512], F32, 12416)
        a_scrB = U5.tensor("a_scrB", [128, 512], F32, 14464)
        s_vh = [U5.tensor(f"s_vh{i}", [128, 32 * 130], BF16, i * 8320) for i in range(2)]
        c_act = U5.tensor("c_act", [128, DC * 512], BF16, 0)
        c_sig = U5.tensor("c_sig", [128, 512], F32, 8 * KB)
        c_t1a = U5.tensor("c_t1a", [128, 512], F32, 10 * KB)
        c_lc = U5.tensor("c_lc", [128, 512], F32, 12 * KB)
        c_ap = U5.tensor("c_ap", [128, 512], F32, 14 * KB)

        U6 = Arena(nc, es, "U6", 16 * KB)
        a_sq = U6.tensor("a_sq", [128, DC * 256], BF16, 0)
        s_stat = U6.tensor("s_stat", [1, 4 * 512], F32, 4 * KB)
        s_statb = U6.tensor("s_statb", [1, 2 * 512], BF16, 12 * KB)
        s_bvrow = U6.tensor("s_bvrow", [1, D], BF16, 14 * KB)
        s_ptb = [U6.tensor(f"s_ptb{i}", [128, 2048], BF16, i * 4 * KB) for i in range(2)]
        s_rec = U6.tensor("s_rec", [1, 2 * 512], F32, 8 * KB)
        s_recb = U6.tensor("s_recb", [1, 2 * 512], BF16, 12 * KB)
        s_rb = U6.tensor("s_rb", [64, 2 * 512], BF16, 14 * KB)
        c_ap1 = U6.tensor("c_ap1", [128, 512], F32, 4 * KB)
        c_gt = U6.tensor("c_gt", [128, 4 * 14 * NE], F32, 0)
        c_gtb = U6.tensor("c_gtb", [128, 4 * 2 * NE], BF16, 1792)
        c_gsc = U6.tensor("c_gsc", [128, 4 * 8], F32, 1920)
        s_b2 = U6.tensor("s_b2", [NE, D], BF16, 2 * KB)
        c_cbs = U6.tensor("c_cbs", [128, 512], F32, 14 * KB)

        U7 = Arena(nc, es, "U7", 4 * KB)
        a_xb = U7.tensor("a_xb", [128, DC * 256], BF16, 0)
        c_ct = U7.tensor("c_ct", [NE, T], BF16, 0)

        PBS = es.enter_context(nc.psum_tensor("pbs", [128, 2048], F32))
        PB2 = es.enter_context(nc.psum_tensor("pb2", [128, 1024], F32))
        PB3 = es.enter_context(nc.psum_tensor("pb3", [128, 1024], F32))
        PB = [PBS[:, 0:1024], PBS[:, 1024:2048], PB2[:], PB3[:]]

        P = Plan()
        SM = [P.sem(n) for n in (
            "ld_c", "ld_w", "ld_x", "ld_kv", "ld_wf", "ld_x1",
            "pe_a", "pe_b", "pe_c", "ac_a", "ac_b", "ac_c",
            "dv_a", "dv_b", "dv_c", "po_a", "po_c",
            "sp_k", "sp_v", "sp_o", "st_o", "st_d")]
        (ld_c, ld_w, ld_x, ld_kv, ld_wf, ld_x1,
         pe_a, pe_b, pe_c, ac_a, ac_b, ac_c,
         dv_a, dv_b, dv_c, po_a, po_c,
         sp_k, sp_v, sp_o, st_o, st_d) = SM

        def cview(t, n):
            return t[:].rearrange("p (c n) -> p c n", c=DC)

        xin_v = cview(a_xin, ATC)
        axb_v = cview(a_xb, 256)
        asq_v = cview(a_sq, 256)
        _h_ln1b = nc.virtual_sbuf_tensor("a_ln1b", [128, DC * ATC], BF16)
        nc._tensor(_h_ln1b.name, [128, DC * ATC], BF16, type="SB")
        _ml = nc.lookup_mloc(_h_ln1b)
        _ml.allocated = True
        _ml.addr = nc.lookup_mloc(s_x1).addr
        ln1_vs = [cview(a_ln1, ATC), cview(_h_ln1b, ATC)]
        ln1_v = ln1_vs[0]
        kst_v = cview(a_kst, ATC)
        vst_v = [t[:].rearrange("p (j c) -> p j c", j=16) for t in a_vst]
        q_v = s_q[:].rearrange("p (c n) -> p c n", c=DC)
        wq_v = cview(s_wq, D)
        wk_v = cview(s_wk, D)
        wv_v = cview(s_wv, D)
        wo_v = cview(s_wo, D)
        vh_v = [t[:].rearrange("p (k c) -> p k c", k=32) for t in s_vh]
        att_v = cview(s_att, 512)
        att2_v = cview(s_att2, 512)
        x1_v = cview(s_x1, 512)
        wf_v = [t[:].rearrange("p (w c n) -> p w c n", w=3, c=DC) for t in s_wf]
        c_sigs = [c_sig, c_sig1]
        c_t1as = [c_t1a, c_t1a1]
        c_lcs = [c_lc, c_lc1]
        c_aps = [c_ap, c_ap1]
        xc_v = cview(c_xc, 512)
        nrm_v = cview(c_nrm, 512)
        acc_v = cview(c_acc, 512)
        act_v = cview(c_act, 512)
        cxb_v = cview(c_xb, 512)
        csq_v = cview(c_sq, 512)
        gw_v = s_gw[:].rearrange("p (c n) -> p c n", c=DC)
        gt_v = c_gt[:].rearrange("p (i n) -> p i n", i=4)
        gtb_v = c_gtb[:].rearrange("p (i n) -> p i n", i=4)
        gsc_v = c_gsc[:].rearrange("p (i n) -> p i n", i=4)
        x_fm_v = x_fm.rearrange("(c p) t -> p c t", p=128)
        o_sp_v = o_sp.rearrange("(c p) t -> p c t", p=128)
        out_fm_v = out_fm.rearrange("(c p) t -> p c t", p=128)

        st_m = s_stat[0:1, 0:512]
        st_m2 = s_stat[0:1, 512:1024]
        st_ln = s_stat[0:1, 1024:1536]
        st_rstd = s_stat[0:1, 1536:2048]

        nc.const_aps.register(F32, float(EPS), s_eps[0:1, 0:1])
        P.op("pool", lambda e: e.dma_start(out=wq_v,
                                           in_=wqT.rearrange("(c p) n -> p c n", p=128)), inc=ld_w, inc_n=16)
        P.op("pool", lambda e: e.dma_start(out=wk_v,
                                           in_=wkT.rearrange("(c p) n -> p c n", p=128)), inc=ld_w, inc_n=16)
        P.op("pool", lambda e: e.dma_start(out=wv_v,
                                           in_=wvT.rearrange("(c p) n -> p c n", p=128)), inc=ld_w, inc_n=16)
        P.op("pool", lambda e: e.dma_start(out=wo_v,
                                           in_=woT.rearrange("(c p) n -> p c n", p=128)), inc=ld_w, inc_n=16)
        P.op("pool", lambda e: e.dma_start(out=s_bvrow[:], in_=bv_row), inc=ld_c, inc_n=16)
        P.op("pool", lambda e: e.dma_start(out=s_b1g[:], in_=b1g_in), inc=ld_c, inc_n=16)
        P.op("pool", lambda e: e.dma_start(out=s_b1l[:], in_=b1l_in), inc=ld_c, inc_n=16)
        P.op("sp", lambda e: e.dma_start(out=s_bias[:], in_=b_qkvo), inc=ld_c, inc_n=16)
        P.op("sp", lambda e: e.dma_start(out=s_gamma[:], in_=gamma_in), inc=ld_c, inc_n=16)
        P.op("sp", lambda e: e.dma_start(out=s_cos[:], in_=cos_in), inc=ld_c, inc_n=16)
        P.op("sp", lambda e: e.dma_start(out=s_sin[:], in_=sin_in), inc=ld_c, inc_n=16)
        P.op("sp", lambda e: e.dma_start(out=gw_v,
                                         in_=wg_in.rearrange("(c p) n -> p c n", p=128)), inc=ld_c, inc_n=16)
        P.op("sp", lambda e: e.dma_start(out=s_gbrow[:], in_=gb_row), inc=ld_c, inc_n=16)
        P.op("sp", lambda e: e.dma_start(out=s_ones[:], in_=ones_bf_in), inc=ld_c, inc_n=16)
        P.op("sp", lambda e: e.dma_start(out=s_id[:], in_=ident_in), inc=ld_c, inc_n=16)
        n_ldc = ld_c.v
        n_ldw = ld_w.v
        P.op("dve", lambda e: e.memset(s_zero8[:], 0.0), waits=[(ld_c, n_ldc)])
        P.op("dve", lambda e: e.memset(s_eps[:], float(EPS)))
        P.op("dve", lambda e: e.tensor_scalar_mul(s_b1gs[:], s_b1g[:], ALPHA))
        P.op("dve", lambda e: e.tensor_scalar_add(s_b1l[:], s_b1l[:], 1.0))
        P.drain("dve", inc=dv_a)
        P.wait("pe", ld_c, n_ldc)
        P.wait("pool", ld_c, n_ldc)
        ldw_q, ldw_k, ldw_v, ldw_o = 16, 32, 48, 64

        _skipA = _LVL < 1
        rec_xin_free = 0
        rec_ln1_free = [0, 0]
        rec_ln1_dump = [0, 0]
        rec_kst_free = 0
        rec_xb_a = 0
        rec_stats_free = 0
        rec_mbrb_ac = 0
        rec_pqk_free = [0, 0]
        rec_scrA_free = 0
        rec_pv_free = 0
        rec_vst_free = [0, 0]
        p_sum1 = PB[0][0:1, 0:512]
        p_sum2 = PB[0][0:1, 512:1024]
        p_mb = PB[1][:, 0:512]
        p_rb = PB[1][:, 512:1024]
        p_qks = [PB[2][:, 0:512], PB[3][:, 512:1024]]
        p_v = [PB[2][:, 512:1024], PB[3][:, 0:512]]

        for tc in range(NATC if not _skipA else 0):
            ts = slice(tc * ATC, (tc + 1) * ATC)
            own = tc < NATC // 2
            lv = ln1_vs[tc % 2]
            P.op("sp", lambda e, ts=ts: e.dma_start(out=xin_v, in_=x_fm_v[:, :, ts]),
                 waits=[(dv_a, rec_xin_free)], inc=ld_x, inc_n=16)
            ldx_here = ld_x.v
            for hf in range(2):
                hsl = slice(tc * ATC + hf * 256, tc * ATC + (hf + 1) * 256)
                lsl = slice(hf * 256, (hf + 1) * 256)
                P.op("dve", lambda e, lsl=lsl: e.tensor_copy(
                    out=a_xb[:], in_=a_xin[:].rearrange("p (c n) -> p c n", c=DC)[:, :, lsl]),
                    waits=[(ld_x, ldx_here), (pe_a, rec_xb_a)])
                P.drain("dve", inc=dv_a)
                dv_xb = dv_a.v
                P.op("act", lambda e: e.activation(out=a_sq[:], in_=a_xb[:], func=AF.Square),
                     waits=[(dv_a, dv_xb)], inc=ac_a)
                ac_sq = ac_a.v
                for ci in range(DC):
                    P.op("pe", lambda e, ci=ci, hf=hf: e.matmul(
                        p_sum1[:, hf * 256:(hf + 1) * 256], s_ones[:, 0:1], axb_v[:, ci, :],
                        start=(ci == 0), stop=(ci == DC - 1)),
                        waits=([(dv_a, dv_xb)] + ([(dv_a, rec_stats_free)] if hf == 0 else [])) if ci == 0 else [],
                        inc=(pe_a if ci == DC - 1 else None))
                for ci in range(DC):
                    P.op("pe", lambda e, ci=ci, hf=hf: e.matmul(
                        p_sum2[:, hf * 256:(hf + 1) * 256], s_ones[:, 0:1], asq_v[:, ci, :],
                        start=(ci == 0), stop=(ci == DC - 1)),
                        waits=([(ac_a, ac_sq)] if ci == 0 else []),
                        inc=(pe_a if ci == DC - 1 else None))
                rec_xb_a = pe_a.v
            pe_stats = pe_a.v
            P.op("dve", lambda e: e.tensor_scalar_mul(st_m, p_sum1, 1.0 / D),
                 waits=[(pe_a, pe_stats)])
            P.drain("dve")
            P.op("dve", lambda e: e.tensor_mul(st_m2, st_m, st_m))
            P.drain("dve")
            P.op("dve", lambda e: e.scalar_tensor_tensor(
                out=st_m2, in0=p_sum2, scalar=1.0 / D, in1=st_m2,
                op0=ALU.mult, op1=ALU.subtract))
            P.drain("dve", inc=dv_a)
            dv_var = dv_a.v
            rec_stats_free = dv_var
            P.op("act", lambda e: e.activation(out=st_ln, in_=st_m2, func=AF.Ln, bias=EPS),
                 waits=[(dv_a, dv_var)])
            P.op("act", lambda e: e.drain())
            P.op("act", lambda e: e.activation(out=st_rstd, in_=st_ln, func=AF.Exp, scale=-0.5),
                 inc=ac_a)
            ac_rstd = ac_a.v
            P.op("dve", lambda e: e.tensor_copy(out=s_statb[0:1, 0:512], in_=st_m),
                 waits=[(ac_a, ac_rstd)])
            P.op("dve", lambda e: e.tensor_copy(out=s_statb[0:1, 512:1024], in_=st_rstd))
            P.drain("dve", inc=dv_a)
            dv_statb = dv_a.v
            P.op("pe", lambda e: e.matmul(p_mb, s_ones[0:1, :], s_statb[0:1, 0:512],
                                          start=True, stop=True),
                 waits=[(dv_a, dv_statb), (ac_a, rec_mbrb_ac)])
            P.op("pe", lambda e: e.matmul(p_rb, s_ones[0:1, :], s_statb[0:1, 512:1024],
                                          start=True, stop=True), inc=pe_a)
            pe_bc = pe_a.v
            P.op("act", lambda e: e.activation(out=a_scrA[:], in_=p_mb, func=AF.Copy),
                 waits=[(pe_a, pe_bc), (po_a, rec_scrA_free)])
            P.op("act", lambda e: e.activation(out=a_scrB[:], in_=p_rb, func=AF.Copy),
                 inc=ac_a)
            ac_mbrb = ac_a.v
            rec_mbrb_ac = ac_mbrb
            for ci in range(4):
                P.op("pool", lambda e, ci=ci: e.tensor_tensor(
                    out=xin_v[:, ci, :], in0=xin_v[:, ci, :], in1=a_scrA[:],
                    op=ALU.subtract),
                    waits=([(ac_a, ac_mbrb)] if ci == 0 else []))
            _lnw = [(pe_a, rec_ln1_free[tc % 2])]
            if debug:
                _lnw.append((st_d, rec_ln1_dump[tc % 2]))
            for ci in range(4):
                P.op("pool", lambda e, ci=ci, lv=lv: e.tensor_mul(
                    lv[:, ci, :], xin_v[:, ci, :], a_scrB[:]),
                    waits=(_lnw if ci == 0 else []),
                    inc=(po_a if ci == 3 else None))
            po_ln1 = po_a.v
            for ci in range(4, DC):
                P.op("dve", lambda e, ci=ci: e.tensor_tensor(
                    out=xin_v[:, ci, :], in0=xin_v[:, ci, :], in1=a_scrA[:],
                    op=ALU.subtract),
                    waits=(([(ac_a, ac_mbrb)] + _lnw) if ci == 4 else []))
            P.drain("dve")
            for ci in range(4, DC):
                P.op("dve", lambda e, ci=ci, lv=lv: e.tensor_mul(
                    lv[:, ci, :], xin_v[:, ci, :], a_scrB[:]))
            P.wait("dve", po_a, po_ln1)
            P.drain("dve", inc=dv_a)
            dv_ln1 = dv_a.v
            rec_xin_free = dv_ln1
            if debug:
                P.op("pool", lambda e, ts=ts: e.dma_start(
                    out=dbg["ln1"].rearrange("(c p) t -> p c t", p=128)[:, :, ts],
                    in_=lv), waits=[(dv_a, dv_ln1)], inc=st_d, inc_n=16)
                rec_ln1_dump[tc % 2] = st_d.v

            for which in (("q",) if own else ()) + ("k",):
                wsel = wq_v if which == "q" else wk_v
                boff = 0 if which == "q" else DC
                for do in range(DC):
                    par = do % 2
                    p_qk = p_qks[par]
                    for ci in range(DC):
                        _pw = ([(dv_a, dv_ln1), (dv_a, rec_pqk_free[par])]
                               if ci == 0 else [])
                        if tc == 0 and do == 0 and ci == 0:
                            _pw = _pw + [(ld_w, ldw_q if which == "q" else ldw_k)]
                        P.op("pe", lambda e, wsel=wsel, do=do, ci=ci, p_qk=p_qk, lv=lv: e.matmul(
                            p_qk, wsel[:, ci, do * 128:(do + 1) * 128],
                            lv[:, ci, :],
                            start=(ci == 0), stop=(ci == DC - 1)),
                            waits=_pw,
                            inc=(pe_a if ci == DC - 1 else None))
                    pe_proj = pe_a.v
                    if which == "q":
                        dst = q_v[:, do, ts]
                    else:
                        dst = kst_v[:, do, :]
                    bslice = s_bias[:, boff + do:boff + do + 1]
                    dwaits = [(pe_a, pe_proj)]
                    if which == "k" and do == 0:
                        dwaits.append((sp_k, rec_kst_free))
                    P.op("dve", lambda e, dst=dst, bslice=bslice, ts=ts, p_qk=p_qk: e.scalar_tensor_tensor(
                        out=dst, in0=p_qk, scalar=bslice, in1=s_cos[:, ts],
                        op0=ALU.add, op1=ALU.mult), waits=dwaits)
                    P.op("dve", lambda e, bslice=bslice, ts=ts, p_qk=p_qk: e.scalar_tensor_tensor(
                        out=a_scrB[:], in0=p_qk, scalar=bslice, in1=s_sin[:, ts],
                        op0=ALU.add, op1=ALU.mult), inc=dv_a)
                    rec_pqk_free[par] = dv_a.v
                    P.drain("dve")
                    mask = [i ^ 1 for i in range(32)]
                    P.op("dve", lambda e, mask=mask: e.stream_shuffle(
                        out=a_scrA[:], in_=a_scrB[:], mask=mask),
                        waits=[(po_a, rec_scrA_free)], inc=dv_a)
                    dv_shf = dv_a.v
                    P.op("pool", lambda e, dst=dst: e.tensor_add(dst, dst, a_scrA[:]),
                         waits=[(dv_a, dv_shf)], inc=po_a)
                    rec_scrA_free = po_a.v
                po_rope = po_a.v
                if which == "k":
                    for hp in range(DC):
                        P.op("pool", lambda e, hp=hp, ts=ts: e.dma_start(
                            out=k_sp[hp, :, ts], in_=kst_v[:, hp, :]),
                            inc=sp_k, inc_n=16)
                    rec_kst_free = sp_k.v
                elif debug:
                    P.op("pool", lambda e, ts=ts: e.dma_start(
                        out=dbg["q"].rearrange("(c p) t -> p c t", p=128)[:, :, ts],
                        in_=q_v[:, :, ts]), inc=st_d, inc_n=16)

            for tt in range(ATC // 128):
                vb = tt % 2
                tsl = slice(tt * 128, (tt + 1) * 128)
                for nv in range(2):
                    for ci in range(DC):
                        _vw = ([(dv_a, rec_pv_free)] if (ci == 0 and nv == 0) else [])
                        if tc == 0 and tt == 0 and nv == 0 and ci == 0:
                            _vw = _vw + [(ld_w, ldw_v)]
                        P.op("pe", lambda e, nv=nv, ci=ci, tsl=tsl, lv=lv: e.matmul(
                            p_v[nv], lv[:, ci, tsl],
                            wv_v[:, ci, nv * 512:(nv + 1) * 512],
                            start=(ci == 0), stop=False),
                            waits=_vw)
                    P.op("pe", lambda e, nv=nv: e.matmul(
                        p_v[nv], s_ones[0:1, 0:128], s_bvrow[0:1, nv * 512:(nv + 1) * 512],
                        start=False, stop=True), inc=pe_a)
                pe_vp = pe_a.v
                P.op("dve", lambda e, vb=vb: e.tensor_copy(
                    out=vst_v[vb][:, 0:8, 0:64],
                    in_=p_v[0].rearrange("p (j c) -> p j c", j=8)),
                    waits=[(pe_a, pe_vp), (sp_v, rec_vst_free[vb])])
                P.op("dve", lambda e, vb=vb: e.tensor_copy(
                    out=vst_v[vb][:, 8:16, 0:64],
                    in_=p_v[1].rearrange("p (j c) -> p j c", j=8)))
                P.op("dve", lambda e, vb=vb: e.memset(vst_v[vb][:, :, 64:65], 1.0),
                     inc=dv_a)
                dv_vst = dv_a.v
                rec_pv_free = dv_vst
                tok0 = tc * ATC + tt * 128
                for hp in range(DC):
                    P.op("pool", lambda e, hp=hp, tok0=tok0, vb=vb: e.dma_start(
                        out=v_sp[hp, tok0:tok0 + 128, :],
                        in_=a_vst[vb][:, hp * 130:(hp + 1) * 130]),
                        waits=([(dv_a, dv_vst)] if hp == 0 else []),
                        inc=sp_v, inc_n=16)
                rec_vst_free[vb] = sp_v.v
            rec_ln1_free[tc % 2] = pe_a.v

        a_dv_end = dv_a.v
        a_pe_end = pe_a.v
        a_po_end = po_a.v
        a_spk_end = sp_k.v
        a_spv_end = sp_v.v

        rec_wf_free = [0, 0]
        wf_loads = [(tc4, e_) for tc4 in range(NQC) for e_ in range(NE)]
        wf_ready = [0] * len(wf_loads)

        def emit_wf_load(idx):
            tc4, e_ = wf_loads[idx]
            wb = idx % 2
            waits = [(pe_c, rec_wf_free[wb])]
            if idx == 0:
                waits.append((pe_a, a_pe_end))
            if idx == 1:
                waits.append((pe_b, b_pe_end))
                waits.append((dv_b, b_dv_end))
            for wi, wt in ((0, w1gT), (1, w1lT), (2, w2T)):
                P.op("sp", lambda e, wb=wb, wi=wi, wt=wt, e_=e_: e.dma_start(
                    out=wf_v[wb][:, wi, :, :],
                    in_=wt[e_].rearrange("(c p) n -> p c n", p=128)),
                    waits=(waits if wi == 0 else []), inc=ld_wf, inc_n=16)
            return ld_wf.v

        _skipB = _LVL < 2
        p_sc = [PB[0][:, 0:1024], PB[1][:, 0:1024]]
        p_pv = [PB[2][:, 0:512], PB[2][:, 512:1024]]
        p_rbc = PB[3][0:64, 0:512]
        p_wo = PB[3][:, 512:1024]
        att_b = [att_v, att2_v]
        pt_slot = [s_ptb[0][:, 0:1024], s_ptb[0][:, 1024:2048], s_ptb[1][:, 0:1024]]

        rec_kh_free = [0, 0]
        rec_vh_free = [0, 0]
        rec_exp_done = [0, 0]
        rec_slot_pv = [0, 0, 0]
        rec_slot_exp = [0, 0, 0]
        rec_num_evac = [0, 0]
        rec_pwo = [0]
        rec_rbc = [0]
        dv_att_q = [0] * NQC
        spill_q = [0] * NQC

        def emit_wo_mm(ci, wo_qc, wo_do):
            wv = att_b[wo_qc % 2]
            w_waits = []
            if ci == 0:
                w_waits = [(dv_b, dv_att_q[wo_qc]), (dv_b, rec_pwo[0])]
                if wo_qc == 0 and wo_do == 0:
                    w_waits.append((ld_w, ldw_o))
            P.op("pe", lambda e, ci=ci, wo_do=wo_do, wv=wv: e.matmul(
                p_wo, wo_v[:, ci, wo_do * 128:(wo_do + 1) * 128], wv[:, ci, :],
                start=(ci == 0), stop=(ci == DC - 1)),
                waits=w_waits, inc=(pe_b if ci == DC - 1 else None))

        def emit_wo_evac(wo_qc, wo_do):
            pe_wo = pe_b.v
            e_waits = [(pe_b, pe_wo)]
            if wo_do == 0 and wo_qc >= 1:
                e_waits.append((sp_o, spill_q[wo_qc - 1]))
            P.op("dve", lambda e, wo_do=wo_do: e.tensor_scalar(
                out=x1_v[:, wo_do, :], in0=p_wo,
                scalar1=s_bias[:, 3 * DC + wo_do:3 * DC + wo_do + 1],
                scalar2=None, op0=ALU.add),
                waits=e_waits, inc=dv_b)
            rec_pwo[0] = dv_b.v

        def emit_spill(wo_qc):
            wqs = slice(wo_qc * 512, (wo_qc + 1) * 512)
            dv_o = dv_b.v
            P.op("pool", lambda e, wqs=wqs: e.dma_start(
                out=o_sp_v[:, :, wqs], in_=x1_v),
                waits=[(dv_b, dv_o)], inc=sp_o, inc_n=16)
            spill_q[wo_qc] = sp_o.v

        def emit_rbc(pc, j):
            P.op("pe", lambda e, j=j: e.matmul(
                p_rbc, s_ones[0:1, 0:64], s_recb[0:1, j * 512:(j + 1) * 512],
                start=True, stop=True),
                waits=[(dv_b, pc["dv_recb"] if j == 0 else 0),
                       (dv_b, rec_rbc[0])],
                inc=pe_b)
            pe_rbc = pe_b.v
            P.op("dve", lambda e, j=j: e.tensor_copy(
                out=s_rb[0:64, j * 512:(j + 1) * 512], in_=p_rbc),
                waits=[(pe_b, pe_rbc)], inc=dv_b)
            rec_rbc[0] = dv_b.v

        def emit_chain_tail(pc):
            emit_rbc(pc, 1)
            P.drain("dve")
            for j in range(2):
                P.op("dve", lambda e, j=j, pc=pc: e.tensor_mul(
                    pc["av"][64 * j:64 * (j + 1), pc["hp"], :],
                    s_num[0:64, j * 512:(j + 1) * 512],
                    s_rb[0:64, j * 512:(j + 1) * 512]))
            P.drain("dve", inc=dv_b)
            if pc["hp"] == DC - 1:
                dv_att_q[pc["qc"]] = dv_b.v
                if debug:
                    dqs = slice(pc["qc"] * 512, (pc["qc"] + 1) * 512)
                    P.op("pool", lambda e, dqs=dqs, dav=pc["av"]: e.dma_start(
                        out=dbg["attn"].rearrange("(c p) t -> p c t", p=128)[:, :, dqs],
                        in_=dav), waits=[(dv_b, dv_att_q[pc["qc"]])],
                        inc=st_d, inc_n=16)

        pend_wo = None
        pend_chain = None
        for qc in range(NQC if not _skipB else 0):
            qs = slice(qc * 512, (qc + 1) * 512)
            av = att_b[qc % 2]
            for hp in range(DC):
                kb = hp % 2
                P.op("sp", lambda e, kb=kb, hp=hp: e.dma_start(
                    out=s_kh[kb][:], in_=k_sp[hp]),
                    waits=[(sp_k, a_spk_end), (dv_a, a_dv_end), (po_a, a_po_end),
                           (pe_b, rec_kh_free[kb])],
                    inc=ld_kv, inc_n=16)
                P.op("sp", lambda e, kb=kb, hp=hp: e.dma_start(
                    out=vh_v[kb], in_=v_sp[hp].rearrange("(k p) c -> p k c", p=128)),
                    waits=[(sp_v, a_spv_end), (sp_k, a_spk_end), (pe_b, rec_vh_free[kb])],
                    inc=ld_kv, inc_n=16)
                ldkv_here = ld_kv.v
                if _LVL >= 3 and qc == 2 and hp == 0:
                    wf_ready[0] = emit_wf_load(0)

                def emit_scores(kt):
                    par = kt % 2
                    sl = kt % 3
                    ksl = slice(kt * 128, (kt + 1) * 128)
                    sc_waits = [(ld_kv, ldkv_here), (ac_b, rec_exp_done[par])]
                    P.op("pe", lambda e, ksl=ksl, par=par, kb=kb, hp=hp, qs=qs: e.matmul(
                        p_sc[par][:, 0:512], s_kh[kb][0:64, ksl], q_v[0:64, hp, qs],
                        start=True, stop=True, tile_position=(0, 0)),
                        waits=sc_waits)
                    P.op("pe", lambda e, ksl=ksl, par=par, kb=kb, hp=hp, qs=qs: e.matmul(
                        p_sc[par][:, 512:1024], s_kh[kb][64:128, ksl], q_v[64:128, hp, qs],
                        start=True, stop=True, tile_position=(64, 0)),
                        inc=pe_b)
                    pe_sc_here = pe_b.v
                    P.op("act", lambda e, par=par, sl=sl: e.activation(
                        out=pt_slot[sl], in_=p_sc[par][:, 0:1024], func=AF.Exp, scale=0.125),
                        waits=[(pe_b, pe_sc_here), (pe_b, rec_slot_pv[sl])],
                        inc=ac_b)
                    rec_exp_done[par] = ac_b.v
                    rec_slot_exp[sl] = ac_b.v

                def emit_pv(kt):
                    sl = kt % 3
                    pv0_waits = [(ac_b, rec_slot_exp[sl])]
                    pv1_waits = []
                    if kt == 0:
                        pv0_waits.append((dv_b, rec_num_evac[0]))
                        pv1_waits = [(dv_b, rec_num_evac[1])]
                    P.op("pe", lambda e, kt=kt, sl=sl, kb=kb: e.matmul(
                        p_pv[0][0:65, :], vh_v[kb][:, kt, 0:65],
                        pt_slot[sl][:, 0:512],
                        start=(kt == 0), stop=(kt == 31)),
                        waits=pv0_waits)
                    P.op("pe", lambda e, kt=kt, sl=sl, kb=kb: e.matmul(
                        p_pv[1][0:65, :], vh_v[kb][:, kt, 65:130],
                        pt_slot[sl][:, 512:1024],
                        start=(kt == 0), stop=(kt == 31)),
                        waits=pv1_waits, inc=pe_b)
                    rec_slot_pv[sl] = pe_b.v

                for kt in range(33):
                    if kt < 32:
                        emit_scores(kt)
                    if kt >= 1:
                        emit_pv(kt - 1)
                    if kt < 32:
                        if pend_chain is not None and kt == 9:
                            emit_rbc(pend_chain, 0)
                        if pend_chain is not None and kt == 11:
                            emit_chain_tail(pend_chain)
                            pend_chain = None
                        if pend_wo is not None and 16 <= kt < 24:
                            emit_wo_mm(kt - 16, pend_wo, hp)
                            if kt == 23:
                                emit_wo_evac(pend_wo, hp)
                                if hp == DC - 1:
                                    emit_spill(pend_wo)
                                    pend_wo = None
                rec_kh_free[kb] = pe_b.v
                rec_vh_free[kb] = pe_b.v
                pe_pv_end = pe_b.v
                P.op("dve", lambda e: e.tensor_copy(
                    out=s_num[0:65, 0:512], in_=p_pv[0][0:65, :]),
                    waits=[(pe_b, pe_pv_end)], inc=dv_b)
                rec_num_evac[0] = dv_b.v
                P.op("dve", lambda e: e.tensor_copy(
                    out=s_num[0:65, 512:1024], in_=p_pv[1][0:65, :]), inc=dv_b)
                rec_num_evac[1] = dv_b.v
                P.drain("dve")
                P.op("dve", lambda e: e.reciprocal(
                    out=s_rec[0:1, 0:1024], in_=s_num[64:65, 0:1024]))
                P.drain("dve")
                P.op("dve", lambda e: e.tensor_copy(out=s_recb[:], in_=s_rec[:]),
                     inc=dv_b)
                pend_chain = dict(dv_recb=dv_b.v, av=av, hp=hp, qc=qc)
            pend_wo = qc
            if qc == NQC - 1:
                emit_rbc(pend_chain, 0)
                emit_chain_tail(pend_chain)
                pend_chain = None
                for do in range(DC):
                    for ci in range(DC):
                        emit_wo_mm(ci, qc, do)
                    emit_wo_evac(qc, do)
                emit_spill(qc)

        b_spo_end = sp_o.v
        b_dv_end = dv_b.v
        b_pe_end = pe_b.v
        b_ac_end = ac_b.v

        _skipC = _LVL < 3
        p_sum1c = PB[0][0:1, 0:512]
        p_sum2c = PB[0][0:1, 512:1024]
        p_mb2 = PB[1][:, 0:512]
        p_rb2 = PB[1][:, 512:1024]
        p_lg = PB[2][:, 0:512]
        p_ct = PB[0][0:8, 512:640]
        p_cb = PB[0][:, 0:512]
        p_gl = [(PB[2][:, 0:512], PB[2][:, 512:1024]),
                (PB[1][:, 0:512], PB[1][:, 512:1024])]
        p_p2 = [PB[3][:, 0:512], PB[3][:, 512:1024]]

        rec_statc_free = 0
        rec_mbrb2_free = 0
        rec_gl_free = [0, 0]
        rec_pl_free = [0, 0]
        rec_sig_free = [0, 0]
        rec_lc_free = [0, 0]
        rec_ap_free = [0, 0]
        rec_ta_free = [0, 0]
        rec_p2_free = [0, 0]
        rec_cb_free = 0
        rec_cbs_ac = 0
        rec_x1_free = 0

        if not _skipC:
            P.op("sp", lambda e: e.dma_start(out=s_sel[0:NE, :], in_=sel_in),
                 waits=[(pe_b, b_pe_end), (ac_b, b_ac_end)], inc=ld_x1, inc_n=16)
            P.op("sp", lambda e: e.dma_start(out=s_b2[0:NE, :], in_=b2_in),
                 inc=ld_x1, inc_n=16)
            sel_ld = ld_x1.v
            if wf_ready[0] == 0:
                wf_ready[0] = emit_wf_load(0)
            wf_ready[1] = emit_wf_load(1)
            next_wf = [2]
        pend_mlp2 = None
        pe_mlp2_prev = [0]
        out_done = [0]
        out_done_ci = [0] * DC
        rec_x1f = [0]
        x1_ld = [0] * NQC

        def emit_x1_load(tcl):
            lqs = slice(tcl * 512, (tcl + 1) * 512)
            P.op("sp", lambda e, lqs=lqs: e.dma_start(out=x1_v, in_=o_sp_v[:, :, lqs]),
                 waits=[(sp_o, b_spo_end), (dv_c, rec_x1f[0])],
                 inc=ld_x1, inc_n=16)
            ld_o = ld_x1.v
            P.op("pool", lambda e, lqs=lqs: e.dma_start(
                out=s_x1[:], in_=x_fm_v[:, :, lqs], accum_op=ALU.add),
                waits=[(ld_x1, ld_o), (pe_b, b_pe_end)], inc=ld_x1, inc_n=16)
            x1_ld[tcl] = ld_x1.v

        for tc4 in range(_CT4 if not _skipC else 0):
            qs = slice(tc4 * 512, (tc4 + 1) * 512)
            ct_s = slice(tc4 * 512, (tc4 + 1) * 512)
            if tc4 == 0:
                emit_x1_load(0)
            ldx1_here = x1_ld[tc4]
            if debug:
                P.op("pool", lambda e, qs=qs: e.dma_start(
                    out=dbg["x1"].rearrange("(c p) t -> p c t", p=128)[:, :, qs],
                    in_=x1_v), waits=[(ld_x1, ldx1_here)], inc=st_d, inc_n=16)
            P.op("dve", lambda e: e.tensor_copy(out=c_xb[:], in_=s_x1[:]),
                 waits=[(ld_x1, ldx1_here), (dv_b, b_dv_end)])
            P.drain("dve", inc=dv_c)
            dv_xbc = dv_c.v
            P.op("act", lambda e: e.activation(out=c_sq[:], in_=c_xb[:], func=AF.Square),
                 waits=[(dv_c, dv_xbc)], inc=ac_c)
            ac_sqc = ac_c.v
            for ci in range(DC):
                P.op("pe", lambda e, ci=ci: e.matmul(
                    p_sum1c, s_ones[:, 0:1], cxb_v[:, ci, :],
                    start=(ci == 0), stop=(ci == DC - 1)),
                    waits=([(dv_c, dv_xbc), (dv_c, rec_statc_free), (pe_b, b_pe_end),
                            (ac_c, rec_cbs_ac)] if ci == 0 else []),
                    inc=(pe_c if ci == DC - 1 else None))
            for ci in range(DC):
                P.op("pe", lambda e, ci=ci: e.matmul(
                    p_sum2c, s_ones[:, 0:1], csq_v[:, ci, :],
                    start=(ci == 0), stop=(ci == DC - 1)),
                    waits=([(ac_c, ac_sqc)] if ci == 0 else []),
                    inc=(pe_c if ci == DC - 1 else None))
            pe_statc = pe_c.v
            P.op("dve", lambda e: e.tensor_scalar_mul(st_m, p_sum1c, 1.0 / D),
                 waits=[(pe_c, pe_statc)])
            P.drain("dve")
            P.op("dve", lambda e: e.tensor_mul(st_m2, st_m, st_m))
            P.drain("dve")
            P.op("dve", lambda e: e.scalar_tensor_tensor(
                out=st_m2, in0=p_sum2c, scalar=1.0 / D, in1=st_m2,
                op0=ALU.mult, op1=ALU.subtract))
            P.drain("dve", inc=dv_c)
            dv_varc = dv_c.v
            P.op("act", lambda e: e.activation(out=st_ln, in_=st_m2, func=AF.Ln, bias=EPS),
                 waits=[(dv_c, dv_varc)])
            P.op("act", lambda e: e.drain())
            P.op("act", lambda e: e.activation(out=st_rstd, in_=st_ln, func=AF.Exp, scale=-0.5),
                 inc=ac_c)
            ac_rstdc = ac_c.v
            P.op("dve", lambda e: e.tensor_copy(out=s_statb[0:1, 0:512], in_=st_m),
                 waits=[(ac_c, ac_rstdc)])
            P.op("dve", lambda e: e.tensor_copy(out=s_statb[0:1, 512:1024], in_=st_rstd))
            P.drain("dve", inc=dv_c)
            dv_statbc = dv_c.v
            P.op("pe", lambda e: e.matmul(p_mb2, s_ones[0:1, :], s_statb[0:1, 0:512],
                                          start=True, stop=True),
                 waits=[(dv_c, dv_statbc), (dv_c, rec_mbrb2_free), (ac_c, rec_cbs_ac)])
            P.op("pe", lambda e: e.matmul(p_rb2, s_ones[0:1, :], s_statb[0:1, 512:1024],
                                          start=True, stop=True), inc=pe_c)
            pe_bc2 = pe_c.v
            for ci in range(DC):
                P.op("dve", lambda e, ci=ci: e.tensor_tensor(
                    out=acc_v[:, ci, :], in0=x1_v[:, ci, :], in1=p_mb2, op=ALU.subtract),
                    waits=([(pe_c, pe_bc2)] if ci == 0 else []) + [(st_o, out_done_ci[ci])])
            P.drain("dve")
            for ci in range(DC):
                P.op("dve", lambda e, ci=ci: e.tensor_mul(
                    xc_v[:, ci, :], acc_v[:, ci, :], p_rb2),
                    waits=([(pe_b, b_pe_end)] if ci == 0 else []))
            P.drain("dve", inc=dv_c)
            dv_xc = dv_c.v
            P.op("dve", lambda e: e.tensor_tensor(
                out=c_acc[:], in0=s_x1[:], in1=c_xc[:], op=ALU.add))
            P.drain("dve", inc=dv_c)
            rec_x1f[0] = dv_c.v
            if tc4 + 1 < _CT4:
                emit_x1_load(tc4 + 1)
            P.op("act", lambda e: e.activation(out=c_sq[:], in_=c_xc[:], func=AF.Square),
                 waits=[(dv_c, dv_xc)], inc=ac_c)
            ac_sq2 = ac_c.v
            for ci in range(DC):
                P.op("pe", lambda e, ci=ci: e.matmul(
                    p_sum1c, s_ones[:, 0:1], csq_v[:, ci, :],
                    start=(ci == 0), stop=(ci == DC - 1)),
                    waits=([(ac_c, ac_sq2), (dv_c, dv_varc)] if ci == 0 else []),
                    inc=(pe_c if ci == DC - 1 else None))
            pe_rms = pe_c.v
            P.op("dve", lambda e: e.tensor_scalar_mul(st_m, p_sum1c, 1.0 / D),
                 waits=[(pe_c, pe_rms)])
            P.drain("dve", inc=dv_c)
            dv_rmsv = dv_c.v
            P.op("act", lambda e: e.activation(out=st_ln, in_=st_m, func=AF.Ln, bias=EPS),
                 waits=[(dv_c, dv_rmsv)])
            P.op("act", lambda e: e.drain())
            P.op("act", lambda e: e.activation(out=st_rstd, in_=st_ln, func=AF.Exp, scale=-0.5),
                 inc=ac_c)
            ac_rms = ac_c.v
            P.op("dve", lambda e: e.tensor_copy(out=s_statb[0:1, 512:1024], in_=st_rstd),
                 waits=[(ac_c, ac_rms)])
            P.drain("dve", inc=dv_c)
            dv_rmsb = dv_c.v
            P.op("pe", lambda e: e.matmul(p_rb2, s_ones[0:1, :], s_statb[0:1, 512:1024],
                                          start=True, stop=True),
                 waits=[(dv_c, dv_rmsb), (dv_c, dv_xc)], inc=pe_c)
            pe_rmsb = pe_c.v
            for ci in range(DC):
                P.op("dve", lambda e, ci=ci: e.scalar_tensor_tensor(
                    out=nrm_v[:, ci, :], in0=xc_v[:, ci, :], scalar=s_gamma[:, ci:ci + 1],
                    in1=p_rb2, op0=ALU.mult, op1=ALU.mult),
                    waits=([(pe_c, pe_rmsb)] if ci == 0 else []))
            P.drain("dve", inc=dv_c)
            dv_nrm = dv_c.v
            rec_mbrb2_free = dv_nrm
            if debug:
                P.op("pool", lambda e, qs=qs: e.dma_start(
                    out=dbg["normed"].rearrange("(c p) t -> p c t", p=128)[:, :, qs],
                    in_=nrm_v), waits=[(dv_c, dv_nrm)], inc=st_d, inc_n=16)

            pe_lg = [0] * 4
            for tt in range(4):
                tsl = slice(tt * 128, (tt + 1) * 128)
                for ci in range(DC):
                    P.op("pe", lambda e, ci=ci, tsl=tsl, tt=tt: e.matmul(
                        p_lg[:, tt * 64:tt * 64 + 8], nrm_v[:, ci, tsl], gw_v[:, ci, :],
                        start=(ci == 0), stop=False),
                        waits=([(dv_c, dv_nrm), (dv_c, rec_gl_free[0])] if (ci == 0 and tt == 0) else []))
                P.op("pe", lambda e, tt=tt: e.matmul(
                    p_lg[:, tt * 64:tt * 64 + 8], s_ones[0:1, 0:128],
                    s_gbrow[0:1, :], start=False, stop=True),
                    inc=pe_c)
                pe_lg[tt] = pe_c.v

            def gstep(build4, waits=(), inc=None):
                for tt in range(4):
                    P.op("dve", lambda e, tt=tt: build4(e, tt),
                         waits=(waits if tt == 0 else []),
                         inc=(inc if tt == 3 else None))
                P.drain("dve")
            g = gt_v
            gb_ = gtb_v
            gs = gsc_v
            gstep(lambda e, tt: e.tensor_copy(out=gb_[:, tt, 0:8],
                                              in_=p_lg[:, tt * 64:tt * 64 + 8]),
                  waits=[(pe_c, pe_lg[3])])
            gstep(lambda e, tt: e.tensor_copy(out=g[:, tt, 0:8], in_=gb_[:, tt, 0:8]))
            gstep(lambda e, tt: e.max(out=g[:, tt, 8:16], in_=g[:, tt, 0:8]))
            gstep(lambda e, tt: e.tensor_sub(gs[:, tt, 0:1], g[:, tt, 9:10], g[:, tt, 8:9]))
            for tt in range(4):
                P.op("dve", lambda e, tt=tt: e.tensor_scalar(
                    out=g[:, tt, 16:24], in0=g[:, tt, 0:8],
                    scalar1=g[:, tt, 8:9], scalar2=None, op0=ALU.is_equal))
            gstep(lambda e, tt: e.tensor_scalar(
                out=g[:, tt, 24:32], in0=g[:, tt, 0:8],
                scalar1=g[:, tt, 9:10], scalar2=None, op0=ALU.is_equal), inc=dv_c)
            dv_eq = dv_c.v
            for tt in range(4):
                P.op("act", lambda e, tt=tt: e.activation(
                    out=gs[:, tt, 4:5], in_=gs[:, tt, 0:1], func=AF.Sigmoid),
                    waits=([(dv_c, dv_eq)] if tt == 0 else []))
            for tt in range(4):
                P.op("act", lambda e, tt=tt: e.activation(
                    out=gs[:, tt, 3:4], in_=gs[:, tt, 0:1], func=AF.Sigmoid, scale=-1.0),
                    inc=(ac_c if tt == 3 else None))
            ac_z = ac_c.v
            P.wait("dve", ac_c, ac_z)
            gstep(lambda e, tt: e.tensor_tensor_scan(
                out=g[:, tt, 32:40], data0=g[:, tt, 16:24], data1=s_zero8[:],
                initial=0.0, op0=ALU.add, op1=ALU.add))
            gstep(lambda e, tt: e.tensor_scalar(
                out=g[:, tt, 40:48], in0=g[:, tt, 32:40],
                scalar1=1.0, scalar2=None, op0=ALU.is_equal))
            gstep(lambda e, tt: e.tensor_mul(g[:, tt, 48:56], g[:, tt, 40:48], g[:, tt, 16:24]))
            gstep(lambda e, tt: e.tensor_scalar(
                out=g[:, tt, 56:64], in0=g[:, tt, 48:56],
                scalar1=-1.0, scalar2=1.0, op0=ALU.mult, op1=ALU.add))
            gstep(lambda e, tt: e.tensor_mul(g[:, tt, 64:72], g[:, tt, 24:32], g[:, tt, 56:64]))
            gstep(lambda e, tt: e.tensor_tensor_scan(
                out=g[:, tt, 72:80], data0=g[:, tt, 64:72], data1=s_zero8[:],
                initial=0.0, op0=ALU.add, op1=ALU.add))
            gstep(lambda e, tt: e.tensor_scalar(
                out=g[:, tt, 80:88], in0=g[:, tt, 72:80],
                scalar1=1.0, scalar2=None, op0=ALU.is_equal))
            gstep(lambda e, tt: e.tensor_mul(g[:, tt, 88:96], g[:, tt, 80:88], g[:, tt, 64:72]))
            gstep(lambda e, tt: e.tensor_scalar(
                out=g[:, tt, 96:104], in0=g[:, tt, 48:56],
                scalar1=gs[:, tt, 3:4], scalar2=None, op0=ALU.mult))
            gstep(lambda e, tt: e.scalar_tensor_tensor(
                out=g[:, tt, 104:112], in0=g[:, tt, 88:96],
                scalar=gs[:, tt, 4:5], in1=g[:, tt, 96:104],
                op0=ALU.mult, op1=ALU.add), inc=dv_c)
            dv_comb = dv_c.v
            for tt in range(4):
                P.op("pe", lambda e, tt=tt: e.transpose(p_ct, g[:, tt, 104:112], s_id[:]),
                     waits=([(dv_c, dv_comb), (dv_c, rec_statc_free)] if tt == 0
                            else [(dv_c, dv_c.v)]),
                     inc=pe_c)
                pe_ct = pe_c.v
                P.op("dve", lambda e, tc4=tc4, tt=tt: e.tensor_copy(
                    out=c_ct[0:8, tc4 * 512 + tt * 128: tc4 * 512 + (tt + 1) * 128],
                    in_=p_ct), waits=[(pe_c, pe_ct)])
                P.drain("dve", inc=dv_c)
            dv_ct = dv_c.v
            rec_statc_free = dv_ct
            rec_gl_free[0] = max(rec_gl_free[0], dv_ct)
            if debug and tc4 == NQC - 1:
                P.op("pool", lambda e: e.dma_start(out=dbg["ct"], in_=c_ct[0:NE, :]),
                     waits=[(dv_c, dv_ct)], inc=st_d, inc_n=16)

            def emit_mlp2(pm):
                e_m, wb_m, po_m = pm["e_"], pm["wb"], pm["po_act"]
                for ht in range(DC):
                    hb = ht % 2
                    for ic in range(DC):
                        P.op("pe", lambda e, wb=wb_m, ht=ht, ic=ic, hb=hb, e_m=e_m: e.matmul(
                            p_p2[hb], wf_v[wb][:, 2, ic, ht * 128:(ht + 1) * 128],
                            act_v[:, ic, :],
                            start=(ic == 0), stop=(ic == DC - 1 and (e_m != NE - 1 or _CNE < NE))),
                            waits=([(po_c, po_m), (dv_c, rec_p2_free[hb])] if ic == 0 else []),
                            inc=(pe_c if (ic == DC - 1 and e_m != NE - 1) else None))
                    if e_m == NE - 1:
                        P.op("pe", lambda e, ht=ht, hb=hb, ct_s=ct_s: e.matmul(
                            p_p2[hb], s_b2[0:NE, ht * 128:(ht + 1) * 128], c_ct[0:NE, ct_s],
                            start=False, stop=True),
                            waits=[(ld_x1, sel_ld)] if ht == 0 else [], inc=pe_c)
                    pe_p2 = pe_c.v
                    P.op("dve", lambda e, ht=ht, hb=hb: e.tensor_add(
                        acc_v[:, ht, :], acc_v[:, ht, :], p_p2[hb]),
                        waits=[(pe_c, pe_p2)], inc=dv_c)
                    rec_p2_free[hb] = dv_c.v
                P.drain("dve")
                rec_wf_free[wb_m] = pe_c.v
                pe_mlp2_prev[0] = pe_c.v
                idx_m = tc4 * NE + e_m
                nonlocal_next = idx_m + 2
                if not _skipC and next_wf[0] < len(wf_loads) and next_wf[0] == nonlocal_next:
                    wf_ready[next_wf[0]] = emit_wf_load(next_wf[0])
                    next_wf[0] += 1

            for e_ in range(_CNE):
                idx = tc4 * NE + e_
                wb = idx % 2
                wf_here = wf_ready[idx]
                dv_t1as = [0] * DC
                dv_lc2s = [0] * DC
                pe_gs = [0] * DC
                pe_ls = [0] * DC

                def emit_gl(it):
                    par = it % 2
                    p_g, p_l = p_gl[par]
                    g_waits = [(ld_wf, wf_here), (dv_c, rec_gl_free[par])]
                    if par == 1 and it == 1:
                        g_waits.append((dv_c, rec_mbrb2_free))
                    for ci in range(DC):
                        P.op("pe", lambda e, it=it, ci=ci, p_g=p_g, wb=wb: e.matmul(
                            p_g, wf_v[wb][:, 0, ci, it * 128:(it + 1) * 128], nrm_v[:, ci, :],
                            start=(ci == 0), stop=(ci == DC - 1)),
                            waits=(g_waits if ci == 0 else []),
                            inc=(pe_c if ci == DC - 1 else None))
                    pe_gs[it] = pe_c.v
                    for ci in range(DC):
                        P.op("pe", lambda e, it=it, ci=ci, p_l=p_l, wb=wb: e.matmul(
                            p_l, wf_v[wb][:, 1, ci, it * 128:(it + 1) * 128], nrm_v[:, ci, :],
                            start=(ci == 0), stop=(ci == DC - 1)),
                            waits=([(ac_c, rec_pl_free[par])] if ci == 0 else []),
                            inc=(pe_c if ci == DC - 1 else None))
                    pe_ls[it] = pe_c.v

                emit_gl(0)
                emit_gl(1)
                P.op("pe", lambda e, e_=e_, ct_s=ct_s: e.matmul(
                    p_cb, s_sel[0:NE, e_ * 128:(e_ + 1) * 128], c_ct[0:NE, ct_s],
                    start=True, stop=True),
                    waits=[(dv_c, dv_ct), (ld_x1, sel_ld), (ac_c, rec_cbs_ac)],
                    inc=pe_c)
                pe_cb = pe_c.v
                P.op("act", lambda e: e.activation(out=c_cbs[:], in_=p_cb, func=AF.Copy),
                     waits=[(pe_c, pe_cb), (dv_c, rec_cb_free)], inc=ac_c)
                ac_cbs = ac_c.v
                rec_cbs_ac = ac_cbs
                if pend_mlp2 is not None:
                    emit_mlp2(pend_mlp2)
                    pend_mlp2 = None

                def emit_stage1(it):
                    par = it % 2
                    p_g, p_l = p_gl[par]
                    pe_g_here = pe_gs[it]
                    pe_l_here = pe_ls[it]
                    bofs = it * NE + e_
                    TA, SG, LC, AP_ = c_t1as[par], c_sigs[par], c_lcs[par], c_aps[par]
                    P.op("act", lambda e, bofs=bofs, p_g=p_g, SG=SG: e.activation(
                        out=SG[:], in_=p_g, func=AF.Sigmoid,
                        bias=s_b1gs[:, bofs:bofs + 1], scale=ALPHA),
                        waits=[(pe_c, pe_g_here), (dv_c, rec_sig_free[par])], inc=ac_c)
                    ac_sig = ac_c.v
                    P.op("act", lambda e, bofs=bofs, p_l=p_l, LC=LC: e.activation(
                        out=LC[:], in_=p_l, func=AF.Identity,
                        bias=s_b1l[:, bofs:bofs + 1]),
                        waits=[(pe_c, pe_l_here), (po_c, rec_lc_free[par])], inc=ac_c)
                    ac_lraw = ac_c.v
                    rec_pl_free[par] = ac_lraw
                    P.op("dve", lambda e, bofs=bofs, p_g=p_g, TA=TA, SG=SG: e.scalar_tensor_tensor(
                        out=TA[:], in0=p_g, scalar=s_b1g[:, bofs:bofs + 1],
                        in1=SG[:], op0=ALU.add, op1=ALU.mult),
                        waits=[(ac_c, ac_sig)], inc=dv_c)
                    dv_t1a = dv_c.v
                    dv_t1as[it] = dv_t1a
                    rec_gl_free[par] = dv_t1a
                    rec_sig_free[par] = dv_t1a
                    P.op("dve", lambda e, LC=LC: e.tensor_scalar(
                        out=LC[:], in0=LC[:], scalar1=-6.0, scalar2=8.0,
                        op0=ALU.max, op1=ALU.min),
                        waits=[(ac_c, ac_lraw)], inc=dv_c)
                    dv_lc2s[it] = dv_c.v

                def emit_stage2(it):
                    par = it % 2
                    TA, LC, AP_ = c_t1as[par], c_lcs[par], c_aps[par]
                    P.op("dve", lambda e, TA=TA, AP_=AP_: e.scalar_tensor_tensor(
                        out=AP_[:], in0=TA[:], scalar=CCAP, in1=c_cbs[:],
                        op0=ALU.min, op1=ALU.mult),
                        waits=[(ac_c, ac_cbs), (po_c, rec_ap_free[par])], inc=dv_c)
                    dv_z = dv_c.v
                    rec_ta_free[par] = dv_z
                    P.op("pool", lambda e, it=it, LC=LC, AP_=AP_: e.tensor_mul(
                        act_v[:, it, :], AP_[:], LC[:]),
                        waits=[(dv_c, dv_z), (dv_c, dv_lc2s[it]),
                               (pe_c, pe_mlp2_prev[0])], inc=po_c)
                    rec_lc_free[par] = po_c.v
                    rec_ap_free[par] = po_c.v

                for it in range(DC + 1):
                    if it < DC:
                        emit_stage1(it)
                        if it + 2 < DC:
                            emit_gl(it + 2)
                    if it >= 1:
                        P.drain("dve")
                        emit_stage2(it - 1)
                P.op("pool", lambda e: e.drain(), inc=po_c)
                rec_cb_free = dv_c.v
                pend_mlp2 = dict(e_=e_, wb=wb, po_act=po_c.v)
            emit_mlp2(pend_mlp2)
            pend_mlp2 = None
            pe_mlp2_end = pe_c.v
            if debug and tc4 == NQC - 1:
                P.op("pool", lambda e, qs=qs: e.dma_start(
                    out=dbg["acc"].rearrange("(c p) t -> p c t", p=128)[:, :, qs],
                    in_=acc_v), waits=[(dv_c, dv_c.v), (pe_c, pe_mlp2_end)], inc=st_d, inc_n=16)
            P.drain("dve", inc=dv_c)
            dv_out = dv_c.v
            for ci in range(DC):
                P.op("sp", lambda e, qs=qs, ci=ci: e.dma_start(
                    out=out_fm_v[:, ci, qs], in_=acc_v[:, ci, :]),
                    waits=([(dv_c, dv_out)] if ci == 0 else []), inc=st_o, inc_n=16)
                out_done_ci[ci] = st_o.v
            out_done[0] = st_o.v

        P.wait("sp", st_o, st_o.v)
        P.wait("sp", sp_k, sp_k.v)
        P.wait("sp", sp_v, sp_v.v)
        P.wait("sp", sp_o, sp_o.v)
        P.wait("sp", ld_c, ld_c.v)
        P.wait("sp", ld_w, ld_w.v)
        P.wait("sp", ld_wf, ld_wf.v)
        if debug:
            P.wait("sp", st_d, st_d.v)

        for s in P.sems.values():
            s.h = es.enter_context(nc.semaphore(s.name))

        blk = es.enter_context(nc.Block())

        @blk.tensor
        def _(t):
            for fn in P.ops["pe"]:
                fn(t)

        @blk.scalar
        def _(sc):
            for fn in P.ops["act"]:
                fn(sc)

        @blk.vector
        def _(v):
            for fn in P.ops["dve"]:
                fn(v)

        @blk.gpsimd
        def _(g):
            for fn in P.ops["pool"]:
                fn(g)

        @blk.sync
        def _(sy):
            for fn in P.ops["sp"]:
                fn(sy)

    return nc


def _sel_const():
    sel = np.zeros((NE, NE * 128), dtype=BF)
    for e in range(NE):
        sel[e, e * 128:(e + 1) * 128] = 1
    return sel


def make_inputs(inputs):
    x = np.asarray(inputs["L0_attn_x"])
    wq = np.asarray(inputs["L0_attn_Wq"]); bq = np.asarray(inputs["L0_attn_bq"])
    wk = np.asarray(inputs["L0_attn_Wk"]); bk = np.asarray(inputs["L0_attn_bk"])
    wv = np.asarray(inputs["L0_attn_Wv"]); bv = np.asarray(inputs["L0_attn_bv"])
    wo = np.asarray(inputs["L0_attn_Wo"]); bo = np.asarray(inputs["L0_attn_bo"])
    gamma = np.asarray(inputs["L0_ffn_norm_weight"])
    gw = np.asarray(inputs["L0_ffn_gate_weight"])
    gb = np.asarray(inputs["L0_ffn_gate_bias"])
    w1 = np.asarray(inputs["L0_ffn_mlp1_weight"])
    b1 = np.asarray(inputs["L0_ffn_mlp1_bias"])
    w2 = np.asarray(inputs["L0_ffn_mlp2_weight"])
    b2 = np.asarray(inputs["L0_ffn_mlp2_bias"])

    def chunked(b):
        return np.ascontiguousarray(b.reshape(DC, 128).T)

    def bias_pe(b):
        return np.ascontiguousarray(
            b.T.reshape(DC, 128, NE).transpose(1, 0, 2).reshape(128, DC * NE))

    shared = dict(
        wqT=np.ascontiguousarray(wq.T), wkT=np.ascontiguousarray(wk.T),
        wvT=np.ascontiguousarray(wv.T), woT=np.ascontiguousarray(wo.T),
        b_qkvo=np.ascontiguousarray(np.concatenate(
            [chunked(bq), chunked(bk), chunked(bv), chunked(bo)], axis=1)),
        bv_row=np.ascontiguousarray(bv.reshape(1, D)),
        gamma_in=chunked(gamma),
        wg_in=np.ascontiguousarray(gw.T),
        gb_row=np.ascontiguousarray(gb.reshape(1, NE)),
        w1gT=np.ascontiguousarray(w1[:, 0::2, :].transpose(0, 2, 1)),
        w1lT=np.ascontiguousarray(w1[:, 1::2, :].transpose(0, 2, 1)),
        w2T=np.ascontiguousarray(w2.transpose(0, 2, 1)),
        b1g_in=bias_pe(b1[:, 0::2]),
        b1l_in=bias_pe(b1[:, 1::2]),
        b2_in=np.ascontiguousarray(b2),
        ones_bf_in=np.ones((128, 128), dtype=BF),
        ident_in=np.eye(128, dtype=np.float32),
        sel_in=_sel_const(),
    )
    in_maps = []
    for c in range(8):
        b, h = c // 2, c % 2
        perm = np.concatenate([np.arange(h * T, S), np.arange(0, h * T)])
        cos_t, sin_t = _rope_tables(perm)
        xc = np.ascontiguousarray(x[b].T[:, perm])
        m = dict(shared)
        m.update(x_fm=xc, cos_in=cos_t, sin_in=sin_t)
        in_maps.append(m)
    return in_maps


def kernel(**inputs):
    in_maps = make_inputs(inputs)
    nc = build_program(debug=False)
    res = run_bass_kernel_spmd(nc, in_maps, core_ids=list(range(8)))
    out = np.empty((4, S, D), dtype=np.float32)
    for c in range(8):
        b, h = c // 2, c % 2
        out[b, h * T:(h + 1) * T, :] = np.asarray(res.results[c]["out_fm"]).T
    return out
